# version 06
# speedup vs baseline: 1.2626x; 1.0087x over previous
import contextlib
import numpy as np
import ml_dtypes
import concourse.bass as bass
import concourse.mybir as mybir
from concourse.bass_utils import run_bass_kernel_spmd

F32 = mybir.dt.float32
BF16 = mybir.dt.bfloat16
AF = mybir.ActivationFunctionType
ALU = mybir.AluOpType
BF = ml_dtypes.bfloat16

NCORES = 8
TC = 2048
D = 1024
NCH = D // 128


STREAMS = ("T", "V", "S", "G", "Y")


class Op:
    __slots__ = ("stream", "fn", "reads", "writes", "inc", "tag", "idx",
                 "sidx", "deps", "need_inc", "cum", "waits", "role")

    def __init__(self, stream, fn, reads, writes, inc, tag, role=None):
        self.stream = stream
        self.fn = fn
        self.reads = reads
        self.writes = writes
        self.inc = inc
        self.tag = tag
        self.role = role
        self.deps = []
        self.need_inc = False
        self.cum = None
        self.waits = []


class Prog:
    def __init__(self):
        self.ops = []
        self.last_writer = {}
        self.readers = {}

    PSUM_PREFIXES = ("ps", "pb", "pot", "pst", "ppv", "pbc", "pstat")

    def add(self, stream, fn, reads=(), writes=(), inc=1, tag="", role=None):
        assert not getattr(self, "frozen", False), f"op added during finalize: {tag}"
        op = Op(stream, fn, tuple(reads), tuple(writes), inc, tag, role)
        op.idx = len(self.ops)
        self.ops.append(op)
        deps = set()
        for k in op.reads:
            w = self.last_writer.get(k)
            if w is not None:
                deps.add(w.idx)
            if k.startswith(self.PSUM_PREFIXES) and stream in ("V", "S"):
                for r in self.readers.get(k, ()):
                    if r.stream in ("V", "S") and r.stream != stream:
                        raise AssertionError(
                            f"PSUM bank {k} read by both {r.stream}({r.tag}) and "
                            f"{stream}({tag}) without intervening write - fatal "
                            f"bank conflict on hardware")
            self.readers.setdefault(k, []).append(op)
        for k in op.writes:
            w = self.last_writer.get(k)
            if w is not None:
                deps.add(w.idx)
            for r in self.readers.get(k, ()):
                deps.add(r.idx)
            self.readers[k] = []
            self.last_writer[k] = op
        deps.discard(op.idx)
        op.deps = sorted(deps)
        return op

    def T(self, fn, reads=(), writes=(), tag=""):
        return self.add("T", fn, reads, writes, 1, tag)

    def V(self, fn, reads=(), writes=(), tag=""):
        return self.add("V", fn, reads, writes, 1, tag)

    def S(self, fn, reads=(), writes=(), tag=""):
        return self.add("S", fn, reads, writes, 1, tag)

    def G(self, fn, reads=(), writes=(), tag=""):
        return self.add("G", fn, reads, writes, 1, tag)

    def dma(self, stream, fn, reads=(), writes=(), tag="", role=None):
        return self.add(stream, fn, reads, writes, 16, tag,
                        role=role or (stream + "d"))

    def cc(self, fn, reads=(), writes=(), tag=""):
        op = self.add("G", fn, reads, writes, 1, tag, role="cc")
        op.inc = 101
        return op

    def barrier_all(self, keys, tag="final"):
        self.add("Y", None, tuple(keys), (), 0, tag)

    def barrier(self, tag="phase"):
        last = {}
        for op in self.ops:
            if op.inc == 0:
                continue
            if op.role is not None:
                last[("R", op.role)] = op
            else:
                last[(op.stream, "c")] = op
        deps = sorted({o.idx for o in last.values()})
        for o in last.values():
            o.need_inc = True
        for st in STREAMS:
            bop = self.add(st, None, (), (), 0, f"bar_{tag}_{st}")
            bop.deps = [d for d in deps if d != bop.idx]

    def finalize(self, nc):
        self.frozen = True
        ops = self.ops
        INORDER = {"T", "V", "S", "Y"}
        for op in ops:
            best = {}
            keep = []
            for d in op.deps:
                st = ops[d].stream
                kind = ops[d].role or "c"
                k = (st, kind)
                if st in INORDER or (st == "G" and kind != "c"):
                    if k not in best or d > best[k]:
                        best[k] = d
                else:
                    keep.append(d)
            op.deps = sorted(set(keep) | set(best.values()))
        for op in ops:
            if op.inc in (16, 101):
                op.need_inc = True
            for d in op.deps:
                p = ops[d]
                if p.stream == "T" and op.stream == "T":
                    continue
                p.need_inc = True

        roles = sorted({op.role for op in ops if op.role})
        sems = {}
        es = contextlib.ExitStack()
        for st in STREAMS:
            sems[(st, "c")] = es.enter_context(nc.semaphore(f"sem_{st}c"))
        for r in roles:
            sems[("R", r)] = es.enter_context(nc.semaphore(f"sem_{r}"))
        self._sem_es = es

        counts = {}
        role_chain = {}
        for op in ops:
            if op.inc == 0:
                continue
            if op.role is not None:
                op.need_inc = True
                key = ("R", op.role)
                amt = 16 if op.inc == 16 else 1
                prev = role_chain.get(key, 0)
                if prev:
                    op.waits.append((key, prev))
                counts[key] = counts.get(key, 0) + amt
                role_chain[key] = counts[key]
                op.cum = (key, counts[key])
            elif op.need_inc:
                key = (op.stream, "c")
                counts[key] = counts.get(key, 0) + 1
                op.cum = (key, counts[key])

        seen = {st: {} for st in STREAMS}
        for op in ops:
            for d in op.deps:
                p = ops[d]
                if p.cum is None:
                    if p.stream == op.stream and p.stream == "T":
                        continue
                    if p.inc == 0:
                        continue
                    raise AssertionError(f"dep without inc: {p.tag} <- {op.tag}")
                key, val = p.cum
                cur = seen[op.stream].get(key, 0)
                if val > cur:
                    op.waits.append((key, val))
                    seen[op.stream][key] = val
            if op.waits:
                best = {}
                for key, val in op.waits:
                    if val > best.get(key, 0):
                        best[key] = val
                op.waits = [(k, v) for k, v in best.items()]

        stream_ops = {st: [o for o in ops if o.stream == st] for st in STREAMS}

        def run_stream(eng, st):
            for op in stream_ops[st]:
                for key, val in op.waits:
                    eng.wait_ge(sems[key], val)
                if op.fn is None:
                    continue
                inst = op.fn(eng)
                if op.cum is not None:
                    key, _ = op.cum
                    amt = 16 if (op.role and op.inc == 16) else 1
                    inst.then_inc(sems[key], amt)

        with nc.Block() as block:
            @block.tensor
            def _(e):
                run_stream(e, "T")

            @block.vector
            def _(e):
                run_stream(e, "V")

            @block.scalar
            def _(e):
                run_stream(e, "S")

            @block.gpsimd
            def _(e):
                run_stream(e, "G")

            @block.sync
            def _(e):
                run_stream(e, "Y")



class Ctx:
    def __init__(self, nc, prog, es):
        self.nc = nc
        self.p = prog
        self.es = es
        self.root_es = es
        self.consts = {}
        self.uid = 0
        self.ns = ""

    def begin_phase(self, ns):
        self.ns = ns
        self.es = contextlib.ExitStack()
        for attr in ("wpool", "wpool_keys", "wpool_cfg"):
            if hasattr(self, attr):
                delattr(self, attr)

    def end_phase(self):
        self.es.close()
        self.es = self.root_es
        self.ns = ""
        self.p.barrier()

    def dram(self, name, shape, dtype, kind="ExternalInput"):
        return self.nc.dram_tensor(self.ns + name, shape, dtype, kind=kind)

    def add_const(self, name, arr):
        self.consts[self.ns + name] = np.ascontiguousarray(arr)

    def name(self, base):
        self.uid += 1
        return f"{base}_{self.uid}"

    def sbuf(self, base, shape, dtype):
        return self.es.enter_context(self.nc.sbuf_tensor(self.name(base), shape, dtype))

    def sbuf_root(self, base, shape, dtype):
        return self.root_es.enter_context(
            self.nc.sbuf_tensor(self.name(base), shape, dtype))

    def psum(self, base, shape, dtype=F32):
        return self.es.enter_context(self.nc.psum_tensor(self.name(base), shape, dtype))

    def cf(self, val, rows=128):
        if not hasattr(self, "cf_tile"):
            self.cf_tile = self.sbuf_root("cft", [128, 64], F32)
            self.cf_map = {}
            self.cf_dram = self.nc.dram_tensor("cf_consts", [128, 64], F32,
                                               kind="ExternalInput")
            load_const(self, self.cf_tile, self.cf_dram, "cft")
        v = float(val)
        if v not in self.cf_map:
            col = len(self.cf_map)
            assert col < 64
            self.cf_map[v] = col
        col = self.cf_map[v]
        return self.cf_tile[0:rows, col:col + 1]

    def cf_array(self):
        arr = np.zeros((128, 64), np.float32)
        if hasattr(self, "cf_map"):
            for v, col in self.cf_map.items():
                arr[:, col] = v
        return arr

    def const_input(self, name, arr):
        arr = np.ascontiguousarray(arr)
        self.consts[name] = arr
        dt = {np.dtype(np.float32): F32, np.dtype(BF): BF16}[arr.dtype]
        return self.nc.dram_tensor(name, list(arr.shape), dt, kind="ExternalInput")


def make_ctx(nc):
    es = contextlib.ExitStack()
    return Ctx(nc, Prog(), es)


def iod(ctx, io, name, shape, dtype, kind="ExternalInput"):
    if io and name in io:
        return io[name]
    return ctx.dram(name, shape, dtype, kind=kind)



def load_xT(ctx, src_dram, dst, key, t0=0, tlen=TC, nch=NCH, stream="Y"):
    p = ctx.p
    ap = src_dram[:, t0:t0 + tlen].rearrange("(c p) t -> p c t", p=128)

    def fn(e, ap=ap, dst=dst, tlen=tlen):
        return e.dma_start(out=dst[:, :, 0:tlen], in_=ap)
    p.dma(stream, fn, reads=(key + "@hbm",), writes=(key,), tag=f"load_{key}")


def store_xT(ctx, dst_dram, src, key, outkey, t0=0, tlen=TC, nch=NCH,
             stream="Y"):
    p = ctx.p
    ap = dst_dram[:, t0:t0 + tlen].rearrange("(c p) t -> p c t", p=128)

    def fn(e, ap=ap, src=src):
        return e.dma_start(out=ap, in_=src[:, :, :])
    p.dma(stream, fn, reads=(key,), writes=(outkey,), tag=f"store_{outkey}")


class LNHelper:

    def __init__(self, ctx, tchunk=512, nch=NCH):
        self.ctx = ctx
        self.tchunk = tchunk
        self.nch = nch
        self.sq_s = [ctx.sbuf(f"ln_sq_{i}", [128, tchunk], F32) for i in range(2)]
        self.sq2_s = None
        self.rows_s = [ctx.sbuf("ln_rows_0", [1, 4 * tchunk], F32)] * 2
        self.cnt = 0
        self.ones_sb = get_ones(ctx)

    def emit(self, x, xkey, out, outkey, ps_stat, ps_stat2, ps_b0, ps_b1,
             eps=1e-5, out2=None, out2key=None, gamma=None, beta=None,
             tlen=None):
        ctx, p = self.ctx, self.ctx.p
        nch, tc = self.nch, (tlen if tlen is not None else self.tchunk)
        nf = nch * 128
        si = self.cnt % 2
        self.cnt += 1
        if gamma is not None and self.sq2_s is None:
            self.sq2_s = [ctx.sbuf(f"ln_sq2_{i}", [128, self.tchunk], F32)
                          for i in range(2)]
        sq = self.sq_s[si]
        sq2 = self.sq2_s[si] if self.sq2_s else None
        kSQ, kSQ2, kRW = f"ln_sq_{si}", f"ln_sq2_{si}", "ln_rows_0"
        stat, stat_k = ps_stat
        stat2, stat2_k = ps_stat2
        b0, b0_k = ps_b0
        b1, b1_k = ps_b1
        for c in range(nch):
            def fn(e, c=c, x=x, stat=stat, tc=tc):
                return e.matmul(stat[0:1, 0:tc], self.ones_sb[:, 0:1],
                                x[:, c, 0:tc], start=(c == 0), stop=(c == nch - 1))
            p.T(fn, reads=(xkey, "ones"), writes=(stat_k,), tag="ln_mean")
        for c in range(nch):
            def fnsq(e, c=c, x=x, tc=tc):
                return e.activation(sq[:, 0:tc], x[:, c, 0:tc], AF.Square)
            p.S(fnsq, reads=(xkey,), writes=(kSQ,), tag=kSQ)

            def fnmm(e, c=c, stat2=stat2, tc=tc):
                return e.matmul(stat2[0:1, 0:tc], self.ones_sb[:, 0:1],
                                sq[:, 0:tc], start=(c == 0), stop=(c == nch - 1))
            p.T(fnmm, reads=(kSQ, "ones"), writes=(stat2_k,), tag="ln_sumsq")
        r = self.rows_s[si]

        def f1(e):
            return e.tensor_scalar_mul(r[0:1, 0:tc], stat[0:1, 0:tc], -1.0 / nf)
        p.V(f1, reads=(stat_k,), writes=(kRW + "0",), tag=kRW + "0")

        def f2(e):
            return e.activation(r[0:1, 2 * tc:3 * tc], r[0:1, 0:tc], AF.Square)
        p.S(f2, reads=(kRW + "0",), writes=(kRW + "2",), tag=kRW + "2")

        def f3(e):
            return e.tensor_scalar_mul(r[0:1, 3 * tc:4 * tc], stat2[0:1, 0:tc],
                                       1.0 / nf)
        p.V(f3, reads=(stat2_k,), writes=(kRW + "3",), tag=kRW + "3")

        def f4(e):
            return e.tensor_tensor(r[0:1, 3 * tc:4 * tc], r[0:1, 3 * tc:4 * tc],
                                   r[0:1, 2 * tc:3 * tc], op=ALU.subtract)
        p.V(f4, reads=(kRW + "3", kRW + "2"), writes=(kRW + "3",), tag="ln_var")

        epsap = ctx.cf(eps, rows=1)

        def f5(e):
            return e.activation(r[0:1, 2 * tc:3 * tc], r[0:1, 3 * tc:4 * tc],
                                AF.Sqrt, bias=epsap)
        p.S(f5, reads=(kRW + "3", "cft"), writes=(kRW + "2",), tag="ln_sqrt")

        def f6(e):
            return e.reciprocal(r[0:1, tc:2 * tc], r[0:1, 2 * tc:3 * tc])
        p.V(f6, reads=(kRW + "2",), writes=(kRW + "1",), tag="ln_rstd")

        def b0fn(e):
            return e.matmul(b0[:, 0:tc], self.ones_sb[0:1, 0:128], r[0:1, 0:tc],
                            start=True, stop=True)
        p.T(b0fn, reads=(kRW + "0", "ones"), writes=(b0_k,), tag="ln_bc0")

        def b1fn(e):
            return e.matmul(b1[:, 0:tc], self.ones_sb[0:1, 0:128], r[0:1, tc:2 * tc],
                            start=True, stop=True)
        p.T(b1fn, reads=(kRW + "1", "ones"), writes=(b1_k,), tag="ln_bc1")

        for c in range(nch):
            def a1(e, c=c, x=x, tc=tc):
                return e.tensor_tensor(sq[:, 0:tc], x[:, c, 0:tc],
                                       b0[:, 0:tc], op=ALU.add)
            p.V(a1, reads=(xkey, b0_k), writes=(kSQ,), tag="ln_apply0")

            if gamma is None:
                def a2(e, c=c, out=out, tc=tc):
                    return e.tensor_tensor(out[:, c, 0:tc], sq[:, 0:tc],
                                           b1[:, 0:tc], op=ALU.mult)
                p.V(a2, reads=(kSQ, b1_k), writes=(outkey,), tag="ln_apply1")
            else:
                def a2(e, c=c, tc=tc):
                    return e.tensor_tensor(sq2[:, 0:tc], sq[:, 0:tc],
                                           b1[:, 0:tc], op=ALU.mult)
                p.V(a2, reads=(kSQ, b1_k), writes=(kSQ2,), tag="ln_apply1")

                def a2b(e, c=c, out=out, tc=tc):
                    return e.tensor_scalar(out[:, c, 0:tc], sq2[:, 0:tc],
                                           gamma(c), beta(c),
                                           op0=ALU.mult, op1=ALU.add)
                p.V(a2b, reads=(kSQ2, "lnp"), writes=(outkey,), tag="ln_gb")
            if out2 is not None:
                w0, w1_ = out2[1]
                if w1_ > w0:
                    def a3(e, c=c, w0=w0, w1_=w1_):
                        return e.tensor_tensor(out2[0](c, w0, w1_),
                                               sq[:, w0:w1_],
                                               b1[:, w0:w1_], op=ALU.mult)
                    p.V(a3, reads=(kSQ, b1_k), writes=(out2key,),
                        tag="ln_apply2")


def get_ones(ctx):
    if not hasattr(ctx, "ones_sb"):
        ctx.ones_sb = ctx.sbuf_root("ones", [128, 512], F32)

        def fn(e):
            return e.memset(ctx.ones_sb[:, :], 1.0)
        ctx.p.G(fn, writes=("ones",), tag="ones")
    return ctx.ones_sb


def load_const(ctx, sb, dram, key, stream="Y"):
    p = ctx.p

    def fn(e, sb=sb, dram=dram):
        shape = dram.shape
        if len(shape) == 2:
            return e.dma_start(out=sb[:, :], in_=dram[:, :])
        elif len(shape) == 3:
            return e.dma_start(out=sb[:, :, :], in_=dram[:, :, :])
        raise ValueError
    p.dma(stream, fn, reads=(), writes=(key,), tag=f"ldc_{key}")


def pack_weight(wt, blk=512):
    d_in, d_out = wt.shape
    nin = (d_in + 127) // 128
    nb = (d_out + blk - 1) // blk
    out = np.zeros((nb, 128, nin, blk), np.float32)
    for b in range(nb):
        o0 = b * blk
        olen = min(blk, d_out - o0)
        w = wt[:, o0:o0 + olen]
        wpad = np.zeros((nin * 128, olen), np.float32)
        wpad[:d_in] = w
        out[b, :, :, :olen] = wpad.reshape(nin, 128, olen).transpose(1, 0, 2)
    return np.ascontiguousarray(out)


def get_wpool(ctx):
    if not hasattr(ctx, "wpool"):
        nin_max, blk = getattr(ctx, "wpool_cfg", (32, 512))
        ctx.wpool = [ctx.sbuf(f"wp{i}", [128, nin_max, blk], BF16)
                     for i in range(2)]
        ctx.wpool_keys = ["wp0", "wp1"]
    return ctx.wpool, ctx.wpool_keys


class WStream:

    def __init__(self, ctx, wname, dram, d_in, d_out, blk=512, tag=None):
        self.ctx = ctx
        self.dram = dram
        self.d_in = d_in
        self.d_out = d_out
        self.blk = blk
        self.nin = (d_in + 127) // 128
        pool, keys = get_wpool(ctx)
        assert self.nin <= pool[0].shape[1] and blk <= pool[0].shape[2], (self.nin, blk)
        self.bufs = [pool[0], pool[1]]
        self.keys = list(keys)
        self.tag = tag or wname

    def load_block(self, bi, b):
        ctx, p = self.ctx, self.ctx.p
        buf = self.bufs[bi]
        nin = self.nin

        def fn(e, b=b, buf=buf, nin=nin):
            return e.dma_start(out=buf[:, 0:nin, 0:self.blk],
                               in_=self.dram[b, :, :, :])
        p.dma("G", fn, reads=(), writes=(self.keys[bi],),
              tag=f"wload_{self.tag}", role=f"w{bi}")


def emit_proj(ctx, ws, rhs_tiles, rhs_key, T_len, evac, psum_banks,
              rhs_rows=None, extra_mm=None, t_outer=False, blk_range=None):
    p = ctx.p
    d_out = ws.d_out
    blk = ws.blk
    nin = ws.nin
    d_in = ws.d_in
    rem = d_in - (d_in // 128) * 128
    nocs = (d_out + 127) // 128
    tch = 512
    nts = (T_len + tch - 1) // tch

    nblocks = (d_out + blk - 1) // blk
    pi = 0

    def inner(b, bi, ti):
        nonlocal pi
        o0 = b * blk
        olen = min(blk, d_out - o0)
        t0 = ti * tch
        tlen = min(tch, T_len - t0)
        for oc in range(0, olen, 128):
            ocw = min(128, olen - oc)
            ps, ps_key = psum_banks[pi % len(psum_banks)]
            pi += 1
            for c in range(nin):
                kw = 128 if (c < nin - 1 or rem == 0) else rem

                def fn(e, c=c, kw=kw, bi=bi, oc=oc, ocw=ocw, t0=t0,
                       tlen=tlen, ps=ps, start=(c == 0)):
                    if callable(rhs_tiles):
                        rhs = rhs_tiles(c)[0:kw, t0:t0 + tlen]
                    else:
                        rhs = rhs_tiles[0:kw, c, t0:t0 + tlen]
                    return e.matmul(ps[0:ocw, 0:tlen],
                                    ws.bufs[bi][0:kw, c, oc:oc + ocw],
                                    rhs,
                                    start=start,
                                    stop=(c == nin - 1 and extra_mm is None))
                p.T(fn, reads=(ws.keys[bi], rhs_key), writes=(ps_key,),
                    tag=f"mm_{ws.tag}")
            if extra_mm is not None:
                def fne(e, oc=oc + o0, ocw=ocw, t0=t0, tlen=tlen, ps=ps):
                    return extra_mm(e, oc, ocw, t0, tlen, ps)
                p.T(fne, reads=(ws.keys[bi], rhs_key) + getattr(extra_mm, "reads", ()),
                    writes=(ps_key,), tag=f"mmx_{ws.tag}")
            evac(o0 + oc, ocw, t0, tlen, ps, ps_key)

    b_lo, b_hi = (0, nblocks) if blk_range is None else blk_range
    if not t_outer:
        ws.load_block(0, b_lo)
        for b in range(b_lo, b_hi):
            bi = (b - b_lo) % 2
            if b + 1 < b_hi:
                ws.load_block((b + 1 - b_lo) % 2, b + 1)
            for ti in range(nts):
                inner(b, bi, ti)
    else:
        li = 0
        ws.load_block(0, 0)
        for ti in range(nts):
            for b in range(nblocks):
                bi = li % 2
                if not (ti == nts - 1 and b == nblocks - 1):
                    nb = (b + 1) % nblocks
                    ws.load_block((li + 1) % 2, nb)
                inner(b, bi, ti)
                li += 1



def build_l1_ffn(nc, Ttile=512, fctx=None, io=None):
    local = fctx is None
    ctx = make_ctx(nc) if local else fctx
    nc = ctx.nc
    p = ctx.p
    xT_h = iod(ctx, io, "xT", [D, TC], F32, kind="ExternalInput")
    w1t = iod(ctx, io, "w1t", [8, 128, 8, 512], F32, kind="ExternalInput")
    w2t = iod(ctx, io, "w2t", [8, 128, 8, 512], F32, kind="ExternalInput")
    h1t = iod(ctx, io, "h1t", [8, 128, 32, 512], F32, kind="ExternalInput")
    h2t = iod(ctx, io, "h2t", [8, 128, 32, 512], F32, kind="ExternalInput")
    w3t = iod(ctx, io, "w3t", [2, 128, 32, 512], F32, kind="ExternalInput")
    s1 = iod(ctx, io, "s1", [128, 32], F32, kind="ExternalInput")
    s2 = iod(ctx, io, "s2", [128, 32], F32, kind="ExternalInput")
    sh1 = iod(ctx, io, "sh1", [128, 32], F32, kind="ExternalInput")
    sh2 = iod(ctx, io, "sh2", [128, 32], F32, kind="ExternalInput")
    s3 = iod(ctx, io, "s3", [128, 8], F32, kind="ExternalInput")
    yT_h = iod(ctx, io, "yT", [D, TC], F32, kind="ExternalOutput")

    with (ctx.es if local else contextlib.nullcontext()):
        emit_l1_ffn_body(ctx, xT_h, yT_h,
                         dict(w1t=w1t, w2t=w2t, h1t=h1t, h2t=h2t, w3t=w3t,
                              s1=s1, s2=s2, sh1=sh1, sh2=sh2, s3=s3),
                         Ttile)
        p.barrier_all([f"yT@out{i}" for i in range(TC // Ttile)])
        if local:
            p.finalize(nc)
    return ctx


def emit_l1_ffn_body(ctx, xT_h, yT_h, wh, Ttile):
    nc, p = ctx.nc, ctx.p
    nt = TC // Ttile

    xtile = ctx.sbuf("x", [128, NCH, Ttile], F32)
    xln = ctx.sbuf("xln", [128, NCH, Ttile], BF16)
    t1 = ctx.sbuf("t1", [128, 32, Ttile], BF16)
    t2 = ctx.sbuf("t2", [128, 32, Ttile], BF16)
    scl = ctx.sbuf("scl", [128, 32 * 4 + 8], F32)

    ln = LNHelper(ctx, tchunk=Ttile)

    banks = [(ctx.psum(f"pb{i}", [128, 512], F32), f"pb{i}") for i in range(4)]
    stat = (ctx.psum("pstat", [1, Ttile], F32), "pstat")
    stat2 = (ctx.psum("pstat2", [1, Ttile], F32), "pstat2")
    bc0 = (ctx.psum("pbc0", [128, Ttile], F32), "pbc0")
    bc1 = (ctx.psum("pbc1", [128, Ttile], F32), "pbc1")

    def lds(e):
        return e.dma_start(out=scl[:, 0:32], in_=wh["s1"][:, :])
    p.dma("Y", lds, writes=("scl",), tag="lds1")
    for nm, off, w in (("s2", 32, 32), ("sh1", 64, 32), ("sh2", 96, 32), ("s3", 128, 8)):
        def ld(e, nm=nm, off=off, w=w):
            return e.dma_start(out=scl[:, off:off + w], in_=wh[nm][:, :])
        p.dma("Y", ld, writes=("scl",), tag=f"ld{nm}")

    ws1 = WStream(ctx, "w1", wh["w1t"], D, 4096, tag="w1")
    ws2 = WStream(ctx, "w2", wh["w2t"], D, 4096, tag="w2")
    wsh1 = WStream(ctx, "h1", wh["h1t"], 4096, 4096, tag="h1")
    wsh2 = WStream(ctx, "h2", wh["h2t"], 4096, 4096, tag="h2")
    ws3 = WStream(ctx, "w3", wh["w3t"], 4096, D, tag="w3")

    for it in range(nt):
        t0 = it * Ttile
        load_xT(ctx, xT_h, xtile, "x", t0=t0, tlen=Ttile)
        ln.emit(xtile, "x", xln, "xln", stat, stat2, bc0, bc1)

        def evac_w1(oc, ocw, tt0, tlen, ps, ps_key):
            def fn(e, oc=oc, ocw=ocw, tt0=tt0, tlen=tlen, ps=ps):
                return e.activation(t1[0:ocw, oc // 128, tt0:tt0 + tlen],
                                    ps[0:ocw, 0:tlen], AF.Silu,
                                    scale=scl[0:ocw, oc // 128:oc // 128 + 1])
            p.S(fn, reads=(ps_key, "scl"), writes=("t1",), tag="ev_w1")
        emit_proj(ctx, ws1, xln, "xln", Ttile, evac_w1, banks[0:2])

        def evac_w2(oc, ocw, tt0, tlen, ps, ps_key):
            def fn(e, oc=oc, ocw=ocw, tt0=tt0, tlen=tlen, ps=ps):
                return e.scalar_tensor_tensor(
                    t2[0:ocw, oc // 128, tt0:tt0 + tlen],
                    ps[0:ocw, 0:tlen],
                    scl[0:ocw, 32 + oc // 128:32 + oc // 128 + 1],
                    t1[0:ocw, oc // 128, tt0:tt0 + tlen],
                    op0=ALU.mult, op1=ALU.mult)
            p.V(fn, reads=(ps_key, "scl", "t1"), writes=("t2",), tag="ev_w2")
        emit_proj(ctx, ws2, xln, "xln", Ttile, evac_w2, banks[2:4])

        def evac_h1(oc, ocw, tt0, tlen, ps, ps_key):
            def fn(e, oc=oc, ocw=ocw, tt0=tt0, tlen=tlen, ps=ps):
                return e.activation(t1[0:ocw, oc // 128, tt0:tt0 + tlen],
                                    ps[0:ocw, 0:tlen], AF.Gelu,
                                    scale=scl[0:ocw, 64 + oc // 128:64 + oc // 128 + 1])
            p.S(fn, reads=(ps_key, "scl"), writes=("t1",), tag="ev_h1")
        emit_proj(ctx, wsh1, t2, "t2", Ttile, evac_h1, banks[0:2])

        def evac_h2(oc, ocw, tt0, tlen, ps, ps_key):
            def fn(e, oc=oc, ocw=ocw, tt0=tt0, tlen=tlen, ps=ps):
                return e.activation(t2[0:ocw, oc // 128, tt0:tt0 + tlen],
                                    ps[0:ocw, 0:tlen], AF.Gelu,
                                    scale=scl[0:ocw, 96 + oc // 128:96 + oc // 128 + 1])
            p.S(fn, reads=(ps_key, "scl"), writes=("t2",), tag="ev_h2")
        emit_proj(ctx, wsh2, t1, "t1", Ttile, evac_h2, banks[2:4])

        def evac_w3(oc, ocw, tt0, tlen, ps, ps_key):
            def fn(e, oc=oc, ocw=ocw, tt0=tt0, tlen=tlen, ps=ps):
                return e.scalar_tensor_tensor(
                    xtile[0:ocw, oc // 128, tt0:tt0 + tlen],
                    ps[0:ocw, 0:tlen],
                    scl[0:ocw, 128 + oc // 128:128 + oc // 128 + 1],
                    xtile[0:ocw, oc // 128, tt0:tt0 + tlen],
                    op0=ALU.mult, op1=ALU.add)
            p.V(fn, reads=(ps_key, "scl", "x"), writes=("x",), tag="ev_w3")
        emit_proj(ctx, ws3, t2, "t2", Ttile, evac_w3, banks[0:2])

        store_xT(ctx, yT_h, xtile, "x", f"yT@out{it}", t0=t0, tlen=Ttile)



class SubView:

    def __init__(self, base, t0, tlen):
        self.base = base
        self.t0 = t0
        self.tlen = tlen

    def __getitem__(self, sl):
        par, ch, t = sl
        a = self.t0 + (t.start or 0)
        b = self.t0 + (t.stop if t.stop is not None else self.tlen)
        return self.base[par, ch, a:b]





A_HEADS_LO = [0, 1, 2, 3, 8, 9, 10, 11]
A_HEADS_HI = [4, 5, 6, 7, 12, 13, 14, 15]


def qperm_attn_a():
    perm = []
    for j in range(8):
        perm += list(range(A_HEADS_LO[j] * 64, A_HEADS_LO[j] * 64 + 64))
        perm += list(range(A_HEADS_HI[j] * 64, A_HEADS_HI[j] * 64 + 64))
    return np.array(perm)


def rope_tables_attn_a(pos):
    ch = np.arange(64)
    inv_freq = 1.0 / 10000.0 ** ((ch % 32) * 2.0 / 64.0)
    f = pos[None, :] * inv_freq[:, None]
    cos = np.cos(f)
    sin = np.sin(f)
    sgn = np.where(ch < 32, -1.0, 1.0)[:, None]
    sinS = sin * sgn
    cosT = np.tile(cos, (2, 1)).astype(np.float32)
    sinT = np.tile(sinS, (2, 1)).astype(np.float32)
    return np.ascontiguousarray(cosT), np.ascontiguousarray(sinT)


def emit_rope(ctx, ps, ps_key, dst, dst_slice, dst_key, cosT, cosT_key,
              sinT, sinT_key, tsl, rbuf, tmp, nrows=128):
    p = ctx.p
    tlen = tsl.stop - tsl.start
    for g in range(nrows // 32):
        src_g = g + 1 if g % 2 == 0 else g - 1

        def fc(e, g=g, src_g=src_g):
            return e.tensor_copy(rbuf[g * 32:(g + 1) * 32, 0:tlen],
                                 ps[src_g * 32:(src_g + 1) * 32, 0:tlen])
        p.V(fc, reads=(ps_key,), writes=("rope_rbuf",), tag="rope_rot")

    def f1(e):
        return e.tensor_tensor(tmp[0:nrows, 0:tlen], ps[0:nrows, 0:tlen],
                               cosT[0:nrows, tsl], op=ALU.mult)
    p.V(f1, reads=(ps_key, cosT_key), writes=("rope_tmp",), tag="rope_cos")

    def f2(e):
        return e.tensor_tensor(rbuf[0:nrows, 0:tlen], rbuf[0:nrows, 0:tlen],
                               sinT[0:nrows, tsl], op=ALU.mult)
    p.V(f2, reads=("rope_rbuf", sinT_key), writes=("rope_rbuf",), tag="rope_sin")

    def f3(e):
        return e.tensor_tensor(dst[dst_slice], tmp[0:nrows, 0:tlen],
                               rbuf[0:nrows, 0:tlen], op=ALU.add)
    p.V(f3, reads=("rope_tmp", "rope_rbuf"), writes=(dst_key,), tag="rope_add")


def build_attn_a1(nc, fctx=None, io=None):
    local = fctx is None
    ctx = make_ctx(nc) if local else fctx
    nc = ctx.nc
    p = ctx.p
    xT_h = iod(ctx, io, "xT", [D, TC], F32, kind="ExternalInput")
    wq = iod(ctx, io, "wq", [2, 128, 8, 512], F32, kind="ExternalInput")
    wk = iod(ctx, io, "wk", [1, 128, 8, 256], F32, kind="ExternalInput")
    wv = iod(ctx, io, "wv", [128, 8, 256], F32, kind="ExternalInput")
    qcos_h = iod(ctx, io, "qcos", [128, TC], F32, kind="ExternalInput")
    qsin_h = iod(ctx, io, "qsin", [128, TC], F32, kind="ExternalInput")
    qT_o = iod(ctx, io, "qT", [D, TC], BF16, kind="ExternalOutput")
    kT_o = iod(ctx, io, "kT", [256, TC], BF16, kind="ExternalOutput")
    va_o = iod(ctx, io, "va", [TC, 260], BF16, kind="ExternalOutput")

    ctx.wpool_cfg = (8, 512)
    with (ctx.es if local else contextlib.nullcontext()):
        xtile = ctx.sbuf("x", [128, NCH, TC], F32)
        xln = ctx.sbuf("xln", [128, NCH, TC], BF16)
        qT = ctx.sbuf("qT", [128, NCH, TC], BF16)
        kT = ctx.sbuf("kT", [128, 2, TC], BF16)
        va = ctx.sbuf("va", [128, TC // 128, 260], BF16)
        cosT = ctx.sbuf("cosT", [128, TC], F32)
        sinT = ctx.sbuf("sinT", [128, TC], F32)
        rbuf = ctx.sbuf("rbuf", [128, 512], F32)
        tmp = ctx.sbuf("tmp", [128, 512], F32)
        wvs = ctx.sbuf("wvs", [128, 8, 256], BF16)

        ln = LNHelper(ctx, tchunk=512)
        banks = [(ctx.psum(f"pb{i}", [128, 512], F32), f"pb{i}") for i in range(2)]
        stat = (ctx.psum("pstat", [1, 512], F32), "pstat")
        stat2 = (ctx.psum("pstat2", [1, 512], F32), "pstat2")
        bc0 = (ctx.psum("pbc0", [128, 512], F32), "pbc0")
        bc1 = (ctx.psum("pbc1", [128, 512], F32), "pbc1")
        pv = (ctx.psum("ppv", [128, 512], F32), "ppv")

        load_xT(ctx, xT_h, xtile, "x")
        for nm, sb in (("qcos", cosT), ("qsin", sinT)):
            def ld(e, nm=nm, sb=sb):
                return e.dma_start(out=sb[:, :], in_={"qcos": qcos_h, "qsin": qsin_h}[nm][:, :])
            p.dma("Y", ld, writes=(nm,), tag=f"ld_{nm}")

        def ldwv(e):
            return e.dma_start(out=wvs[:, :, :], in_=wv[:, :, :])
        p.dma("G", ldwv, writes=("wvs",), tag="ld_wvs")

        for tch in range(TC // 512):
            t0 = tch * 512
            xv = SubView(xtile, t0, 512)
            ov = SubView(xln, t0, 512)
            ln.emit(xv, "x", ov, "xln", stat, stat2, bc0, bc1)

        ws_q = WStream(ctx, "wq", wq, D, 1024, tag="wq")

        def evac_q(oc, ocw, tt0, tlen, ps, ps_key):
            j = oc // 128
            emit_rope(ctx, ps, ps_key, qT, (slice(0, 128), j, slice(tt0, tt0 + tlen)),
                      "qT", cosT, "qcos", sinT, "qsin", slice(tt0, tt0 + tlen),
                      rbuf, tmp)
        emit_proj(ctx, ws_q, xln, "xln", TC, evac_q, banks)

        ws_k = WStream(ctx, "wk", wk, D, 256, blk=256, tag="wk")

        def evac_k(oc, ocw, tt0, tlen, ps, ps_key):
            j = oc // 128
            emit_rope(ctx, ps, ps_key, kT, (slice(0, 128), j, slice(tt0, tt0 + tlen)),
                      "kT", cosT, "qcos", sinT, "qsin", slice(tt0, tt0 + tlen),
                      rbuf, tmp)
        emit_proj(ctx, ws_k, xln, "xln", TC, evac_k, banks)

        ones = get_ones(ctx)
        for tq in range(TC // 128):
            pvp, pv_key = pv
            for e_ in range(NCH):
                def fmm(e, e_=e_, tq=tq, pvp=pvp):
                    return e.matmul(pvp[:, 0:256], xln[:, e_, tq * 128:(tq + 1) * 128],
                                    wvs[:, e_, :], start=(e_ == 0), stop=(e_ == NCH - 1))
                p.T(fmm, reads=("xln", "wvs"), writes=(pv_key,), tag="mm_v")
            for h in range(4):
                def fcp(e, h=h, tq=tq, pvp=pvp):
                    return e.tensor_copy(va[:, tq, h * 65:h * 65 + 64],
                                         pvp[:, h * 64:(h + 1) * 64])
                p.V(fcp, reads=(pv_key,), writes=("va",), tag="v_cp")

        def fones(e):
            return e.memset(va[:, :, 64:260:65], 1.0)
        p.G(fones, writes=("va",), tag="va_ones")

        store_xT(ctx, qT_o, qT, "qT", "qT@out", tlen=TC)
        def stk(e):
            return e.dma_start(out=kT_o[:, :].rearrange("(c p) t -> p c t", p=128),
                               in_=kT[:, :, :])
        p.dma("Y", stk, reads=("kT",), writes=("kT@out",), tag="st_kT")

        def stv(e):
            return e.dma_start(out=va_o[:, :].rearrange("(c p) v -> p c v", p=128),
                               in_=va[:, :, :])
        p.dma("Y", stv, reads=("va",), writes=("va@out",), tag="st_va")
        p.barrier_all(["qT@out", "kT@out", "va@out"])
        if local:
            p.finalize(nc)
    return ctx


def build_attn_a2(nc, NB=8192, fctx=None, io=None):
    local = fctx is None
    ctx = make_ctx(nc) if local else fctx
    nc = ctx.nc
    p = ctx.p
    xT_h = iod(ctx, io, "xT", [D, TC], F32, kind="ExternalInput")
    qT_h = iod(ctx, io, "qT", [D, TC], BF16, kind="ExternalInput")
    kT_h = iod(ctx, io, "kT", [NB // TC, 256, TC], BF16, kind="ExternalInput")
    va_h = iod(ctx, io, "va", [NB // TC, TC, 260], BF16, kind="ExternalInput")
    wo = iod(ctx, io, "wo", [2, 128, 8, 512], F32, kind="ExternalInput")
    bo = iod(ctx, io, "bo", [1, D], F32, kind="ExternalInput")
    yT_h = iod(ctx, io, "yT", [D, TC], F32, kind="ExternalOutput")


    ctx.wpool_cfg = (8, 512)
    with (ctx.es if local else contextlib.nullcontext()):
        qT = ctx.sbuf("qT", [128, NCH, TC], BF16)
        kT = ctx.sbuf("kT", [128, 2, NB], BF16)
        va = ctx.sbuf("va", [128, NB // 128, 260], BF16)
        OT = ctx.sbuf("OT", [128, NCH, TC], BF16)
        pT = [ctx.sbuf(f"pT{i}", [128, 512], BF16) for i in range(3)]
        lrow = ctx.sbuf("lrow", [1, 512], F32)
        lbc = ctx.sbuf("lbc", [64, 512], F32)
        xtile = ctx.sbuf("x", [128, NCH, 512], F32)
        bos = ctx.sbuf("bos", [1, D], F32)
        ones = get_ones(ctx)

        def ldq(e):
            return e.dma_start(out=qT[:, :, :], in_=qT_h[:, :].rearrange("(c p) t -> p c t", p=128))
        p.dma("Y", ldq, writes=("qT",), tag="ld_qT")

        for r in range(NB // TC):
            def ldk(e, r=r):
                return e.dma_start(out=kT[:, :, r * TC:(r + 1) * TC],
                                   in_=kT_h[r, :, :].rearrange("(c p) t -> p c t", p=128))
            p.dma("Y", ldk, writes=("kT",), tag="ld_kT")

            def ldv(e, r=r):
                return e.dma_start(out=va[:, r * (TC // 128):(r + 1) * (TC // 128), :],
                                   in_=va_h[r, :, :].rearrange("(c p) v -> p c v", p=128))
            p.dma("Y", ldv, writes=("va",), tag="ld_va")

        def ldb(e):
            return e.dma_start(out=bos[:, :], in_=bo[:, :])
        p.dma("Y", ldb, writes=("bos",), tag="ld_bo")

        pst = [(ctx.psum(f"pst{i}", [128, 512], F32), f"pst{i}") for i in range(3)]
        pot = [(ctx.psum(f"pot{i}", [128, 512], F32), f"pot{i}") for i in range(2)]
        pbc = (ctx.psum("pbc", [64, 512], F32), "pbc")
        banks = [(ctx.psum(f"pb{i}", [128, 512], F32), f"pb{i}") for i in range(2)]

        scale_ap = ctx.cf(64.0 ** -0.5)
        nkt = NB // 128

        for qc in range(TC // 512):
            qsl = slice(qc * 512, (qc + 1) * 512)
            for j in range(NCH):
                for half in range(2):
                    base = half * 64
                    h = (A_HEADS_LO[j] if half == 0 else A_HEADS_HI[j])
                    kh = h // 4
                    kch, kbase = kh // 2, (kh % 2) * 64
                    po, po_key = pot[(j * 2 + half) % 2]

                    def emit_st(kt, kch=kch, kbase=kbase, base=base, qsl=qsl,
                                j=j):
                        ps, ps_key = pst[kt % 3]

                        def fst(e, kch=kch, kbase=kbase, kt=kt, base=base,
                                qsl=qsl, ps=ps, j=j):
                            return e.matmul(
                                ps[:, 0:512],
                                kT[kbase:kbase + 64, kch, kt * 128:(kt + 1) * 128],
                                qT[base:base + 64, j, qsl],
                                start=True, stop=True)
                        p.T(fst, reads=("kT", "qT"), writes=(ps_key,), tag="mm_st")

                    emit_st(0)
                    emit_st(1)
                    for kt in range(nkt):
                        if kt + 2 < nkt:
                            emit_st(kt + 2)
                        ps, ps_key = pst[kt % 3]
                        pb = pT[kt % 3]

                        def fexp(e, ps=ps, pb=pb):
                            return e.activation(pb[:, :], ps[:, 0:512], AF.Exp,
                                                scale=scale_ap)
                        p.S(fexp, reads=(ps_key, "cft"), writes=(f"pT{kt % 3}",),
                            tag="exp")

                        def fpv(e, kt=kt, pb=pb, po=po, kh=kh):
                            return e.matmul(po[0:65, 0:512],
                                            va[:, kt, kh * 65:kh * 65 + 65],
                                            pb[:, :],
                                            start=(kt == 0), stop=(kt == nkt - 1))
                        p.T(fpv, reads=("va", f"pT{kt % 3}"), writes=(po_key,),
                            tag="mm_pv")
                    def frc(e, po=po):
                        return e.reciprocal(lrow[0:1, :], po[64:65, 0:512])
                    p.V(frc, reads=(po_key,), writes=("lrow",), tag="recip")

                    def fbc(e):
                        return e.matmul(pbc[0][0:64, 0:512], ones[0:1, 0:64],
                                        lrow[0:1, :], start=True, stop=True)
                    p.T(fbc, reads=("lrow", "ones"), writes=(pbc[1],), tag="mm_bc")

                    def fcl(e):
                        return e.copy(lbc[:, :], pbc[0][0:64, 0:512])
                    p.S(fcl, reads=(pbc[1],), writes=("lbc",), tag="cp_lbc")

                    def fdv(e, po=po, base=base, j=j, qsl=qsl):
                        return e.tensor_tensor(OT[base:base + 64, j, qsl],
                                               po[0:64, 0:512], lbc[:, :],
                                               op=ALU.mult)
                    p.V(fdv, reads=(po_key, "lbc"), writes=("OT",), tag="div")

        ws_o = WStream(ctx, "wo", wo, D, D, tag="wo")

        def extra_bias(e, oc, ocw, t0, tlen, ps):
            return e.matmul(ps[0:ocw, 0:tlen], bos[0:1, oc:oc + ocw],
                            ones[0:1, 0:tlen], start=False, stop=True)
        extra_bias.reads = ("bos", "ones")

        tload = {}

        def evac_o(oc, ocw, t0, tlen, ps, ps_key):
            j = oc // 128
            if t0 not in tload:
                load_xT(ctx, xT_h, xtile, "xt", t0=t0, tlen=512)
                tload[t0] = True

            def fn(e, j=j, tlen=tlen, ps=ps, ocw=ocw):
                return e.tensor_tensor(xtile[0:ocw, j, 0:tlen], ps[0:ocw, 0:tlen],
                                       xtile[0:ocw, j, 0:tlen], op=ALU.add)
            p.V(fn, reads=(ps_key, "xt"), writes=("xt",), tag="resid")
            if j == NCH - 1:
                store_xT(ctx, yT_h, xtile, "xt", f"yT@o{t0}", t0=t0, tlen=512)
        emit_proj(ctx, ws_o, OT, "OT", TC, evac_o, banks, extra_mm=extra_bias,
                  t_outer=True)

        p.barrier_all([f"yT@o{t0}" for t0 in range(0, TC, 512)])
        if local:
            p.finalize(nc)
    return ctx



def build_attn_b1(nc, fctx=None, io=None):
    local = fctx is None
    ctx = make_ctx(nc) if local else fctx
    nc = ctx.nc
    ctx.wpool_cfg = (8, 512)
    p = ctx.p
    xT_h = iod(ctx, io, "xT", [D, TC], F32, kind="ExternalInput")
    wq = iod(ctx, io, "wq", [2, 128, 8, 512], F32, kind="ExternalInput")
    wk = iod(ctx, io, "wk", [2, 128, 8, 512], F32, kind="ExternalInput")
    wkh = iod(ctx, io, "wkh", [128, 8, 1024], F32, kind="ExternalInput")
    qT_o = iod(ctx, io, "qT", [64, 16, TC], BF16, kind="ExternalOutput")
    kT_o = iod(ctx, io, "kT", [D, TC], BF16, kind="ExternalOutput")
    va_o = iod(ctx, io, "va", [16, TC, 65], BF16, kind="ExternalOutput")

    with (ctx.es if local else contextlib.nullcontext()):
        xtile = ctx.sbuf("x", [128, NCH, 512], F32)
        xln = ctx.sbuf("xln", [128, NCH, TC], BF16)
        qT = ctx.sbuf("qT", [64, 16, TC], BF16)
        kst = [ctx.sbuf(f"kst{i}", [128, 512], BF16) for i in range(2)]
        va = ctx.sbuf("va", [128, TC // 128, 1040], BF16)
        wkhs = ctx.sbuf("wkhs", [128, 8, 1024], BF16)

        ln = LNHelper(ctx, tchunk=512)
        banks = [(ctx.psum(f"pb{i}", [128, 512], F32), f"pb{i}") for i in range(2)]
        stat = (ctx.psum("pstat", [1, 512], F32), "pstat")
        stat2 = (ctx.psum("pstat2", [1, 512], F32), "pstat2")
        bc0 = (ctx.psum("pbc0", [128, 512], F32), "pbc0")
        bc1 = (ctx.psum("pbc1", [128, 512], F32), "pbc1")
        pv = (ctx.psum("ppv", [128, 512], F32), "ppv")

        def ldwkh(e):
            return e.dma_start(out=wkhs[:, :, :], in_=wkh[:, :, :])
        p.dma("G", ldwkh, writes=("wkhs",), tag="ld_wkh")

        for tch in range(TC // 512):
            t0 = tch * 512
            load_xT(ctx, xT_h, xtile, "x", t0=t0, tlen=512)
            ln.emit(xtile, "x", SubView(xln, t0, 512), "xln",
                    stat, stat2, bc0, bc1)

        ws_q = WStream(ctx, "wq", wq, D, 1024, tag="wq")

        def evac_q(oc, ocw, tt0, tlen, ps, ps_key):
            j = oc // 128
            for half in range(2):
                def fc(e, j=j, half=half, tt0=tt0, tlen=tlen, ps=ps):
                    return e.tensor_copy(qT[0:64, 2 * j + half, tt0:tt0 + tlen],
                                         ps[half * 64:half * 64 + 64, 0:tlen])
                p.V(fc, reads=(ps_key,), writes=("qT",), tag="ev_q")
        emit_proj(ctx, ws_q, xln, "xln", TC, evac_q, banks)

        ws_k = WStream(ctx, "wk", wk, D, 1024, tag="wk")
        kc = [0]

        def evac_k(oc, ocw, tt0, tlen, ps, ps_key):
            j = oc // 128
            si = kc[0] % 2
            kc[0] += 1

            def fc(e, si=si, tlen=tlen, ps=ps, ocw=ocw):
                return e.tensor_copy(kst[si][0:ocw, 0:tlen], ps[0:ocw, 0:tlen])
            p.V(fc, reads=(ps_key,), writes=(f"kst{si}",), tag="ev_k")

            def fs(e, si=si, j=j, tt0=tt0, tlen=tlen, ocw=ocw):
                return e.dma_start(out=kT_o[j * 128:j * 128 + ocw, tt0:tt0 + tlen],
                                   in_=kst[si][0:ocw, 0:tlen])
            p.dma("Y", fs, reads=(f"kst{si}",), writes=(f"kT@out{j}_{tt0}",),
                  tag="st_k")
        emit_proj(ctx, ws_k, xln, "xln", TC, evac_k, banks)

        pvp, pv_key = pv
        for tq in range(TC // 128):
            for hg in range(2):
                for e_ in range(NCH):
                    def fmm(e, e_=e_, tq=tq, hg=hg):
                        return e.matmul(pvp[:, 0:512],
                                        xln[:, e_, tq * 128:(tq + 1) * 128],
                                        wkhs[:, e_, hg * 512:(hg + 1) * 512],
                                        start=(e_ == 0), stop=(e_ == NCH - 1))
                    p.T(fmm, reads=("xln", "wkhs"), writes=(pv_key,), tag="mm_v")
                for hh in range(8):
                    h = hg * 8 + hh

                    def fcp(e, h=h, hh=hh, tq=tq):
                        return e.tensor_copy(va[:, tq, h * 65:h * 65 + 64],
                                             pvp[:, hh * 64:(hh + 1) * 64])
                    p.V(fcp, reads=(pv_key,), writes=("va",), tag="v_cp")

        def fones(e):
            return e.memset(va[:, :, 64:1040:65], 1.0)
        p.G(fones, writes=("va",), tag="va_ones")

        def stq(e):
            return e.dma_start(out=qT_o[:, :, :], in_=qT[:, :, :])
        p.dma("Y", stq, reads=("qT",), writes=("qT@out",), tag="st_qT")

        for h in range(16):
            def stv(e, h=h):
                return e.dma_start(out=va_o[h, :, :].rearrange("(q p) v -> p q v", p=128),
                                   in_=va[:, :, h * 65:(h + 1) * 65])
            p.dma("Y", stv, reads=("va",), writes=(f"va@out{h}",), tag="st_va")
        p.barrier_all([f"va@out{h}" for h in range(16)] + ["qT@out"]
                      + [f"kT@out{j}_{t0}" for j in range(8) for t0 in range(0, TC, 512)])
        if local:
            p.finalize(nc)
    return ctx


def build_attn_b2(nc, NB=8192, fctx=None, io=None):
    local = fctx is None
    ctx = make_ctx(nc) if local else fctx
    nc = ctx.nc
    ctx.wpool_cfg = (8, 512)
    p = ctx.p
    xT_h = iod(ctx, io, "xT", [D, TC], F32, kind="ExternalInput")
    qT_h = iod(ctx, io, "qT", [64, 16, TC], BF16, kind="ExternalInput")
    kT_h = iod(ctx, io, "kT", [NB // TC, D, TC], BF16, kind="ExternalInput")
    va_h = iod(ctx, io, "va", [NB // TC, 16, TC, 65], BF16, kind="ExternalInput")
    wo = iod(ctx, io, "wo", [2, 128, 8, 512], F32, kind="ExternalInput")
    bo = iod(ctx, io, "bo", [1, D], F32, kind="ExternalInput")
    yT_h = iod(ctx, io, "yT", [D, TC], F32, kind="ExternalOutput")

    with (ctx.es if local else contextlib.nullcontext()):
        qT = ctx.sbuf("qT", [64, 16, TC], BF16)
        kth = [ctx.sbuf(f"kth{i}", [64, NB], BF16) for i in range(2)]
        vah = [ctx.sbuf(f"vah{i}", [128, NB // 128, 65], BF16) for i in range(2)]
        OT = ctx.sbuf("OT", [128, NCH, TC], BF16)
        pT = [ctx.sbuf(f"pT{i}", [128, 512], BF16) for i in range(3)]
        lrow = ctx.sbuf("lrow", [1, 512], F32)
        lbc = ctx.sbuf("lbc", [64, 512], F32)
        xtile = ctx.sbuf("x", [128, NCH, 512], F32)
        bos = ctx.sbuf("bos", [1, D], F32)
        ones = get_ones(ctx)

        def ldq(e):
            return e.dma_start(out=qT[:, :, :], in_=qT_h[:, :, :])
        p.dma("Y", ldq, writes=("qT",), tag="ld_qT")

        def ldb(e):
            return e.dma_start(out=bos[:, :], in_=bo[:, :])
        p.dma("Y", ldb, writes=("bos",), tag="ld_bo")

        pst = [(ctx.psum(f"pst{i}", [128, 512], F32), f"pst{i}") for i in range(3)]
        pot = [(ctx.psum(f"pot{i}", [128, 512], F32), f"pot{i}") for i in range(2)]
        pbc = (ctx.psum("pbc", [64, 512], F32), "pbc")
        banks = [(ctx.psum(f"pb{i}", [128, 512], F32), f"pb{i}") for i in range(2)]

        scale_ap = ctx.cf(1.0 / 32.0)
        nkt = NB // 128

        def load_head(h):
            bi = h % 2
            for r in range(NB // TC):
                def fk(e, h=h, bi=bi, r=r):
                    return e.dma_start(out=kth[bi][:, r * TC:(r + 1) * TC],
                                       in_=kT_h[r, h * 64:(h + 1) * 64, :])
                p.dma("Y", fk, reads=(), writes=(f"kth{bi}",), tag="ld_kth",
                      role=f"kv{bi}")

                def fv(e, h=h, bi=bi, r=r):
                    return e.dma_start(
                        out=vah[bi][:, r * (TC // 128):(r + 1) * (TC // 128), :],
                        in_=va_h[r, h, :, :].rearrange("(c p) v -> p c v", p=128))
                p.dma("Y", fv, reads=(), writes=(f"vah{bi}",), tag="ld_vah",
                      role=f"kv{bi}")

        load_head(0)
        for h in range(16):
            bi = h % 2
            if h + 1 < 16:
                load_head(h + 1)
            for qc in range(TC // 512):
                qsl = slice(qc * 512, (qc + 1) * 512)
                po, po_key = pot[qc % 2]

                def emit_st(kt, h=h, bi=bi, qsl=qsl):
                    ps, ps_key = pst[kt % 3]

                    def fst(e, h=h, bi=bi, kt=kt, qsl=qsl, ps=ps):
                        return e.matmul(ps[:, 0:512],
                                        kth[bi][0:64, kt * 128:(kt + 1) * 128],
                                        qT[0:64, h, qsl], start=True, stop=True)
                    p.T(fst, reads=(f"kth{bi}", "qT"), writes=(ps_key,), tag="mm_st")

                emit_st(0)
                emit_st(1)
                for kt in range(nkt):
                    if kt + 2 < nkt:
                        emit_st(kt + 2)
                    ps, ps_key = pst[kt % 3]
                    pb = pT[kt % 3]

                    def fexp(e, ps=ps, pb=pb):
                        return e.activation(pb[:, :], ps[:, 0:512], AF.Exp,
                                            scale=scale_ap)
                    p.S(fexp, reads=(ps_key, "cft"), writes=(f"pT{kt % 3}",), tag="exp")

                    def fpv(e, kt=kt, pb=pb, po=po, bi=bi):
                        return e.matmul(po[0:65, 0:512], vah[bi][:, kt, :], pb[:, :],
                                        start=(kt == 0), stop=(kt == nkt - 1))
                    p.T(fpv, reads=(f"vah{bi}", f"pT{kt % 3}"), writes=(po_key,),
                        tag="mm_pv")

                def frc(e, po=po):
                    return e.reciprocal(lrow[0:1, :], po[64:65, 0:512])
                p.V(frc, reads=(po_key,), writes=("lrow",), tag="recip")

                def fbc(e):
                    return e.matmul(pbc[0][0:64, 0:512], ones[0:1, 0:64],
                                    lrow[0:1, :], start=True, stop=True)
                p.T(fbc, reads=("lrow", "ones"), writes=(pbc[1],), tag="mm_bc")

                def fcl(e):
                    return e.copy(lbc[:, :], pbc[0][0:64, 0:512])
                p.S(fcl, reads=(pbc[1],), writes=("lbc",), tag="cp_lbc")

                def fdv(e, po=po, h=h, qsl=qsl):
                    return e.tensor_tensor(OT[(h % 2) * 64:(h % 2) * 64 + 64,
                                              h // 2, qsl],
                                           po[0:64, 0:512], lbc[:, :], op=ALU.mult)
                p.V(fdv, reads=(po_key, "lbc"), writes=("OT",), tag="div")

        ws_o = WStream(ctx, "wo", wo, D, D, tag="wo")

        def extra_bias(e, oc, ocw, t0, tlen, ps):
            return e.matmul(ps[0:ocw, 0:tlen], bos[0:1, oc:oc + ocw],
                            ones[0:1, 0:tlen], start=False, stop=True)
        extra_bias.reads = ("bos", "ones")

        tload = {}

        def evac_o(oc, ocw, t0, tlen, ps, ps_key):
            j = oc // 128
            if t0 not in tload:
                load_xT(ctx, xT_h, xtile, "xt", t0=t0, tlen=512)
                tload[t0] = True

            def fn(e, j=j, tlen=tlen, ps=ps, ocw=ocw):
                return e.tensor_tensor(xtile[0:ocw, j, 0:tlen], ps[0:ocw, 0:tlen],
                                       xtile[0:ocw, j, 0:tlen], op=ALU.add)
            p.V(fn, reads=(ps_key, "xt"), writes=("xt",), tag="resid")
            if j == NCH - 1:
                store_xT(ctx, yT_h, xtile, "xt", f"yT@o{t0}", t0=t0, tlen=512)
        emit_proj(ctx, ws_o, OT, "OT", TC, evac_o, banks, extra_mm=extra_bias,
                  t_outer=True)

        p.barrier_all([f"yT@o{t0}" for t0 in range(0, TC, 512)])
        if local:
            p.finalize(nc)
    return ctx


def attn_b_host(inputs, xT_full):
    g = lambda k: np.ascontiguousarray(np.asarray(inputs[k], np.float32))
    wq = pack_weight(g("L0_attn_b_Wq"))
    wkm = g("L0_attn_b_Wk")
    wk = pack_weight(wkm)
    wkh = np.ascontiguousarray(wkm.reshape(8, 128, 1024).transpose(1, 0, 2))
    wo = pack_weight(g("L0_attn_b_Wo"))
    bo = g("L0_attn_b_bo").reshape(1, D)

    nc1 = bass.Bass("TRN2", target_bir_lowering=False)
    c1 = build_attn_b1(nc1)
    maps = []
    for c in range(NCORES):
        m = dict(xT=_shard_cols(xT_full, c), wq=wq, wk=wk, wkh=wkh)
        m.update(c1.consts)
        if hasattr(c1, "cf_map"):
            m["cf_consts"] = c1.cf_array()
        maps.append(m)
    r1 = run_bass_kernel_spmd(nc1, maps, core_ids=list(range(NCORES)))

    kT_full = [np.stack([r1.results[b * 4 + i]["kT"] for i in range(4)])
               for b in range(2)]
    va_full = [np.stack([r1.results[b * 4 + i]["va"] for i in range(4)])
               for b in range(2)]

    nc2 = bass.Bass("TRN2", target_bir_lowering=False)
    c2 = build_attn_b2(nc2)
    maps2 = []
    for c in range(NCORES):
        b = c // 4
        m = dict(xT=_shard_cols(xT_full, c), qT=r1.results[c]["qT"],
                 kT=kT_full[b], va=va_full[b], wo=wo, bo=bo)
        m.update(c2.consts)
        m["cf_consts"] = c2.cf_array()
        maps2.append(m)
    r2 = run_bass_kernel_spmd(nc2, maps2, core_ids=list(range(NCORES)))
    return np.concatenate([r2.results[c]["yT"] for c in range(NCORES)], axis=1)





FXE = 3200
FKE = 3072
FQE = 2560
FOWN = 1152


class ShiftView:

    def __init__(self, base, nch_split=4):
        self.base = base
        self.split = nch_split

    def __getitem__(self, sl):
        par, c, t = sl
        off = 1 if c < self.split else 0
        a = (t.start or 0) + off
        b = t.stop + off
        return self.base[par, c, a:b]


def rope_tables_xpos(pos, invert):
    DH, WINDOW, N = 64, 512, 8192
    ch = np.arange(64)
    half = (ch % 32).astype(np.float64)
    inv_freq = 1.0 / 10000.0 ** (half * 2.0 / DH)
    posc = np.clip(pos, 0, None).astype(np.float64)
    f = posc[None, :] * inv_freq[:, None]
    base = (2.0 * half + 0.4 * DH) / (1.4 * DH)
    power = (posc - N // 2) / float(WINDOW // 2)
    scale = base[:, None] ** (power[None, :] * (-1.0 if invert else 1.0))
    cos = np.cos(f) * scale
    sin = np.sin(f) * scale
    sgn = np.where(ch < 32, -1.0, 1.0)[:, None]
    cosT = np.tile(cos, (2, 1)) * 1.0
    sinT = np.tile(sin, (2, 1)) * np.concatenate([sgn[:64]] * 2, 0)
    return (np.ascontiguousarray(cosT.astype(np.float32)),
            np.ascontiguousarray(sinT.astype(np.float32)))


def local_masks():
    M = np.zeros((128, 8, 512), np.float32)
    for di, d in enumerate(range(-4, 4)):
        kp = np.arange(128)[:, None]
        qp = np.arange(512)[None, :]
        imj = qp - (d * 128 + kp)
        M[:, di, :] = np.where((imj >= 0) & (imj <= 512), 0.0, -30000.0)
    return np.ascontiguousarray(M.astype(BF))


def build_l0ffn_attn(nc, do_ln, fctx=None, io=None):
    local = fctx is None
    ctx = make_ctx(nc) if local else fctx
    nc = ctx.nc
    ctx.wpool_cfg = (8, 512)
    p = ctx.p
    xin_h = iod(ctx, io, "xin", [D, FXE], F32, kind="ExternalInput")
    wk_h = iod(ctx, io, "wkp", [1, 128, 8, 512], F32, kind="ExternalInput")
    wq_h = iod(ctx, io, "wqp", [1, 128, 8, 512], F32, kind="ExternalInput")
    wvt_h = iod(ctx, io, "wvt", [128, 8, 512], F32, kind="ExternalInput")
    wo_h = iod(ctx, io, "wop", [2, 128, 4, 512], F32, kind="ExternalInput")
    qs_h = iod(ctx, io, "qs", [128, 2], F32, kind="ExternalInput")
    qcos_h = iod(ctx, io, "qcos", [128, FQE], BF16, kind="ExternalInput")
    qsin_h = iod(ctx, io, "qsin", [128, FQE], BF16, kind="ExternalInput")
    kcos_h = iod(ctx, io, "kcos", [128, FKE], BF16, kind="ExternalInput")
    ksin_h = iod(ctx, io, "ksin", [128, FKE], BF16, kind="ExternalInput")
    msk_h = iod(ctx, io, "msk", [128, 8, 512], BF16, kind="ExternalInput")
    mv_h = iod(ctx, io, "mv", [128, 12], F32, kind="ExternalInput")
    xp_h = iod(ctx, io, "xp", [D, 2560], F32,
                          kind="ExternalOutput" if do_ln else "Internal")
    x1_h = iod(ctx, io, "x1", [D, 2560], F32, kind="ExternalOutput")
    if do_ln:
        xlnx_h = iod(ctx, io, "xlnx", [D, FXE], BF16, kind="Internal")

    with (ctx.es if local else contextlib.nullcontext()):
        B = [(ctx.psum(f"ps{i}", [128, 512], F32), f"ps{i}") for i in range(8)]
        xsm = ctx.sbuf("xsm", [128, NCH, 520], F32)
        xlt = [ctx.sbuf("xlt0", [128, NCH, 520], BF16)]
        kT = ctx.sbuf("kT", [128, 4, FKE], BF16)
        qT = ctx.sbuf("qT", [128, 4, FQE], BF16)
        va = ctx.sbuf("va", [128, FKE // 128, 520], BF16)
        OT = ctx.sbuf("OT", [128, 4, FQE], BF16)
        tmpA_s = [ctx.sbuf(f"tmpA{i}", [128, 512], F32) for i in range(2)]
        tmpB_s = [ctx.sbuf("tmpB0", [128, 512], F32)] * 2
        rbuf_s = [ctx.sbuf("rbufX", [128, 512], F32)] * 2
        rn_s = [ctx.sbuf("rn0", [1, 1024], F32)] * 2
        nr_cnt = [0]
        tcos = ctx.sbuf("tcos", [128, FKE], BF16)
        tsin = ctx.sbuf("tsin", [128, FKE], BF16)
        qcos = kcos = tcos
        qsin = ksin = tsin
        msk = ctx.sbuf("msk", [128, 8, 512], BF16)
        mv = ctx.sbuf("mv", [128, 12], F32)
        qss = ctx.sbuf("qss", [128, 2], F32)
        pT = [ctx.sbuf(f"pTl{i}", [128, 512], BF16) for i in range(3)]
        lrow = ctx.sbuf("lrow", [1, 512], F32)
        lbc = ctx.sbuf("lbc", [64, 512], F32)
        xres = xsm
        ones = get_ones(ctx)
        ln = LNHelper(ctx, tchunk=512) if do_ln else None

        for nm, sb in (("mv", mv), ("qss", qss)):
            def ld(e, nm=nm, sb=sb):
                h = {"mv": mv_h, "qss": qs_h}[nm]
                return e.dma_start(out=sb[:, :], in_=h[:, :])
            p.dma("Y", ld, writes=(nm,), tag=f"ld_{nm}")

        def load_tables(which):
            for nm, sb, h, w in ((which + "cos", tcos,
                                  kcos_h if which == "k" else qcos_h,
                                  FKE if which == "k" else FQE),
                                 (which + "sin", tsin,
                                  ksin_h if which == "k" else qsin_h,
                                  FKE if which == "k" else FQE)):
                def ld(e, sb=sb, h=h, w=w):
                    return e.dma_start(out=sb[:, 0:w], in_=h[:, :])
                p.dma("Y", ld, writes=("tcos" if "cos" in nm else "tsin",),
                      tag=f"ld_{nm}")

        def ldm(e):
            return e.dma_start(out=msk[:, :, :], in_=msk_h[:, :, :])
        p.dma("Y", ldm, writes=("msk",), tag="ld_msk")

        if do_ln:
            nchk = (FXE + 511) // 512
            xpt = ctx.sbuf("xpt", [128, NCH, 512], F32)
            for ic in range(nchk):
                t0 = ic * 512
                tl = min(512, FXE - t0)
                load_xT(ctx, xin_h, xsm, "xsm", t0=t0, tlen=tl)
                w0 = max(0, 640 - t0)
                w1 = max(w0, min(tl, FXE - t0))
                pass

                def out2fn(c, a, b, xpt=xpt, w0=w0):
                    return xpt[:, c, a - w0:b - w0]
                ln.emit(xsm, "xsm", xlt[0], "xlt0", B[0], B[1], B[2], B[3],
                        tlen=tl,
                        out2=((out2fn, (w0, w1)) if w1 > w0 else None),
                        out2key="xpt")
                st = xlnx_h[:, t0:t0 + tl].rearrange("(c p) t -> p c t", p=128)

                def fst(e, st=st, tl=tl):
                    return e.dma_start(out=st, in_=xlt[0][:, :, 0:tl])
                p.dma("Y", fst, reads=("xlt0",), writes=(f"xlnx{ic}",), tag="st_xln")
                if w1 > w0:
                    dst = xp_h[:, t0 + w0 - 640:t0 + w1 - 640]

                    def fsp(e, dst=dst, xpt=xpt, w0=w0, w1=w1):
                        return e.dma_start(
                            out=dst.rearrange("(c p) t -> p c t", p=128),
                            in_=xpt[:, :, 0:w1 - w0])
                    p.dma("Y", fsp, reads=("xpt",), writes=(f"xp{ic}", "xpt"), tag="st_xp")
            xsrc_h = xlnx_h
            src_keys = [f"xlnx{i}" for i in range(nchk)]
        else:
            xsrc_h = xin_h
            src_keys = []

        def load_xtile(bi, c0, width, key):
            ap = xsrc_h[:, c0 - 1:c0 + width].rearrange("(c p) t -> p c t", p=128)

            def fn(e, ap=ap, bi=bi, width=width):
                return e.dma_start(out=xlt[bi][:, :, 0:width + 1], in_=ap)
            p.dma("G" if not do_ln else "Y", fn, reads=tuple(src_keys),
                  writes=(key,), tag="ld_xt")

        load_tables("k")
        ws_k = WStream(ctx, "wk", wk_h, D, 512, tag="wk")
        wvs = ctx.sbuf("wvs", [128, 8, 512], BF16)

        def ldwv(e):
            return e.dma_start(out=wvs[:, :, :], in_=wvt_h[:, :, :])
        p.dma("G", ldwv, writes=("wvs",), tag="ld_wvs")

        def norm_rope_evac(dst, dst_key, cosT, cosK, sinT, sinK, scol,
                           oc, tt0, tlen, ps, ps_key, tabsl):
            j = oc // 128
            si = nr_cnt[0] % 2
            nr_cnt[0] += 1
            tmpA, tmpB, rbuf, rn = tmpA_s[si], tmpB_s[si], rbuf_s[si], rn_s[si]
            kA, kB, kR, kN = f"tmpA{si}", "tmpB0", "rbufX", "rn0"
            def fsq(e, ps=ps, tlen=tlen):
                return e.activation(tmpA[:, 0:tlen], ps[0:128, 0:tlen], AF.Square)
            p.S(fsq, reads=(ps_key,), writes=(kA,), tag="nr_sq")
            prs = [B[6], B[5]] if si == 0 else [B[7], B[6]]
            for half in range(2):
                pr, pr_key = prs[half]

                def fmm(e, half=half, tlen=tlen, pr=pr):
                    return e.matmul(pr[0:1, 0:tlen],
                                    ones[half * 64:half * 64 + 64, 0:1],
                                    tmpA[half * 64:half * 64 + 64, 0:tlen],
                                    start=True, stop=True)
                p.T(fmm, reads=(kA, "ones"), writes=(pr_key,), tag="nr_mm")
            for half in range(2):
                o_ = half * 512
                pr, pr_key = prs[half]

                def fmx(e, o_=o_, tlen=tlen, pr=pr):
                    return e.tensor_scalar_max(rn[0:1, o_:o_ + tlen],
                                               pr[0:1, 0:tlen], 1e-24)
                p.V(fmx, reads=(pr_key,), writes=(kN,), tag="nr_max")

                def fsqr(e, o_=o_, tlen=tlen):
                    return e.activation(rn[0:1, o_:o_ + tlen], rn[0:1, o_:o_ + tlen],
                                        AF.Sqrt)
                p.S(fsqr, reads=(kN,), writes=(kN,), tag="nr_sqrt")

                def frec(e, o_=o_, tlen=tlen):
                    return e.reciprocal(rn[0:1, o_:o_ + tlen], rn[0:1, o_:o_ + tlen])
                p.V(frec, reads=(kN,), writes=(kN,), tag="nr_rec")
            pb_, pb_key = B[7] if si == 0 else B[5]
            for half in range(2):
                def fbc(e, half=half, tlen=tlen, pb_=pb_):
                    return e.matmul(pb_[half * 64:half * 64 + 64, 0:tlen],
                                    ones[0:1, 0:64],
                                    rn[0:1, half * 512:half * 512 + tlen],
                                    start=True, stop=True)
                p.T(fbc, reads=(kN, "ones"), writes=(pb_key,), tag="nr_bc")
            def fsc(e, ps=ps, tlen=tlen, scol=scol):
                return e.activation(tmpA[:, 0:tlen], ps[0:128, 0:tlen], AF.Copy,
                                    scale=qss[:, scol:scol + 1])
            p.S(fsc, reads=(ps_key, "qss"), writes=(kA,), tag="nr_scale")

            def fnm(e, tlen=tlen, pb_=pb_):
                return e.tensor_tensor(tmpB[:, 0:tlen], tmpA[:, 0:tlen],
                                       pb_[0:128, 0:tlen], op=ALU.mult)
            p.V(fnm, reads=(kA, pb_key), writes=(kB,), tag="nr_norm")
            for gq in range(4):
                src_g = gq + 1 if gq % 2 == 0 else gq - 1

                def fc(e, gq=gq, src_g=src_g, tlen=tlen):
                    return e.tensor_copy(rbuf[gq * 32:(gq + 1) * 32, 0:tlen],
                                         tmpB[src_g * 32:(src_g + 1) * 32, 0:tlen])
                p.V(fc, reads=(kB,), writes=(kR,), tag="nr_rot")

            def f1(e, tlen=tlen, tabsl=tabsl):
                return e.tensor_tensor(tmpA[:, 0:tlen], tmpB[:, 0:tlen],
                                       cosT[:, tabsl], op=ALU.mult)
            p.V(f1, reads=(kB, cosK), writes=(kA,), tag="nr_cos")

            def f2(e, tlen=tlen, tabsl=tabsl):
                return e.tensor_tensor(rbuf[:, 0:tlen], rbuf[:, 0:tlen],
                                       sinT[:, tabsl], op=ALU.mult)
            p.V(f2, reads=(kR, sinK), writes=(kR,), tag="nr_sin")

            def f3(e, j=j, tt0=tt0, tlen=tlen):
                return e.tensor_tensor(dst[:, j, tt0:tt0 + tlen], tmpA[:, 0:tlen],
                                       rbuf[:, 0:tlen], op=ALU.add)
            p.V(f3, reads=(kA, kR), writes=(dst_key,), tag="nr_add")

        for kt6 in range(FKE // 512):
            c0 = 128 + kt6 * 512
            load_xtile(0, c0, 512, "xlt0")
            sv = ShiftView(xlt[0])

            def rhs_fn(c, sv=sv):
                return sv[slice(0, 128), c, slice(0, 512)]
            def evac_k(oc, ocw, tt0, tlen, ps, ps_key, kt6=kt6):
                norm_rope_evac(kT, "kT", kcos, "tcos", ksin, "tsin", 1,
                               oc, kt6 * 512 + tt0, tlen, ps, ps_key,
                               slice(kt6 * 512 + tt0, kt6 * 512 + tt0 + tlen))
            emit_proj(ctx, ws_k, rhs_fn, "xlt0", 512, evac_k, B[0:2])
            for tq in range(4):
                pvp, pv_key = B[2 + tq % 2]
                for e_ in range(NCH):
                    def fmm(e, e_=e_, tq=tq, sv=sv, pvp=pvp):
                        return e.matmul(pvp[:, 0:512],
                                        sv[slice(0, 128), e_,
                                           slice(tq * 128, (tq + 1) * 128)],
                                        wvs[:, e_, :],
                                        start=(e_ == 0), stop=(e_ == NCH - 1))
                    p.T(fmm, reads=("xlt0", "wvs"), writes=(pv_key,), tag="mm_v")
                kti = kt6 * 4 + tq
                for h in range(8):
                    def fcp(e, h=h, kti=kti, pvp=pvp):
                        return e.tensor_copy(va[:, kti, h * 65:h * 65 + 64],
                                             pvp[:, h * 64:(h + 1) * 64])
                    p.V(fcp, reads=(pv_key,), writes=("va",), tag="v_cp")

        def fones(e):
            return e.memset(va[:, :, 64:520:65], 1.0)
        p.G(fones, writes=("va",), tag="va_ones")

        load_tables("q")
        ws_q = WStream(ctx, "wq", wq_h, D, 512, tag="wq")
        for qt5 in range(FQE // 512):
            c0 = 640 + qt5 * 512
            load_xtile(0, c0, 512, "xlt0")
            sv = ShiftView(xlt[0])

            def rhs_fn(c, sv=sv):
                return sv[slice(0, 128), c, slice(0, 512)]

            def evac_q(oc, ocw, tt0, tlen, ps, ps_key, qt5=qt5):
                norm_rope_evac(qT, "qT", qcos, "tcos", qsin, "tsin", 0,
                               oc, qt5 * 512 + tt0, tlen, ps, ps_key,
                               slice(qt5 * 512 + tt0, qt5 * 512 + tt0 + tlen))
            emit_proj(ctx, ws_q, rhs_fn, "xlt0", 512, evac_q, B[0:2])

        exp8 = ctx.cf(8.0)
        for qt in range(5):
            qsl = slice(qt * 512, (qt + 1) * 512)
            for h in range(8):
                base = (h % 2) * 64
                ch = h // 2
                po, po_key = B[4 + (qt * 8 + h) % 2]
                STB = [B[2], B[3], B[0]]

                def emit_st(di, base=base, ch=ch, qt=qt, qsl=qsl):
                    kcol = qt * 512 + di * 128
                    ps, ps_key = STB[di % 3]

                    def fst(e, base=base, ch=ch, kcol=kcol, qsl=qsl, ps=ps):
                        return e.matmul(ps[:, 0:512],
                                        kT[base:base + 64, ch, kcol:kcol + 128],
                                        qT[base:base + 64, ch, qsl],
                                        start=True, stop=True)
                    p.T(fst, reads=("kT", "qT"), writes=(ps_key,), tag="lmm_st")

                emit_st(0)
                emit_st(1)
                for di in range(8):
                    if di + 2 < 8:
                        emit_st(di + 2)
                    kti = (qt * 512 + di * 128) // 128
                    ps, ps_key = STB[di % 3]
                    pb = pT[di % 3]

                    def fmsk(e, ps=ps, di=di):
                        return e.tensor_tensor(ps[:, 0:512], ps[:, 0:512],
                                               msk[:, di, :], op=ALU.add)
                    p.V(fmsk, reads=(ps_key, "msk"), writes=(ps_key,), tag="lmsk")

                    def fexp(e, ps=ps, pb=pb):
                        return e.activation(pb[:, :], ps[:, 0:512], AF.Exp,
                                            scale=exp8)
                    p.S(fexp, reads=(ps_key, "cft"), writes=(f"pTl{di % 3}",),
                        tag="lexp")
                    mvi = None
                    if qt * 512 + di * 128 < 1024:
                        mvi = qt * 8 + di if qt == 0 else 8 + di - 4
                    if mvi is not None:
                        def fmv(e, mvi=mvi, pb=pb):
                            return e.tensor_scalar_mul(pb[:, :], pb[:, :],
                                                       mv[:, mvi:mvi + 1])
                        p.V(fmv, reads=("mv", f"pTl{di % 3}"),
                            writes=(f"pTl{di % 3}",), tag="lmv")

                    def fpv(e, kti=kti, h=h, pb=pb, po=po, di=di):
                        return e.matmul(po[0:65, 0:512],
                                        va[:, kti, h * 65:h * 65 + 65],
                                        pb[:, :], start=(di == 0), stop=(di == 7))
                    p.T(fpv, reads=("va", f"pTl{di % 3}"), writes=(po_key,),
                        tag="lmm_pv")
                def fre(e, po=po):
                    return e.tensor_scalar_add(lrow[0:1, :], po[64:65, 0:512],
                                               1e-9)
                p.V(fre, reads=(po_key,), writes=("lrow",), tag="leps")

                def frc(e):
                    return e.reciprocal(lrow[0:1, :], lrow[0:1, :])
                p.V(frc, reads=("lrow",), writes=("lrow",), tag="lrecip")
                pb2, pb2_key = B[6]

                def fbc(e, pb2=pb2):
                    return e.matmul(pb2[0:64, 0:512], ones[0:1, 0:64],
                                    lrow[0:1, :], start=True, stop=True)
                p.T(fbc, reads=("lrow", "ones"), writes=(pb2_key,), tag="lmm_bc")

                def fcl(e, pb2=pb2):
                    return e.copy(lbc[:, :], pb2[0:64, 0:512])
                p.S(fcl, reads=(pb2_key,), writes=("lbc",), tag="lcp")

                def fdv(e, po=po, base=base, ch=ch, qsl=qsl):
                    return e.tensor_tensor(OT[base:base + 64, ch, qsl],
                                           po[0:64, 0:512], lbc[:, :], op=ALU.mult)
                p.V(fdv, reads=(po_key, "lbc"), writes=("OT",), tag="ldiv")

        ws_o = WStream(ctx, "wo", wo_h, 512, D, tag="wo")
        xp_src = xp_h if do_ln else None

        def rhs_o(c):
            return OTview[c]

        class OView:
            def __getitem__(self, c):
                return None
        OTs = [None]

        def rhs_o2(c):
            return OT[:, c, 0:2560]

        tload = {}

        def evac_o(oc, ocw, t0, tlen, ps, ps_key):
            j = oc // 128
            if t0 not in tload:
                tload[t0] = True
                if do_ln:
                    ap = xp_h[:, t0:t0 + tlen]
                else:
                    ap = xin_h[:, 640 + t0:640 + t0 + tlen]

                def fl(e, ap=ap, tlen=tlen):
                    return e.dma_start(out=xres[:, :, 0:tlen],
                                       in_=ap.rearrange("(c p) t -> p c t", p=128))
                p.dma("Y", fl, reads=tuple(f"xp{i}" for i in
                                           range((FXE + 511) // 512)) if do_ln
                      else (), writes=("xsm",), tag="ld_xres")

            def fn(e, j=j, tlen=tlen, ps=ps, ocw=ocw):
                return e.tensor_tensor(xres[0:ocw, j, 0:tlen], ps[0:ocw, 0:tlen],
                                       xres[0:ocw, j, 0:tlen], op=ALU.add)
            p.V(fn, reads=(ps_key, "xsm"), writes=("xsm",), tag="o_resid")
            if j == NCH - 1:
                dst = x1_h[:, t0:t0 + tlen]

                def fs(e, dst=dst, tlen=tlen):
                    return e.dma_start(out=dst.rearrange("(c p) t -> p c t", p=128),
                                       in_=xres[:, :, 0:tlen])
                p.dma("Y", fs, reads=("xsm",), writes=(f"x1@o{t0}",), tag="st_x1")
        emit_proj(ctx, ws_o, rhs_o2, "OT", 2560, evac_o, B[0:2], t_outer=True)

        p.barrier_all([f"x1@o{t0}" for t0 in range(0, 2560, 512)]
                      + ([f"xp{i}" for i in range((FXE + 511) // 512)] if do_ln else []))
        if local:
            p.finalize(nc)
    return ctx


def build_l0ffn_ffn(nc, add_module, fctx=None, io=None):
    local = fctx is None
    ctx = make_ctx(nc) if local else fctx
    nc = ctx.nc
    ctx.wpool_cfg = (22, 512)
    p = ctx.p
    x1_h = iod(ctx, io, "x1", [D, 2560], F32, kind="ExternalInput")
    w1_h = iod(ctx, io, "w1p", [11, 128, 8, 512], F32, kind="ExternalInput")
    w2_h = iod(ctx, io, "w2p", [2, 128, 22, 512], F32, kind="ExternalInput")
    lng_h = iod(ctx, io, "lng", [128, 8], F32, kind="ExternalInput")
    lnb_h = iod(ctx, io, "lnb", [128, 8], F32, kind="ExternalInput")
    b1_h = iod(ctx, io, "b1p", [128, 44], F32, kind="ExternalInput")
    b2_h = iod(ctx, io, "b2r", [1, D], F32, kind="ExternalInput")
    m0_h = iod(ctx, io, "m0", [128, 1], F32, kind="ExternalInput")
    if add_module:
        xm_h = iod(ctx, io, "xmod", [D, TC], F32, kind="ExternalInput")
    x2_h = iod(ctx, io, "x2", [D, TC], F32, kind="ExternalOutput")

    with (ctx.es if local else contextlib.nullcontext()):
        B = [(ctx.psum(f"ps{i}", [128, 512], F32), f"ps{i}") for i in range(8)]
        x1t = ctx.sbuf("x1t", [128, NCH, 513], F32)
        xln2 = ctx.sbuf("xln2", [128, NCH, 512], BF16)
        t1 = ctx.sbuf("t1", [128, 44, 512], BF16)
        t2 = ctx.sbuf("t2", [128, 22, 512], BF16)
        lng = ctx.sbuf("lngs", [128, 8], F32)
        lnb = ctx.sbuf("lnbs", [128, 8], F32)
        b1 = ctx.sbuf("b1s", [128, 44], F32)
        b2r = ctx.sbuf("b2rs", [1, D], F32)
        xmt = ctx.sbuf("xmt", [128, NCH, 512], F32)
        m0s = ctx.sbuf("m0s", [128, 1], F32)
        ones = get_ones(ctx)
        ln = LNHelper(ctx, tchunk=512)

        def ldm0(e):
            return e.dma_start(out=m0s[:, :], in_=m0_h[:, :])
        p.dma("Y", ldm0, writes=("m0s",), tag="ld_m0")

        for nm, sb, h in (("lng", lng, lng_h), ("lnb", lnb, lnb_h),
                          ("b1", b1, b1_h), ("b2r", b2r, b2_h)):
            def ld(e, sb=sb, h=h):
                return e.dma_start(out=sb[:, :], in_=h[:, :])
            p.dma("Y", ld, writes=("lnp" if nm in ("lng", "lnb") else nm,),
                  tag=f"ld_{nm}")

        ws1 = WStream(ctx, "w1", w1_h, D, 5632, tag="w1")
        ws2 = WStream(ctx, "w2", w2_h, 2816, D, tag="w2")

        for it in range(TC // 512):
            t0 = it * 512

            def flx(e, t0=t0):
                return e.dma_start(out=x1t[:, :, :],
                                   in_=x1_h[:, 511 + t0:511 + t0 + 513]
                                   .rearrange("(c p) t -> p c t", p=128))
            p.dma("Y", flx, writes=("x1t",), tag="ld_x1t")
            if it == 0:
                for c_ in range(NCH):
                    def fz(e, c_=c_):
                        return e.tensor_scalar_mul(x1t[:, c_, 0:1],
                                                   x1t[:, c_, 0:1], m0s[:, 0:1])
                    p.V(fz, reads=("x1t", "m0s"), writes=("x1t",), tag="zcol0")
            sv = ShiftView(x1t)
            ln.emit(sv, "x1t", xln2, "xln2", B[4], B[5], B[6], B[7],
                    gamma=lambda c: lng[:, c:c + 1],
                    beta=lambda c: lnb[:, c:c + 1])

            def evac_w1(oc, ocw, tt0, tlen, ps, ps_key):
                j = oc // 128
                func = AF.Gelu if j >= 22 else AF.Identity

                def fn(e, j=j, tt0=tt0, tlen=tlen, ps=ps, ocw=ocw, func=func):
                    return e.activation(t1[0:ocw, j, tt0:tt0 + tlen],
                                        ps[0:ocw, 0:tlen], func,
                                        bias=b1[0:ocw, j:j + 1])
                p.S(fn, reads=(ps_key, "b1"), writes=("t1",), tag="ev_w1")
            emit_proj(ctx, ws1, xln2, "xln2", 512, evac_w1, B[0:2])

            for j in range(22):
                def fg(e, j=j):
                    return e.tensor_tensor(t2[:, j, :], t1[:, j, :],
                                           t1[:, 22 + j, :], op=ALU.mult)
                p.V(fg, reads=("t1",), writes=("t2",), tag="geglu")

            if add_module:
                def flm(e, t0=t0):
                    return e.dma_start(out=xmt[:, :, :],
                                       in_=xm_h[:, t0:t0 + 512]
                                       .rearrange("(c p) t -> p c t", p=128))
                p.dma("Y", flm, writes=("xmt",), tag="ld_xm")

            def extra_b2(e, oc, ocw, tt0, tlen, ps):
                return e.matmul(ps[0:ocw, 0:tlen], b2r[0:1, oc:oc + ocw],
                                ones[0:1, 0:tlen], start=False, stop=True)
            extra_b2.reads = ("b2r", "ones")

            def evac_w2(oc, ocw, tt0, tlen, ps, ps_key):
                j = oc // 128

                def fn(e, j=j, tt0=tt0, tlen=tlen, ps=ps, ocw=ocw):
                    return e.tensor_tensor(x1t[0:ocw, j, 1 + tt0:1 + tt0 + tlen],
                                           ps[0:ocw, 0:tlen],
                                           x1t[0:ocw, j, 1 + tt0:1 + tt0 + tlen],
                                           op=ALU.add)
                p.V(fn, reads=(ps_key, "x1t"), writes=("x1t",), tag="w2_resid")
                if add_module:
                    def fm(e, j=j, tt0=tt0, tlen=tlen, ocw=ocw):
                        return e.tensor_tensor(
                            x1t[0:ocw, j, 1 + tt0:1 + tt0 + tlen],
                            x1t[0:ocw, j, 1 + tt0:1 + tt0 + tlen],
                            xmt[0:ocw, j, tt0:tt0 + tlen], op=ALU.add)
                    p.V(fm, reads=("xmt", "x1t"), writes=("x1t",), tag="w2_mod")

                def fs(e, j=j, t0=t0, tt0=tt0, tlen=tlen, ocw=ocw):
                    return e.dma_start(
                        out=x2_h[j * 128:j * 128 + ocw, t0 + tt0:t0 + tt0 + tlen],
                        in_=x1t[0:ocw, j, 1 + tt0:1 + tt0 + tlen])
                p.dma("Y", fs, reads=("x1t",), writes=(f"x2@o{t0}_{oc}",),
                      tag="st_x2")
            emit_proj(ctx, ws2, t2, "t2", 512, evac_w2, B[2:4],
                      extra_mm=extra_b2)

        p.barrier_all([f"x2@o{t0}_{oc}" for t0 in range(0, TC, 512)
                       for oc in range(0, D, 128)])
        if local:
            p.finalize(nc)
    return ctx




def build_cum1(nc, fctx=None, io=None):
    local = fctx is None
    ctx = make_ctx(nc) if local else fctx
    nc = ctx.nc
    ctx.wpool_cfg = (8, 512)
    p = ctx.p
    xT_h = iod(ctx, io, "xT", [D, TC], F32, kind="ExternalInput")
    wk_h = iod(ctx, io, "wkp", [2, 128, 8, 512], F32, kind="ExternalInput")
    wv_h = iod(ctx, io, "wvp", [2, 128, 8, 512], F32, kind="ExternalInput")
    kc_o = iod(ctx, io, "kc", [D, TC], F32, kind="ExternalOutput")
    kvc_o = iod(ctx, io, "kvc", [D, TC], F32, kind="ExternalOutput")
    tot_o = iod(ctx, io, "tot", [D, 2], F32, kind="ExternalOutput")

    with (ctx.es if local else contextlib.nullcontext()):
        B = [(ctx.psum(f"ps{i}", [128, 512], F32), f"ps{i}") for i in range(8)]
        xtile = ctx.sbuf("x", [128, NCH, 512], F32)
        xln = ctx.sbuf("xln", [128, NCH, TC], BF16)
        kraw = ctx.sbuf("kraw", [128, 4, TC], BF16)
        kcb = ctx.sbuf("kcb", [128, 4, TC], F32)
        kvcb = ctx.sbuf("kvcb", [128, 4, TC], F32)
        kvt = ctx.sbuf("kvt", [128, 512], F32)
        tot = ctx.sbuf("tot", [128, NCH, 2], F32)
        ones = get_ones(ctx)
        ln = LNHelper(ctx, tchunk=512)

        for tch in range(TC // 512):
            t0 = tch * 512
            load_xT(ctx, xT_h, xtile, "x", t0=t0, tlen=512)
            ln.emit(xtile, "x", SubView(xln, t0, 512), "xln",
                    B[4], B[5], B[6], B[7])

        ws_k = WStream(ctx, "wk", wk_h, D, D, tag="wk")
        ws_v = WStream(ctx, "wv", wv_h, D, D, tag="wv")

        for blk in range(2):
            def evac_k(oc, ocw, tt0, tlen, ps, ps_key):
                j4 = (oc // 128) % 4
                oc8 = oc // 128

                def fe(e, j4=j4, tt0=tt0, tlen=tlen, ps=ps):
                    return e.activation(kraw[:, j4, tt0:tt0 + tlen],
                                        ps[:, 0:tlen], AF.Exp)
                p.S(fe, reads=(ps_key,), writes=("kraw",), tag="c_exp")

                init = 0.0 if tt0 == 0 else kcb[:, j4, tt0 - 1:tt0]

                def fs(e, j4=j4, tt0=tt0, tlen=tlen, init=init):
                    return e.tensor_tensor_scan(
                        kcb[:, j4, tt0:tt0 + tlen], ones[:, 0:tlen],
                        kraw[:, j4, tt0:tt0 + tlen], init,
                        op0=ALU.mult, op1=ALU.add)
                p.V(fs, reads=("kraw", "ones", "kcb"), writes=("kcb",),
                    tag="c_scan")
                def fst(e, j4=j4, tt0=tt0, tlen=tlen, oc8=oc8):
                    return e.dma_start(
                        out=kc_o[oc8 * 128:(oc8 + 1) * 128, tt0:tt0 + tlen],
                        in_=kcb[:, j4, tt0:tt0 + tlen])
                p.dma("Y", fst, reads=("kcb",), writes=(f"kc@o{oc8}_{tt0}",),
                      tag="st_kc")
                if tt0 + tlen == TC:
                    def ftot(e, j4=j4, oc8=oc8):
                        return e.tensor_copy(tot[:, oc8, 0:1],
                                             kcb[:, j4, TC - 1:TC])
                    p.V(ftot, reads=("kcb",), writes=("tot",), tag="c_tot")
            emit_proj(ctx, ws_k, xln, "xln", TC, evac_k, B[0:2],
                      blk_range=(blk, blk + 1))

            def evac_v(oc, ocw, tt0, tlen, ps, ps_key):
                j4 = (oc // 128) % 4
                oc8 = oc // 128

                def fm(e, j4=j4, tt0=tt0, tlen=tlen, ps=ps):
                    return e.tensor_tensor(kvt[:, 0:tlen], ps[:, 0:tlen],
                                           kraw[:, j4, tt0:tt0 + tlen],
                                           op=ALU.mult)
                p.V(fm, reads=(ps_key, "kraw"), writes=("kvt",), tag="c_kv")

                init = 0.0 if tt0 == 0 else kvcb[:, j4, tt0 - 1:tt0]

                def fs(e, j4=j4, tt0=tt0, tlen=tlen, init=init):
                    return e.tensor_tensor_scan(
                        kvcb[:, j4, tt0:tt0 + tlen], ones[:, 0:tlen],
                        kvt[:, 0:tlen], init, op0=ALU.mult, op1=ALU.add)
                p.V(fs, reads=("kvt", "ones", "kvcb"), writes=("kvcb",),
                    tag="c_scan2")

                def fst(e, j4=j4, tt0=tt0, tlen=tlen, oc8=oc8):
                    return e.dma_start(
                        out=kvc_o[oc8 * 128:(oc8 + 1) * 128, tt0:tt0 + tlen],
                        in_=kvcb[:, j4, tt0:tt0 + tlen])
                p.dma("Y", fst, reads=("kvcb",), writes=(f"kvc@o{oc8}_{tt0}",),
                      tag="st_kvc")
                if tt0 + tlen == TC:
                    def ftot(e, j4=j4, oc8=oc8):
                        return e.tensor_copy(tot[:, oc8, 1:2],
                                             kvcb[:, j4, TC - 1:TC])
                    p.V(ftot, reads=("kvcb",), writes=("tot",), tag="c_tot2")
            emit_proj(ctx, ws_v, xln, "xln", TC, evac_v, B[2:4],
                      blk_range=(blk, blk + 1))

        def fstt(e):
            return e.dma_start(out=tot_o[:, :].rearrange("(c p) q -> p c q", p=128),
                               in_=tot[:, :, :])
        p.dma("Y", fstt, reads=("tot",), writes=("tot@o",), tag="st_tot")
        p.barrier_all(["tot@o"] + [f"kc@o{j}_{t}" for j in range(8)
                                   for t in range(0, TC, 512)]
                      + [f"kvc@o{j}_{t}" for j in range(8)
                         for t in range(0, TC, 512)])
        if local:
            p.finalize(nc)
    return ctx


def build_cum2(nc, fctx=None, io=None):
    local = fctx is None
    ctx = make_ctx(nc) if local else fctx
    nc = ctx.nc
    ctx.wpool_cfg = (8, 512)
    p = ctx.p
    xT_h = iod(ctx, io, "xT", [D, TC], F32, kind="ExternalInput")
    wq_h = iod(ctx, io, "wqp", [2, 128, 8, 512], F32, kind="ExternalInput")
    wc_h = iod(ctx, io, "wcp", [2, 128, 8, 512], F32, kind="ExternalInput")
    kc_h = iod(ctx, io, "kc", [D, TC], F32, kind="ExternalInput")
    kvc_h = iod(ctx, io, "kvc", [D, TC], F32, kind="ExternalInput")
    tots_h = iod(ctx, io, "tots", [4, D, 2], F32, kind="ExternalInput")
    cm_h = iod(ctx, io, "cmask", [128, 4], F32, kind="ExternalInput")
    yT_h = iod(ctx, io, "yT", [D, TC], F32, kind="ExternalOutput")

    with (ctx.es if local else contextlib.nullcontext()):
        B = [(ctx.psum(f"ps{i}", [128, 512], F32), f"ps{i}") for i in range(8)]
        xtile = ctx.sbuf("x", [128, NCH, 512], F32)
        xln = ctx.sbuf("xln", [128, NCH, TC], BF16)
        Ob = ctx.sbuf("Ob", [128, NCH, TC], BF16)
        tots = ctx.sbuf("tots", [128, NCH, 8], F32)
        offs = ctx.sbuf("offs", [128, NCH, 2], F32)
        cm = ctx.sbuf("cm", [128, 4], F32)
        sct = ctx.sbuf("sct", [128, 512], F32)
        sct2 = ctx.sbuf("sct2", [128, 512], F32)
        xres = ctx.sbuf("xres", [128, NCH, 512], F32)
        ones = get_ones(ctx)
        ln = LNHelper(ctx, tchunk=512)

        for r in range(4):
            def ldt(e, r=r):
                return e.dma_start(out=tots[:, :, r * 2:(r + 1) * 2],
                                   in_=tots_h[r, :, :].rearrange("(c p) q -> p c q", p=128))
            p.dma("Y", ldt, writes=("tots",), tag="ld_tots")

        def ldc(e):
            return e.dma_start(out=cm[:, :], in_=cm_h[:, :])
        p.dma("Y", ldc, writes=("cm",), tag="ld_cm")

        def fz(e):
            return e.memset(offs[:, :, :], 0.0)
        p.G(fz, writes=("offs",), tag="off_z")
        for q in range(2):
            for cp in range(4):
                for j in range(NCH):
                    def fa(e, q=q, cp=cp, j=j):
                        return e.scalar_tensor_tensor(
                            offs[:, j, q:q + 1], tots[:, j, cp * 2 + q:cp * 2 + q + 1],
                            cm[:, cp:cp + 1], offs[:, j, q:q + 1],
                            op0=ALU.mult, op1=ALU.add)
                    p.V(fa, reads=("tots", "cm", "offs"), writes=("offs",),
                        tag="off_acc")

        for tch in range(TC // 512):
            t0 = tch * 512
            load_xT(ctx, xT_h, xtile, "x", t0=t0, tlen=512)
            ln.emit(xtile, "x", SubView(xln, t0, 512), "xln",
                    B[4], B[5], B[6], B[7])

        ws_q = WStream(ctx, "wq", wq_h, D, D, tag="wq")
        kct = ctx.sbuf("kct", [128, 512], F32)

        def evac_q(oc, ocw, tt0, tlen, ps, ps_key):
            j = oc // 128

            def flk(e, j=j, tt0=tt0, tlen=tlen):
                return e.dma_start(out=kct[:, 0:tlen],
                                   in_=kc_h[j * 128:(j + 1) * 128, tt0:tt0 + tlen])
            p.dma("G", flk, writes=("kct",), tag="ld_kc", role="kcl")

            def f1(e, j=j, tlen=tlen):
                return e.tensor_scalar_add(kct[:, 0:tlen], kct[:, 0:tlen],
                                           offs[:, j, 0:1])
            p.V(f1, reads=("kct", "offs"), writes=("kct",), tag="q_offk")

            def f2(e, tlen=tlen):
                return e.tensor_scalar_add(kct[:, 0:tlen], kct[:, 0:tlen], 1e-6)
            p.V(f2, reads=("kct",), writes=("kct",), tag="q_eps")

            def f3(e, tlen=tlen):
                return e.reciprocal(kct[:, 0:tlen], kct[:, 0:tlen])
            p.V(f3, reads=("kct",), writes=("kct",), tag="q_rec")

            def flv(e, j=j, tt0=tt0, tlen=tlen):
                return e.dma_start(out=sct[:, 0:tlen],
                                   in_=kvc_h[j * 128:(j + 1) * 128, tt0:tt0 + tlen])
            p.dma("G", flv, writes=("sct",), tag="ld_kvc", role="kcl")

            def f4(e, j=j, tlen=tlen):
                return e.tensor_scalar_add(sct[:, 0:tlen], sct[:, 0:tlen],
                                           offs[:, j, 1:2])
            p.V(f4, reads=("sct", "offs"), writes=("sct",), tag="q_offv")

            def f5(e, tlen=tlen):
                return e.tensor_tensor(sct[:, 0:tlen], sct[:, 0:tlen],
                                       kct[:, 0:tlen], op=ALU.mult)
            p.V(f5, reads=("sct", "kct"), writes=("sct",), tag="q_s")

            def f6(e, tlen=tlen, ps=ps):
                return e.activation(sct2[:, 0:tlen], ps[:, 0:tlen], AF.Sigmoid)
            p.S(f6, reads=(ps_key,), writes=("sct2",), tag="q_sig")

            def f7(e, j=j, tt0=tt0, tlen=tlen):
                return e.tensor_tensor(Ob[:, j, tt0:tt0 + tlen], sct[:, 0:tlen],
                                       sct2[:, 0:tlen], op=ALU.mult)
            p.V(f7, reads=("sct", "sct2"), writes=("Ob",), tag="q_o")
        emit_proj(ctx, ws_q, xln, "xln", TC, evac_q, B[0:2])

        ws_c = WStream(ctx, "wc", wc_h, D, D, tag="wc")
        tload = {}

        def evac_o(oc, ocw, t0, tlen, ps, ps_key):
            j = oc // 128
            if t0 not in tload:
                tload[t0] = True
                load_xT(ctx, xT_h, xres, "xres", t0=t0, tlen=512)

            def fn(e, j=j, tlen=tlen, ps=ps, ocw=ocw):
                return e.tensor_tensor(xres[0:ocw, j, 0:tlen], ps[0:ocw, 0:tlen],
                                       xres[0:ocw, j, 0:tlen], op=ALU.add)
            p.V(fn, reads=(ps_key, "xres"), writes=("xres",), tag="resid")
            if j == NCH - 1:
                store_xT(ctx, yT_h, xres, "xres", f"yT@o{t0}", t0=t0, tlen=512)
        emit_proj(ctx, ws_c, Ob, "Ob", TC, evac_o, B[2:4], t_outer=True)

        p.barrier_all([f"yT@o{t0}" for t0 in range(0, TC, 512)])
        if local:
            p.finalize(nc)
    return ctx


def cum_host(inputs, xT_full):
    g = lambda k: np.ascontiguousarray(np.asarray(inputs[k], np.float32))
    wk = pack_weight(g("L1_attn_a_Wk"))
    wv = pack_weight(g("L1_attn_a_Wv"))
    wq = pack_weight(g("L1_attn_a_Wq"))
    wc = pack_weight(g("L1_attn_a_Wc"))

    nc1 = bass.Bass("TRN2", target_bir_lowering=False)
    c1 = build_cum1(nc1)
    maps = []
    for c in range(NCORES):
        m = dict(xT=_shard_cols(xT_full, c), wkp=wk, wvp=wv)
        m.update(c1.consts)
        if hasattr(c1, "cf_map"):
            m["cf_consts"] = c1.cf_array()
        maps.append(m)
    r1 = run_bass_kernel_spmd(nc1, maps, core_ids=list(range(NCORES)))

    tots_by_batch = [np.stack([r1.results[b * 4 + i]["tot"] for i in range(4)])
                     for b in range(2)]

    nc2 = bass.Bass("TRN2", target_bir_lowering=False)
    c2 = build_cum2(nc2)
    maps2 = []
    for c in range(NCORES):
        cmask = np.zeros((128, 4), np.float32)
        for cp in range(4):
            if cp < c % 4:
                cmask[:, cp] = 1.0
        m = dict(xT=_shard_cols(xT_full, c), wqp=wq, wcp=wc,
                 kc=r1.results[c]["kc"], kvc=r1.results[c]["kvc"],
                 tots=tots_by_batch[c // 4], cmask=cmask)
        m.update(c2.consts)
        m["cf_consts"] = c2.cf_array()
        maps2.append(m)
    r2 = run_bass_kernel_spmd(nc2, maps2, core_ids=list(range(NCORES)))
    return np.concatenate([r2.results[c]["yT"] for c in range(NCORES)], axis=1)




def build_ssm1(nc, fctx=None, io=None):
    local = fctx is None
    ctx = make_ctx(nc) if local else fctx
    nc = ctx.nc
    ctx.wpool_cfg = (8, 512)
    p = ctx.p
    xT_h = iod(ctx, io, "xT", [D, TC], F32, kind="ExternalInput")
    wi_h = iod(ctx, io, "wip", [8, 128, 8, 512], F32, kind="ExternalInput")
    xp_o = iod(ctx, io, "xpart", [2048, TC], F32, kind="ExternalOutput")
    z_o = iod(ctx, io, "z", [2048, TC], F32, kind="ExternalOutput")

    with (ctx.es if local else contextlib.nullcontext()):
        B = [(ctx.psum(f"ps{i}", [128, 512], F32), f"ps{i}") for i in range(8)]
        xtile = ctx.sbuf("x", [128, NCH, 512], F32)
        xln = ctx.sbuf("xln", [128, NCH, TC], BF16)
        stg = [ctx.sbuf(f"stg{i}", [128, 512], F32) for i in range(2)]
        ln = LNHelper(ctx, tchunk=512)
        get_ones(ctx)

        for tch in range(TC // 512):
            t0 = tch * 512
            load_xT(ctx, xT_h, xtile, "x", t0=t0, tlen=512)
            ln.emit(xtile, "x", SubView(xln, t0, 512), "xln",
                    B[4], B[5], B[6], B[7])

        ws = WStream(ctx, "wi", wi_h, D, 4096, tag="wi")
        cnt = [0]

        def evac(oc, ocw, tt0, tlen, ps, ps_key):
            si = cnt[0] % 2
            cnt[0] += 1
            dst = xp_o if oc < 2048 else z_o
            ro = oc if oc < 2048 else oc - 2048

            def fc(e, si=si, tlen=tlen, ps=ps):
                return e.copy(stg[si][:, 0:tlen], ps[:, 0:tlen])
            p.S(fc, reads=(ps_key,), writes=(f"stg{si}",), tag="s1_cp")

            def fs(e, si=si, dst=dst, ro=ro, tt0=tt0, tlen=tlen):
                return e.dma_start(out=dst[ro:ro + 128, tt0:tt0 + tlen],
                                   in_=stg[si][:, 0:tlen])
            p.dma("Y", fs, reads=(f"stg{si}",), writes=(f"s1@o{ro}_{tt0}_{dst.name}",),
                  tag="s1_st")
        emit_proj(ctx, ws, xln, "xln", TC, evac, B[0:2])
        p.barrier_all([f"s1@o{ro}_{tt0}_{nm}" for ro in range(0, 2048, 128)
                       for tt0 in range(0, TC, 512) for nm in ("xpart", "z")])
        if local:
            p.finalize(nc)
    return ctx


def build_ssm2(nc, fctx=None, io=None, xin_pad=3):
    local = fctx is None
    ctx = make_ctx(nc) if local else fctx
    nc = ctx.nc
    ctx.wpool_cfg = (16, 512)
    p = ctx.p
    xin_h = iod(ctx, io, "xin", [2048, TC + xin_pad], F32, kind="ExternalInput")
    wxp_h = iod(ctx, io, "wxp", [1, 128, 16, 96], F32, kind="ExternalInput")
    wdt_h = iod(ctx, io, "wdt", [4, 128, 1, 512], F32, kind="ExternalInput")
    cw_h = iod(ctx, io, "cw", [128, 16, 4], F32, kind="ExternalInput")
    cb_h = iod(ctx, io, "cb", [128, 16], F32, kind="ExternalInput")
    dtb_h = iod(ctx, io, "dtb", [128, 16], F32, kind="ExternalInput")
    xc_o = iod(ctx, io, "xc", [2048, TC], BF16, kind="ExternalOutput")
    dtt_o = iod(ctx, io, "dtt", [2048, TC], BF16, kind="ExternalOutput")
    bc_o = iod(ctx, io, "bc", [32, TC], F32, kind="ExternalOutput")

    with (ctx.es if local else contextlib.nullcontext()):
        B = [(ctx.psum(f"ps{i}", [128, 512], F32), f"ps{i}") for i in range(8)]
        xint = ctx.sbuf("xint", [128, 16, 515], F32)
        xc = ctx.sbuf("xc", [128, 16, TC], BF16)
        xdb = ctx.sbuf("xdb", [96, TC], F32)
        xdbb = ctx.sbuf("xdbb", [64, TC], BF16)
        cw = ctx.sbuf("cws", [128, 16, 4], F32)
        cb = ctx.sbuf("cbs", [128, 16], F32)
        dtb = ctx.sbuf("dtbs", [128, 16], F32)
        ta = ctx.sbuf("ta", [128, 512], F32)
        tb = ctx.sbuf("tb", [128, 512], F32)
        stg = [ctx.sbuf(f"stg{i}", [128, 512], BF16) for i in range(2)]
        get_ones(ctx)

        for nm, sb, h in (("cw", cw, cw_h), ("cb", cb, cb_h), ("dtb", dtb, dtb_h)):
            def ld(e, sb=sb, h=h, nm=nm):
                if nm == "cw":
                    return e.dma_start(out=sb[:, :, :], in_=h[:, :, :])
                return e.dma_start(out=sb[:, :], in_=h[:, :])
            p.dma("Y", ld, writes=(nm,), tag=f"ld_{nm}")

        for tch in range(TC // 512):
            t0 = tch * 512

            def fl(e, t0=t0):
                return e.dma_start(out=xint[:, :, :],
                                   in_=xin_h[:, t0 + xin_pad - 3:t0 + xin_pad + 512]
                                   .rearrange("(c p) t -> p c t", p=128))
            p.dma("Y", fl, writes=("xint",), tag="ld_xin")
            for j in range(16):
                def f0(e, j=j):
                    return e.tensor_scalar_mul(ta[:, :], xint[:, j, 0:512],
                                               cw[:, j, 0:1])
                p.V(f0, reads=("xint", "cw"), writes=("ta",), tag="cv0")
                for k in range(1, 4):
                    def fk(e, j=j, k=k):
                        return e.scalar_tensor_tensor(ta[:, :], xint[:, j, k:k + 512],
                                                      cw[:, j, k:k + 1], ta[:, :],
                                                      op0=ALU.mult, op1=ALU.add)
                    p.V(fk, reads=("xint", "cw", "ta"), writes=("ta",), tag="cvk")

                def fsl(e, j=j, t0=t0):
                    return e.activation(xc[:, j, t0:t0 + 512], ta[:, :], AF.Silu,
                                        bias=cb[:, j:j + 1])
                p.S(fsl, reads=("ta", "cb"), writes=("xc",), tag="cv_silu")

                def fst(e, j=j, t0=t0):
                    return e.dma_start(out=xc_o[j * 128:(j + 1) * 128, t0:t0 + 512],
                                       in_=xc[:, j, t0:t0 + 512])
                p.dma("Y", fst, reads=("xc",), writes=(f"xc@o{j}_{t0}",), tag="st_xc")

        ws_xp = WStream(ctx, "wxp", wxp_h, 2048, 96, blk=96, tag="wxp")

        def evac_xdb(oc, ocw, tt0, tlen, ps, ps_key):
            def fc(e, tt0=tt0, tlen=tlen, ps=ps, ocw=ocw):
                return e.copy(xdb[0:ocw, tt0:tt0 + tlen], ps[0:ocw, 0:tlen])
            p.S(fc, reads=(ps_key,), writes=("xdb",), tag="xdb_cp")

            def fcb(e, tt0=tt0, tlen=tlen):
                return e.tensor_copy(xdbb[0:64, tt0:tt0 + tlen],
                                     xdb[0:64, tt0:tt0 + tlen])
            p.V(fcb, reads=("xdb",), writes=("xdbb",), tag="xdbb_cp")
        emit_proj(ctx, ws_xp, xc, "xc", TC, evac_xdb, B[0:2])

        def fbc_st(e):
            return e.dma_start(out=bc_o[:, :], in_=xdb[64:96, :])
        p.dma("Y", fbc_st, reads=("xdb",), writes=("bc@o",), tag="st_bc")

        ws_dt = WStream(ctx, "wdt", wdt_h, 64, 2048, tag="wdt")

        def rhs_dt(c):
            return xdbb[0:64, :]

        cnt = [0]

        def evac_dt(oc, ocw, tt0, tlen, ps, ps_key):
            j = oc // 128
            si = cnt[0] % 2
            cnt[0] += 1

            def f1(e, j=j, tlen=tlen, ps=ps):
                return e.activation(ta[:, 0:tlen], ps[:, 0:tlen], AF.Abs,
                                    bias=dtb[:, j:j + 1])
            p.S(f1, reads=(ps_key, "dtb"), writes=("ta",), tag="dt_abs")

            def f2(e, tlen=tlen):
                return e.activation(ta[:, 0:tlen], ta[:, 0:tlen], AF.Exp,
                                    scale=-1.0)
            p.S(f2, reads=("ta",), writes=("ta",), tag="dt_exp")

            def f3(e, tlen=tlen):
                return e.activation(ta[:, 0:tlen], ta[:, 0:tlen], AF.Ln, bias=1.0)
            p.S(f3, reads=("ta",), writes=("ta",), tag="dt_ln")

            def f4(e, j=j, tlen=tlen, ps=ps):
                return e.activation(tb[:, 0:tlen], ps[:, 0:tlen], AF.Relu,
                                    bias=dtb[:, j:j + 1])
            p.S(f4, reads=(ps_key, "dtb"), writes=("tb",), tag="dt_relu")

            def f5(e, si=si, tlen=tlen):
                return e.tensor_tensor(stg[si][:, 0:tlen], ta[:, 0:tlen],
                                       tb[:, 0:tlen], op=ALU.add)
            p.V(f5, reads=("ta", "tb"), writes=(f"stg{si}",), tag="dt_add")

            def fs(e, si=si, j=j, tt0=tt0, tlen=tlen):
                return e.dma_start(out=dtt_o[j * 128:(j + 1) * 128, tt0:tt0 + tlen],
                                   in_=stg[si][:, 0:tlen])
            p.dma("Y", fs, reads=(f"stg{si}",), writes=(f"dt@o{j}_{tt0}",),
                  tag="st_dt")
        emit_proj(ctx, ws_dt, rhs_dt, "xdbb", TC, evac_dt, B[2:4])

        p.barrier_all(["bc@o"] + [f"xc@o{j}_{t}" for j in range(16)
                                  for t in range(0, TC, 512)]
                      + [f"dt@o{j}_{t}" for j in range(16)
                         for t in range(0, TC, 512)])
        if local:
            p.finalize(nc)
    return ctx


def build_ssm3(nc, TFULL=8192, fctx=None, io=None, dyn=False):
    local = fctx is None
    ctx = make_ctx(nc) if local else fctx
    nc = ctx.nc
    p = ctx.p
    NDC = 4
    if dyn:
        dts_h = io["dts"]
        xcs_h = io["xcs"]
        ys_o = iod(ctx, io, "ys", [512, TFULL], BF16, kind="Internal")
    else:
        dts_h = iod(ctx, io, "dts", [4, 512, TC], BF16, kind="ExternalInput")
        xcs_h = iod(ctx, io, "xcs", [4, 512, TC], BF16, kind="ExternalInput")
        ys_o = iod(ctx, io, "ys", [4, 512, TC], BF16, kind="ExternalOutput")
    bf_h = iod(ctx, io, "bcf", [4, 32, TC], F32, kind="ExternalInput")
    al_h = iod(ctx, io, "alog", [128, 4, 16], F32, kind="ExternalInput")
    dd_h = iod(ctx, io, "dd", [128, 4], F32, kind="ExternalInput")

    T2 = 2048
    ntc = TFULL // T2
    with (ctx.es if local else contextlib.nullcontext()):
        B = [(ctx.psum(f"ps{i}", [128, 512], F32), f"ps{i}") for i in range(8)]
        dts = ctx.sbuf("dts", [128, NDC, TFULL], BF16)
        xcs = ctx.sbuf("xcs", [128, NDC, TFULL], BF16)
        brow = [ctx.sbuf(f"brow{i}", [1, 2048], F32) for i in range(2)]
        an = ctx.sbuf("an", [128, NDC, 16], F32)
        dd = ctx.sbuf("dd", [128, NDC], F32)
        carr = ctx.sbuf("carr", [128, NDC, 16], F32)
        dat = ctx.sbuf("dat", [128, T2], BF16)
        ut = ctx.sbuf("ut", [128, T2], BF16)
        dbu = ctx.sbuf("dbu", [128, T2], BF16)
        ht = ctx.sbuf("ht", [128, T2], F32)
        yt = ctx.sbuf("yt", [128, T2], F32)
        tm = ctx.sbuf("tm", [128, T2], F32)
        ysb = ctx.sbuf("ysb", [128, T2], BF16)
        ones = get_ones(ctx)

        if dyn:
            for r in range(4):
                for dc in range(4):
                    def ldt(e, r=r, dc=dc):
                        return e.dma_start(
                            out=dts[:, dc, r * TC:(r + 1) * TC],
                            in_=dts_h[r, bass.ds(ctx.snaps[f"d{dc}"], 128), :])
                    p.add("G", ldt, reads=("ridx_regs",), writes=("dts",),
                          inc=16, tag="ld_dts", role="Gd")

                    def ldx(e, r=r, dc=dc):
                        return e.dma_start(
                            out=xcs[:, dc, r * TC:(r + 1) * TC],
                            in_=xcs_h[r, bass.ds(ctx.snaps[f"d{dc}"], 128), :])
                    p.add("G", ldx, reads=("ridx_regs",), writes=("xcs",),
                          inc=16, tag="ld_xcs", role="Gd")
        else:
            for r in range(4):
                def ldt(e, r=r):
                    return e.dma_start(out=dts[:, :, r * TC:(r + 1) * TC],
                                       in_=dts_h[r, :, :].rearrange("(c p) t -> p c t", p=128))
                p.dma("Y", ldt, writes=("dts",), tag="ld_dts")

                def ldx(e, r=r):
                    return e.dma_start(out=xcs[:, :, r * TC:(r + 1) * TC],
                                       in_=xcs_h[r, :, :].rearrange("(c p) t -> p c t", p=128))
                p.dma("Y", ldx, writes=("xcs",), tag="ld_xcs")

        def lda(e):
            return e.dma_start(out=an[:, :, :], in_=al_h[:, :, :])
        p.dma("Y", lda, writes=("an",), tag="ld_al")

        def ldd(e):
            return e.dma_start(out=dd[:, :], in_=dd_h[:, :])
        p.dma("Y", ldd, writes=("dd",), tag="ld_dd")

        def fea(e):
            return e.activation(an[:, :, :], an[:, :, :], AF.Exp)
        p.S(fea, reads=("an",), writes=("an",), tag="a_exp")

        def fng(e):
            return e.tensor_scalar_mul(an[:, :, :], an[:, :, :], -1.0)
        p.V(fng, reads=("an",), writes=("an",), tag="a_neg")

        def fcz(e):
            return e.memset(carr[:, :, :], 0.0)
        p.G(fcz, writes=("carr",), tag="carr_z")

        for tci in range(ntc):
            t0 = tci * T2
            for dch in range(NDC):
                def fu(e, dch=dch, t0=t0):
                    return e.tensor_tensor(ut[:, :], dts[:, dch, t0:t0 + T2],
                                           xcs[:, dch, t0:t0 + T2], op=ALU.mult)
                p.V(fu, reads=("dts", "xcs"), writes=("ut",), tag="s_u")
                for n in range(16):
                    if True:
                        rk = t0 // TC
                        tt = t0 % TC

                        def flb(e, n=n, rk=rk, tt=tt):
                            return e.dma_start(out=brow[0][0:1, :],
                                               in_=bf_h[rk, n:n + 1, tt:tt + T2])
                        p.dma("Y", flb, writes=("brow0",), tag="ld_b", role="br")

                        def flc(e, n=n, rk=rk, tt=tt):
                            return e.dma_start(out=brow[1][0:1, :],
                                               in_=bf_h[rk, 16 + n:17 + n, tt:tt + T2])
                        p.dma("Y", flc, writes=("brow1",), tag="ld_c", role="br")

                    def fda(e, dch=dch, n=n, t0=t0):
                        return e.activation(dat[:, :], dts[:, dch, t0:t0 + T2],
                                            AF.Exp, scale=an[:, dch, n:n + 1])
                    p.S(fda, reads=("dts", "an"), writes=("dat",), tag="s_da")
                    for sc in range(T2 // 512):
                        pb_, pb_key = B[sc % 2]

                        def fb(e, sc=sc, pb_=pb_):
                            return e.matmul(pb_[:, 0:512], ones[0:1, 0:128],
                                            brow[0][0:1, sc * 512:(sc + 1) * 512],
                                            start=True, stop=True)
                        p.T(fb, reads=("brow0", "ones"), writes=(pb_key,), tag="s_bb")

                        def fdb(e, sc=sc, pb_=pb_):
                            return e.tensor_tensor(dbu[:, sc * 512:(sc + 1) * 512],
                                                   ut[:, sc * 512:(sc + 1) * 512],
                                                   pb_[:, 0:512], op=ALU.mult)
                        p.V(fdb, reads=("ut", pb_key), writes=("dbu",), tag="s_db")
                    def fsc(e, dch=dch, n=n):
                        return e.tensor_tensor_scan(ht[:, :], dat[:, :], dbu[:, :],
                                                    carr[:, dch, n:n + 1],
                                                    op0=ALU.mult, op1=ALU.add)
                    p.V(fsc, reads=("dat", "dbu", "carr"), writes=("ht",), tag="s_scan")

                    def fcu(e, dch=dch, n=n):
                        return e.tensor_copy(carr[:, dch, n:n + 1], ht[:, T2 - 1:T2])
                    p.V(fcu, reads=("ht",), writes=("carr",), tag="s_carr")
                    for sc in range(T2 // 512):
                        pc_, pc_key = B[2 + sc % 2]

                        def fb2(e, sc=sc, pc_=pc_):
                            return e.matmul(pc_[:, 0:512], ones[0:1, 0:128],
                                            brow[1][0:1, sc * 512:(sc + 1) * 512],
                                            start=True, stop=True)
                        p.T(fb2, reads=("brow1", "ones"), writes=(pc_key,), tag="s_cb")

                        def fm1(e, sc=sc, pc_=pc_):
                            return e.tensor_tensor(tm[:, sc * 512:(sc + 1) * 512],
                                                   ht[:, sc * 512:(sc + 1) * 512],
                                                   pc_[:, 0:512], op=ALU.mult)
                        p.V(fm1, reads=("ht", pc_key), writes=("tm",), tag="s_yc")
                    if n == 0:
                        def fy0(e):
                            return e.tensor_copy(yt[:, :], tm[:, :])
                        p.V(fy0, reads=("tm",), writes=("yt",), tag="s_y0")
                    else:
                        def fya(e):
                            return e.tensor_tensor(yt[:, :], yt[:, :], tm[:, :],
                                                   op=ALU.add)
                        p.V(fya, reads=("yt", "tm"), writes=("yt",), tag="s_ya")
                def fdt(e, dch=dch, t0=t0):
                    return e.scalar_tensor_tensor(ysb[:, :], xcs[:, dch, t0:t0 + T2],
                                                  dd[:, dch:dch + 1], yt[:, :],
                                                  op0=ALU.mult, op1=ALU.add)
                p.V(fdt, reads=("xcs", "dd", "yt"), writes=("ysb",), tag="s_d")

                def fst(e, dch=dch, t0=t0):
                    if dyn:
                        return e.dma_start(out=ys_o[dch * 128:(dch + 1) * 128,
                                                    t0:t0 + T2], in_=ysb[:, :])
                    rk, tt = t0 // TC, t0 % TC
                    return e.dma_start(out=ys_o[rk, dch * 128:(dch + 1) * 128,
                                                tt:tt + T2],
                                       in_=ysb[:, :])
                p.dma("Y", fst, reads=("ysb",), writes=(f"ys@o{dch}_{t0}",),
                      tag="st_ys")
        p.barrier_all([f"ys@o{d}_{t}" for d in range(NDC)
                       for t in range(0, TFULL, T2)])
        if local:
            p.finalize(nc)
    return ctx


def build_ssm4(nc, fctx=None, io=None, dyn=False):
    local = fctx is None
    ctx = make_ctx(nc) if local else fctx
    nc = ctx.nc
    ctx.wpool_cfg = (16, 512)
    p = ctx.p
    xT_h = iod(ctx, io, "xT", [D, TC], F32, kind="ExternalInput")
    if dyn:
        y_h = io["y"]
    else:
        y_h = iod(ctx, io, "y", [4, 512, TC], BF16, kind="ExternalInput")
    z_h = iod(ctx, io, "z", [2048, TC], F32, kind="ExternalInput")
    wo_h = iod(ctx, io, "wop", [2, 128, 16, 512], F32, kind="ExternalInput")
    yT_o = iod(ctx, io, "yT", [D, TC], F32, kind="ExternalOutput")

    with (ctx.es if local else contextlib.nullcontext()):
        B = [(ctx.psum(f"ps{i}", [128, 512], F32), f"ps{i}") for i in range(8)]
        rhs = ctx.sbuf("rhs", [128, 16, TC], BF16)
        zt = ctx.sbuf("zt", [128, 512], F32)
        zs = ctx.sbuf("zs", [128, 512], F32)
        xres = ctx.sbuf("xres", [128, NCH, 512], F32)
        get_ones(ctx)

        if dyn:
            for r in range(4):
                for dc in range(4):
                    def ldy(e, r=r, dc=dc):
                        return e.dma_start(
                            out=rhs[:, 4 * r + dc, :],
                            in_=y_h[r, dc * 128:(dc + 1) * 128,
                                    bass.ds(ctx.snaps["t0"], TC)])
                    p.add("G", ldy, reads=("ridx_regs",), writes=("rhs",),
                          inc=16, tag="ld_y", role="Gd")
        else:
            for r in range(4):
                def ldy(e, r=r):
                    return e.dma_start(out=rhs[:, 4 * r:4 * (r + 1), :],
                                       in_=y_h[r, :, :].rearrange("(c p) t -> p c t", p=128))
                p.dma("Y", ldy, writes=("rhs",), tag="ld_y")

        for j in range(16):
            for tch in range(TC // 512):
                t0 = tch * 512

                def flz(e, j=j, t0=t0):
                    return e.dma_start(out=zt[:, :],
                                       in_=z_h[j * 128:(j + 1) * 128, t0:t0 + 512])
                p.dma("G", flz, writes=("zt",), tag="ld_z", role="zl")

                def fsl(e):
                    return e.activation(zs[:, :], zt[:, :], AF.Silu)
                p.S(fsl, reads=("zt",), writes=("zs",), tag="z_silu")

                def fm(e, j=j, t0=t0):
                    return e.tensor_tensor(rhs[:, j, t0:t0 + 512],
                                           rhs[:, j, t0:t0 + 512], zs[:, :],
                                           op=ALU.mult)
                p.V(fm, reads=("rhs", "zs"), writes=("rhs",), tag="yz")

        ws = WStream(ctx, "wo", wo_h, 2048, D, tag="wo")
        tload = {}

        def evac(oc, ocw, t0, tlen, ps, ps_key):
            j = oc // 128
            if t0 not in tload:
                tload[t0] = True
                load_xT(ctx, xT_h, xres, "xres", t0=t0, tlen=512)

            def fn(e, j=j, tlen=tlen, ps=ps, ocw=ocw):
                return e.tensor_tensor(xres[0:ocw, j, 0:tlen], ps[0:ocw, 0:tlen],
                                       xres[0:ocw, j, 0:tlen], op=ALU.add)
            p.V(fn, reads=(ps_key, "xres"), writes=("xres",), tag="resid")
            if j == NCH - 1:
                store_xT(ctx, yT_o, xres, "xres", f"yT@o{t0}", t0=t0, tlen=512)
        emit_proj(ctx, ws, rhs, "rhs", TC, evac, B[0:2], t_outer=True)
        p.barrier_all([f"yT@o{t0}" for t0 in range(0, TC, 512)])
        if local:
            p.finalize(nc)
    return ctx


def ssm_host(inputs, xT_full):
    g = lambda k: np.ascontiguousarray(np.asarray(inputs[k], np.float32))
    wi = pack_weight(np.ascontiguousarray(g("L1_attn_b_in_proj_w").T))
    wxp = pack_weight(np.ascontiguousarray(g("L1_attn_b_x_proj_w").T), blk=96)
    wdtT = np.ascontiguousarray(g("L1_attn_b_dt_proj_w").T)
    wdt = pack_weight(wdtT)
    cw = np.ascontiguousarray(g("L1_attn_b_conv_w")[:, 0, :]
                              .reshape(16, 128, 4).transpose(1, 0, 2))
    cb = np.ascontiguousarray(g("L1_attn_b_conv_b").reshape(16, 128).T)
    dtb = np.ascontiguousarray(g("L1_attn_b_dt_proj_b").reshape(16, 128).T)
    wo = pack_weight(np.ascontiguousarray(g("L1_attn_b_out_proj_w").T))
    alog = g("L1_attn_b_A_log")
    ddv = g("L1_attn_b_D")

    nc1 = bass.Bass("TRN2", target_bir_lowering=False)
    c1 = build_ssm1(nc1)
    maps = []
    for c in range(NCORES):
        m = dict(xT=_shard_cols(xT_full, c), wip=wi)
        m.update(c1.consts)
        if hasattr(c1, "cf_map"):
            m["cf_consts"] = c1.cf_array()
        maps.append(m)
    r1 = run_bass_kernel_spmd(nc1, maps, core_ids=list(range(NCORES)))
    xpart = [r1.results[c]["xpart"] for c in range(NCORES)]
    zs = [r1.results[c]["z"] for c in range(NCORES)]

    nc2 = bass.Bass("TRN2", target_bir_lowering=False)
    c2 = build_ssm2(nc2)
    maps2 = []
    for c in range(NCORES):
        xin = np.zeros((2048, TC + 3), np.float32)
        xin[:, 3:] = xpart[c]
        if c % 4 != 0:
            xin[:, 0:3] = xpart[c - 1][:, -3:]
        m = dict(xin=xin, wxp=wxp, wdt=wdt, cw=cw, cb=cb, dtb=dtb)
        m.update(c2.consts)
        if hasattr(c2, "cf_map"):
            m["cf_consts"] = c2.cf_array()
        maps2.append(m)
    r2 = run_bass_kernel_spmd(nc2, maps2, core_ids=list(range(NCORES)))

    TF = 4 * TC
    maps3 = []
    nc3 = bass.Bass("TRN2", target_bir_lowering=False)
    c3 = build_ssm3(nc3, TFULL=TF)
    xc_full = [[r2.results[b * 4 + i]["xc"] for i in range(4)] for b in range(2)]
    dt_full = [[r2.results[b * 4 + i]["dtt"] for i in range(4)] for b in range(2)]
    bc_full = [np.stack([r2.results[b * 4 + i]["bc"] for i in range(4)])
               for b in range(2)]
    for c in range(NCORES):
        b = c // 4
        d0 = (c % 4) * 512
        m = dict(dts=np.stack([dt_full[b][r][d0:d0 + 512] for r in range(4)]),
                 xcs=np.stack([xc_full[b][r][d0:d0 + 512] for r in range(4)]),
                 bcf=bc_full[b],
                 alog=np.ascontiguousarray(
                     alog[d0:d0 + 512].reshape(4, 128, 16).transpose(1, 0, 2)),
                 dd=np.ascontiguousarray(ddv[d0:d0 + 512].reshape(4, 128).T))
        m.update(c3.consts)
        if hasattr(c3, "cf_map"):
            m["cf_consts"] = c3.cf_array()
        maps3.append(m)
    r3 = run_bass_kernel_spmd(nc3, maps3, core_ids=list(range(NCORES)))

    y_full = [[r3.results[b * 4 + i]["ys"] for i in range(4)] for b in range(2)]
    maps4 = []
    nc4 = bass.Bass("TRN2", target_bir_lowering=False)
    c4 = build_ssm4(nc4)
    for c in range(NCORES):
        b = c // 4
        t0 = (c % 4) * TC
        m = dict(xT=_shard_cols(xT_full, c),
                 y=np.stack([y_full[b][r][c % 4] for r in range(4)]),
                 z=zs[c], wop=wo)
        m.update(c4.consts)
        if hasattr(c4, "cf_map"):
            m["cf_consts"] = c4.cf_array()
        maps4.append(m)
    r4 = run_bass_kernel_spmd(nc4, maps4, core_ids=list(range(NCORES)))
    return np.concatenate([r4.results[c]["yT"] for c in range(NCORES)], axis=1)


def _ext_slice(stream_full, c, lo, width):
    b = c // 4
    gs = (c % 4) * TC
    out = np.zeros((D, width), np.float32)
    a = gs + lo
    bcols = stream_full[:, b * 8192:(b + 1) * 8192]
    s0 = max(0, a)
    s1 = min(8192, a + width)
    if s1 > s0:
        out[:, s0 - a:s1 - a] = bcols[:, s0:s1]
    return np.ascontiguousarray(out)


def l0ffn_prep_layer(inputs, l):
    g = lambda k: np.ascontiguousarray(np.asarray(inputs[k], np.float32))
    wqkv = g("L0_ffn_wqkv")[l]
    wq = pack_weight(wqkv[:, 0:512], blk=512)
    wk = pack_weight(wqkv[:, 512:1024], blk=512)
    wvt = np.ascontiguousarray(wqkv[:, 1024:1536].reshape(8, 128, 512).transpose(1, 0, 2))
    wo = pack_weight(g("L0_ffn_wo")[l])
    qs = np.zeros((128, 2), np.float32)
    qs[:, 0] = np.tile(g("L0_ffn_q_scale")[l], 2)
    qs[:, 1] = np.tile(g("L0_ffn_k_scale")[l], 2)
    w1 = g("L0_ffn_w1")[l]
    w1p = np.zeros((D, 5632), np.float32)
    w1p[:, 0:2730] = w1[:, 0:2730]
    w1p[:, 2816:2816 + 2730] = w1[:, 2730:5460]
    w1p = pack_weight(w1p)
    b1 = g("L0_ffn_b1")[l]
    b1p = np.zeros(5632, np.float32)
    b1p[0:2730] = b1[0:2730]
    b1p[2816:2816 + 2730] = b1[2730:5460]
    b1p = np.ascontiguousarray(b1p.reshape(44, 128).T)
    w2 = g("L0_ffn_w2")[l]
    w2p = np.zeros((2816, D), np.float32)
    w2p[0:2730] = w2
    w2p = pack_weight(w2p)
    b2r = g("L0_ffn_b2")[l].reshape(1, D)
    lng = np.ascontiguousarray(g("L0_ffn_ln_g")[l].reshape(8, 128).T)
    lnb = np.ascontiguousarray(g("L0_ffn_ln_b")[l].reshape(8, 128).T)
    return dict(wqp=wq, wkp=wk, wvt=wvt, wop=wo, qs=qs,
                w1p=w1p, b1p=b1p, w2p=w2p, b2r=b2r, lng=lng, lnb=lnb)


def l0ffn_core_tables(c):
    ps0 = (c % 4) * TC
    qpos = np.arange(ps0 - 512, ps0 + TC)
    kpos = np.arange(ps0 - 1024, ps0 + TC)
    qcos, qsin = rope_tables_xpos(qpos, False)
    kcos, ksin = rope_tables_xpos(kpos, True)
    mv = np.ones((128, 12), np.float32)
    if c % 4 == 0:
        for qt in range(2):
            for di in range(8):
                if qt == 1 and di >= 4:
                    continue
                kabs = ps0 - 1024 + qt * 512 + di * 128
                mvi = di if qt == 0 else 8 + di - 4
                if kabs < 0:
                    mv[:, mvi] = 0.0
    m0 = np.full((128, 1), 0.0 if c % 4 == 0 else 1.0, np.float32)
    return dict(qcos=qcos.astype(BF), qsin=qsin.astype(BF),
                kcos=kcos.astype(BF), ksin=ksin.astype(BF),
                msk=local_masks(), mv=mv), m0


def l0ffn_call_host(inputs, xT_full, cores=range(NCORES), runner=None):
    runner = runner or (lambda nc, maps: run_bass_kernel_spmd(
        nc, maps, core_ids=list(range(NCORES))).results)
    stream = xT_full
    for l in range(2):
        wl = l0ffn_prep_layer(inputs, l)
        nc = bass.Bass("TRN2", target_bir_lowering=False)
        ca = build_l0ffn_attn(nc, do_ln=(l == 0))
        maps = []
        for c in range(NCORES):
            tabs, m0 = l0ffn_core_tables(c)
            m = dict(xin=_ext_slice(stream, c, -1152, FXE),
                     wkp=wl["wkp"], wqp=wl["wqp"], wvt=wl["wvt"],
                     wop=wl["wop"], qs=wl["qs"], **tabs)
            m.update(ca.consts)
            m["cf_consts"] = ca.cf_array()
            maps.append(m)
        r1 = runner(nc, maps)

        nc2 = bass.Bass("TRN2", target_bir_lowering=False)
        cf = build_l0ffn_ffn(nc2, add_module=(l == 1))
        maps2 = []
        for c in range(NCORES):
            tabs, m0 = l0ffn_core_tables(c)
            m = dict(x1=r1[c]["x1"], lng=wl["lng"], lnb=wl["lnb"],
                     b1p=wl["b1p"], b2r=wl["b2r"], w1p=wl["w1p"],
                     w2p=wl["w2p"], m0=m0)
            if l == 1:
                m["xmod"] = _shard_cols(xT_full, c)
            m.update(cf.consts)
            m["cf_consts"] = cf.cf_array()
            maps2.append(m)
        r2 = runner(nc2, maps2)
        stream = np.concatenate([r2[c]["x2"] for c in range(NCORES)], axis=1)
    return stream



def _shard_cols(a, c):
    return np.ascontiguousarray(a[:, c * TC:(c + 1) * TC])


def l1_ffn_inputs(inputs, xT_full):
    g = lambda k: np.ascontiguousarray(inputs[k])
    w1t = pack_weight(g("L1_ffn_w1_w").T)
    w2t = pack_weight(g("L1_ffn_w2_w").T)
    h1t = pack_weight(g("L1_ffn_h1_w").T)
    h2t = pack_weight(g("L1_ffn_h2_w").T)
    w3t = pack_weight(g("L1_ffn_w3_w").T)
    s1 = np.ascontiguousarray(g("L1_ffn_w1_s").reshape(32, 128).T)
    s2 = np.ascontiguousarray(g("L1_ffn_w2_s").reshape(32, 128).T)
    sh1 = np.ascontiguousarray(g("L1_ffn_h1_s").reshape(32, 128).T)
    sh2 = np.ascontiguousarray(g("L1_ffn_h2_s").reshape(32, 128).T)
    s3 = np.ascontiguousarray(g("L1_ffn_w3_s").reshape(8, 128).T)
    maps = []
    for c in range(NCORES):
        maps.append(dict(xT=_shard_cols(xT_full, c), w1t=w1t, w2t=w2t,
                         h1t=h1t, h2t=h2t, w3t=w3t, s1=s1, s2=s2, sh1=sh1,
                         sh2=sh2, s3=s3))
    return maps


def run_l1_ffn(inputs, xT_full):
    nc = bass.Bass("TRN2", target_bir_lowering=False)
    ctx = build_l1_ffn(nc)
    maps = l1_ffn_inputs(inputs, xT_full)
    extra = dict(ctx.consts)
    extra["cf_consts"] = ctx.cf_array()
    for m in maps:
        m.update(extra)
    res = run_bass_kernel_spmd(nc, maps, core_ids=list(range(NCORES)))
    out = np.concatenate([res.results[c]["yT"] for c in range(NCORES)], axis=1)
    return out



def attn_a_host(inputs, xT_full):
    g = lambda k: np.ascontiguousarray(np.asarray(inputs[k], np.float32))
    perm = qperm_attn_a()
    wq = pack_weight(g("L0_attn_a_Wq")[:, perm])
    wkv = g("L0_attn_a_Wkv")
    wk = pack_weight(wkv[:, 0:256], blk=256)
    wv = np.ascontiguousarray(wkv[:, 256:512].reshape(8, 128, 256).transpose(1, 0, 2))
    wo = pack_weight(g("L0_attn_a_Wo")[perm, :])
    bo = g("L0_attn_a_bo").reshape(1, D)

    nc1 = bass.Bass("TRN2", target_bir_lowering=False)
    c1 = build_attn_a1(nc1)
    maps = []
    for c in range(NCORES):
        pos = ((c % 4) * TC + np.arange(TC)).astype(np.float32)
        qcos, qsin = rope_tables_attn_a(pos)
        m = dict(xT=_shard_cols(xT_full, c), wq=wq, wk=wk, wv=wv,
                 qcos=qcos, qsin=qsin)
        m.update(c1.consts)
        if hasattr(c1, "cf_map"):
            m["cf_consts"] = c1.cf_array()
        maps.append(m)
    r1 = run_bass_kernel_spmd(nc1, maps, core_ids=list(range(NCORES)))

    kT_full = [np.stack([r1.results[b * 4 + i]["kT"] for i in range(4)])
               for b in range(2)]
    va_full = [np.stack([r1.results[b * 4 + i]["va"] for i in range(4)])
               for b in range(2)]

    nc2 = bass.Bass("TRN2", target_bir_lowering=False)
    c2 = build_attn_a2(nc2)
    maps2 = []
    for c in range(NCORES):
        b = c // 4
        m = dict(xT=_shard_cols(xT_full, c), qT=r1.results[c]["qT"],
                 kT=kT_full[b], va=va_full[b], wo=wo, bo=bo)
        m.update(c2.consts)
        m["cf_consts"] = c2.cf_array()
        maps2.append(m)
    r2 = run_bass_kernel_spmd(nc2, maps2, core_ids=list(range(NCORES)))
    return np.concatenate([r2.results[c]["yT"] for c in range(NCORES)], axis=1)




GROUPS4 = [[0, 1, 2, 3], [4, 5, 6, 7]]


def emit_cc(ctx, kind, src, dst, tag):
    p = ctx.p

    def fn(e, src=src, dst=dst, kind=kind):
        return e.collective_compute(kind, mybir.AluOpType.bypass,
                                    replica_groups=GROUPS4,
                                    ins=[src.ap().opt()], outs=[dst.ap().opt()])
    p.cc(fn, reads=(), writes=(tag,), tag=tag)


def emit_ridx_setup(ctx, ridx_h, nvals=8):
    p = ctx.p
    ri = ctx.sbuf_root("ri", [1, nvals], mybir.dt.int32)

    def fl(e):
        return e.dma_start(out=ri[:, :], in_=ridx_h[:, :])
    p.dma("G", fl, writes=("ri",), tag="ld_ridx")

    names = ["d0", "rk", "rm1", "t0", "x4", "x5", "x6", "x7"]
    ctx.snaps = {}

    def fs(e):
        last = None
        for k in range(nvals):
            reg = e.alloc_register()
            last = e.reg_load(reg, ri[0:1, k:k + 1])
            ctx.snaps[names[k]] = e.snap(reg)
        return last
    p.G(fs, reads=("ri",), writes=("ridx_regs",), tag="ridx_regs")


def emit_halo_asm(ctx, stream, xin, hw, nrows, maskname, ns):
    p = ctx.p
    nch = nrows // 128
    hin = ctx.dram(ns + "hin", [nrows, hw], F32, kind="Internal")
    hout = ctx.dram(ns + "hout", [4, nrows, hw], F32, kind="Internal")
    mask_h = ctx.dram(ns + "hmask", [128, 1], F32, kind="ExternalInput")
    ctx.begin_phase(ns)
    skey = "rm1"

    def f1(e):
        return e.dma_start(out=hin[:, :], in_=stream[:, TC - hw:TC])
    p.dma("Y", f1, writes=(ns + "hin",), tag="hin")

    def f2(e):
        return e.dma_start(out=xin[:, hw:hw + TC], in_=stream[:, :])
    p.dma("Y", f2, writes=(ns + "own",), tag="own")
    ctx.p.barrier(tag=ns + "pre")
    emit_cc(ctx, "AllGather", hin, hout, ns + "ag")
    ctx.p.barrier(tag=ns + "cc")

    def f25(e):
        return e.dma_start(out=xin[:, 0:hw],
                           in_=hout[bass.ds(ctx.snaps[skey], 1), :, :])
    p.add("G", f25, reads=("ridx_regs",), writes=(ns + "hx",), inc=16,
          tag="halo_d2d", role="Gd")

    tile = ctx.sbuf("halo", [128, nch, hw], F32)
    msk = ctx.sbuf("hmask", [128, 1], F32)

    def f3(e):
        return e.dma_start(out=msk[:, :], in_=mask_h[:, :])
    p.dma("Y", f3, writes=("hmask",), tag="hmask")

    def f4(e):
        return e.dma_start(out=tile[:, :, :],
                           in_=xin[:, 0:hw].rearrange("(c p) t -> p c t", p=128))
    p.dma("Y", f4, reads=(ns + "hx",), writes=("halo",), tag="halo_ld")
    for ch in range(nch):
        def f5(e, ch=ch):
            return e.tensor_scalar_mul(tile[:, ch, :], tile[:, ch, :],
                                       msk[:, 0:1])
        p.V(f5, reads=("halo", "hmask"), writes=("halo",), tag="halo_msk")

    def f6(e):
        return e.dma_start(out=xin[:, 0:hw].rearrange("(c p) t -> p c t", p=128),
                           in_=tile[:, :, :])
    p.dma("Y", f6, reads=("halo",), writes=(ns + "halo@o",), tag="halo_st")
    ctx.end_phase()


def emit_dyn_extract(ctx, src, dst, which, tag):
    p = ctx.p
    n = dst.shape[1]
    for r in range(4):
        if which == "rows":
            def fn(e, r=r, n=n):
                return e.dma_start(out=dst[r, :, :],
                                   in_=src[r, bass.ds(ctx.snaps["d0"], n), :])
        else:
            def fn(e, r=r):
                return e.dma_start(out=dst[r, :, :],
                                   in_=src[r, bass.ds(ctx.snaps["rk"], 1), :, :])
        p.add("G", fn, reads=("ridx_regs",), writes=(tag,), inc=16,
              tag=tag, role="Gd")


def build_full(nc):
    ctx = make_ctx(nc)
    p = ctx.p
    xT_in = nc.dram_tensor("xT", [D, TC], F32, kind="ExternalInput")
    xT_out = nc.dram_tensor("out", [D, TC], F32, kind="ExternalOutput")
    ridx_h = nc.dram_tensor("ridx", [1, 8], mybir.dt.int32, kind="ExternalInput")

    st = {i: nc.dram_tensor(f"s{i}", [D, TC], F32) for i in range(1, 8)}

    with ctx.es:
        get_ones(ctx)
        ctx.cf(0.0)
        emit_ridx_setup(ctx, ridx_h)
        p.barrier(tag="setup")

        aa_qT = nc.dram_tensor("aa_qT", [D, TC], BF16)
        aa_kT = nc.dram_tensor("aa_kT", [256, TC], BF16)
        aa_va = nc.dram_tensor("aa_va", [TC, 260], BF16)
        aag_kT = nc.dram_tensor("aag_kT", [4, 256, TC], BF16)
        aag_va = nc.dram_tensor("aag_va", [4, TC, 260], BF16)
        ctx.begin_phase("a1_")
        build_attn_a1(None, fctx=ctx, io=dict(xT=xT_in, qT=aa_qT, kT=aa_kT,
                                              va=aa_va))
        ctx.end_phase()
        emit_cc(ctx, "AllGather", aa_kT, aag_kT, "cc_aakT")
        emit_cc(ctx, "AllGather", aa_va, aag_va, "cc_aava")
        p.barrier(tag="cc_aa")
        ctx.begin_phase("a2_")
        build_attn_a2(None, fctx=ctx, io=dict(xT=xT_in, qT=aa_qT, kT=aag_kT,
                                              va=aag_va, yT=st[1]))
        ctx.end_phase()

        ctx.ns = "fl_"
        fw = []
        for l in range(2):
            pre = f"l{l}_"
            fw.append(dict(
                wkp=ctx.dram(pre + "wkp", [1, 128, 8, 512], F32, "ExternalInput"),
                wqp=ctx.dram(pre + "wqp", [1, 128, 8, 512], F32, "ExternalInput"),
                wvt=ctx.dram(pre + "wvt", [128, 8, 512], F32, "ExternalInput"),
                wop=ctx.dram(pre + "wop", [2, 128, 4, 512], F32, "ExternalInput"),
                qs=ctx.dram(pre + "qs", [128, 2], F32, "ExternalInput"),
                w1p=ctx.dram(pre + "w1p", [11, 128, 8, 512], F32, "ExternalInput"),
                b1p=ctx.dram(pre + "b1p", [128, 44], F32, "ExternalInput"),
                w2p=ctx.dram(pre + "w2p", [2, 128, 22, 512], F32, "ExternalInput"),
                b2r=ctx.dram(pre + "b2r", [1, D], F32, "ExternalInput"),
                lng=ctx.dram(pre + "lng", [128, 8], F32, "ExternalInput"),
                lnb=ctx.dram(pre + "lnb", [128, 8], F32, "ExternalInput"),
            ))
        fsh = dict(
            qcos=ctx.dram("qcos", [128, FQE], BF16, "ExternalInput"),
            qsin=ctx.dram("qsin", [128, FQE], BF16, "ExternalInput"),
            kcos=ctx.dram("kcos", [128, FKE], BF16, "ExternalInput"),
            ksin=ctx.dram("ksin", [128, FKE], BF16, "ExternalInput"),
            msk=ctx.dram("msk", [128, 8, 512], BF16, "ExternalInput"),
            mv=ctx.dram("mv", [128, 12], F32, "ExternalInput"),
            m0=ctx.dram("m0", [128, 1], F32, "ExternalInput"),
        )
        ctx.ns = ""

        def l0ffn_call(callidx, src, dst):
            cur = src
            for l in range(2):
                ns = f"f{callidx}{l}_"
                xin = nc.dram_tensor(ns + "xin", [D, FXE], F32)
                x1 = nc.dram_tensor(ns + "x1", [D, 2560], F32)
                nxt = dst if l == 1 else nc.dram_tensor(ns + "mid", [D, TC], F32)
                emit_halo_asm(ctx, cur, xin, 1152, D, None, ns + "h_")
                ctx.begin_phase(ns + "a_")
                io = dict(xin=xin, x1=x1, **fw[l], **fsh)
                build_l0ffn_attn(None, do_ln=(l == 0), fctx=ctx, io=io)
                ctx.end_phase()
                ctx.begin_phase(ns + "f_")
                io2 = dict(x1=x1, x2=nxt, **fw[l], **fsh)
                if l == 1:
                    io2["xmod"] = src
                build_l0ffn_ffn(None, add_module=(l == 1), fctx=ctx, io=io2)
                ctx.end_phase()
                cur = nxt

        l0ffn_call(0, st[1], st[2])

        b_qT = nc.dram_tensor("b_qT", [64, 16, TC], BF16)
        b_kT = nc.dram_tensor("b_kT", [D, TC], BF16)
        b_va = nc.dram_tensor("b_va", [16, TC, 65], BF16)
        bg_kT = nc.dram_tensor("bg_kT", [4, D, TC], BF16)
        bg_va = nc.dram_tensor("bg_va", [4, 16, TC, 65], BF16)
        ctx.begin_phase("b1_")
        build_attn_b1(None, fctx=ctx, io=dict(xT=st[2], qT=b_qT, kT=b_kT,
                                              va=b_va))
        ctx.end_phase()
        emit_cc(ctx, "AllGather", b_kT, bg_kT, "cc_bkT")
        emit_cc(ctx, "AllGather", b_va, bg_va, "cc_bva")
        p.barrier(tag="cc_b")
        ctx.begin_phase("b2_")
        build_attn_b2(None, fctx=ctx, io=dict(xT=st[2], qT=b_qT, kT=bg_kT,
                                              va=bg_va, yT=st[3]))
        ctx.end_phase()

        l0ffn_call(1, st[3], st[4])

        c_kc = nc.dram_tensor("c_kc", [D, TC], F32)
        c_kvc = nc.dram_tensor("c_kvc", [D, TC], F32)
        c_tot = nc.dram_tensor("c_tot", [D, 2], F32)
        cg_tot = nc.dram_tensor("cg_tot", [4, D, 2], F32)
        ctx.begin_phase("c1_")
        build_cum1(None, fctx=ctx, io=dict(xT=st[4], kc=c_kc, kvc=c_kvc,
                                           tot=c_tot))
        ctx.end_phase()
        emit_cc(ctx, "AllGather", c_tot, cg_tot, "cc_tot")
        p.barrier(tag="cc_c")
        ctx.begin_phase("c2_")
        build_cum2(None, fctx=ctx, io=dict(xT=st[4], kc=c_kc, kvc=c_kvc,
                                           tots=cg_tot, yT=st[5]))
        ctx.end_phase()

        ctx.ns = "l1f_"
        l1w = dict(
            w1t=ctx.dram("w1t", [8, 128, 8, 512], F32, "ExternalInput"),
            w2t=ctx.dram("w2t", [8, 128, 8, 512], F32, "ExternalInput"),
            h1t=ctx.dram("h1t", [8, 128, 32, 512], F32, "ExternalInput"),
            h2t=ctx.dram("h2t", [8, 128, 32, 512], F32, "ExternalInput"),
            w3t=ctx.dram("w3t", [2, 128, 32, 512], F32, "ExternalInput"),
            s1=ctx.dram("s1", [128, 32], F32, "ExternalInput"),
            s2=ctx.dram("s2", [128, 32], F32, "ExternalInput"),
            sh1=ctx.dram("sh1", [128, 32], F32, "ExternalInput"),
            sh2=ctx.dram("sh2", [128, 32], F32, "ExternalInput"),
            s3=ctx.dram("s3", [128, 8], F32, "ExternalInput"),
        )
        ctx.ns = ""
        ctx.begin_phase("m1_")
        ctx.wpool_cfg = (32, 512)
        emit_l1_ffn_body(ctx, st[5], st[6], l1w, 512)
        ctx.end_phase()

        sm_xp = nc.dram_tensor("sm_xp", [2048, TC], F32)
        sm_z = nc.dram_tensor("sm_z", [2048, TC], F32)
        sm_xin = nc.dram_tensor("sm_xin", [2048, TC + 128], F32)
        sm_xc = nc.dram_tensor("sm_xc", [2048, TC], BF16)
        sm_dt = nc.dram_tensor("sm_dt", [2048, TC], BF16)
        sm_bc = nc.dram_tensor("sm_bc", [32, TC], F32)
        smg_xc = nc.dram_tensor("smg_xc", [4, 2048, TC], BF16)
        smg_dt = nc.dram_tensor("smg_dt", [4, 2048, TC], BF16)
        smg_bc = nc.dram_tensor("smg_bc", [4, 32, TC], F32)
        sm_ys = nc.dram_tensor("sm_ys", [4, 512, TC], BF16)
        smg_ys = nc.dram_tensor("smg_ys", [4, 4, 512, TC], BF16)
        scr_dt = nc.dram_tensor("scr_dt", [4, 512, TC], BF16)
        scr_xc = nc.dram_tensor("scr_xc", [4, 512, TC], BF16)
        scr_y = nc.dram_tensor("scr_y", [4, 512, TC], BF16)
        ctx.begin_phase("sm1_")
        build_ssm1(None, fctx=ctx, io=dict(xT=st[6], xpart=sm_xp, z=sm_z))
        ctx.end_phase()
        emit_halo_asm(ctx, sm_xp, sm_xin, 128, 2048, None, "smh_")
        ctx.begin_phase("sm2_")
        build_ssm2(None, fctx=ctx, io=dict(xin=sm_xin, xc=sm_xc, dtt=sm_dt,
                                           bc=sm_bc), xin_pad=128)
        ctx.end_phase()
        emit_cc(ctx, "AllGather", sm_xc, smg_xc, "cc_xc")
        emit_cc(ctx, "AllGather", sm_dt, smg_dt, "cc_dt")
        emit_cc(ctx, "AllGather", sm_bc, smg_bc, "cc_bc")
        p.barrier(tag="cc_sm")
        emit_dyn_extract(ctx, smg_dt, scr_dt, "rows", "xdt")
        emit_dyn_extract(ctx, smg_xc, scr_xc, "rows", "xxc")
        p.barrier(tag="xtr_sm")
        ctx.begin_phase("sm3_")
        build_ssm3(None, fctx=ctx, io=dict(dts=scr_dt, xcs=scr_xc, bcf=smg_bc,
                                           ys=sm_ys))
        ctx.end_phase()
        emit_cc(ctx, "AllGather", sm_ys, smg_ys, "cc_ys")
        p.barrier(tag="cc_ys")
        emit_dyn_extract(ctx, smg_ys, scr_y, "blk", "xy")
        p.barrier(tag="xtr_y")
        ctx.begin_phase("sm4_")
        build_ssm4(None, fctx=ctx, io=dict(xT=st[6], y=scr_y, z=sm_z,
                                           yT=st[7]))
        ctx.end_phase()

        ctx.begin_phase("m2_")
        ctx.wpool_cfg = (32, 512)
        emit_l1_ffn_body(ctx, st[7], xT_out, l1w, 512)
        ctx.end_phase()

        p.finalize(nc)
    return ctx


def fused_maps(inputs, ctx):
    g = lambda k: np.ascontiguousarray(np.asarray(inputs[k], np.float32))
    x = np.asarray(inputs["L0_attn_a_x"], np.float32)
    xT_full = np.ascontiguousarray(x.reshape(2 * 8192, D).T)

    perm = qperm_attn_a()
    shared = {}
    shared["a1_wq"] = pack_weight(g("L0_attn_a_Wq")[:, perm])
    wkv = g("L0_attn_a_Wkv")
    shared["a1_wk"] = pack_weight(wkv[:, 0:256], blk=256)
    shared["a1_wv"] = np.ascontiguousarray(
        wkv[:, 256:512].reshape(8, 128, 256).transpose(1, 0, 2))
    shared["a2_wo"] = pack_weight(g("L0_attn_a_Wo")[perm, :])
    shared["a2_bo"] = g("L0_attn_a_bo").reshape(1, D)

    for l in range(2):
        wl = l0ffn_prep_layer(inputs, l)
        for k, v in wl.items():
            shared[f"fl_l{l}_" + {"wkp": "wkp", "wqp": "wqp", "wvt": "wvt",
                                  "wop": "wop", "qs": "qs", "w1p": "w1p",
                                  "b1p": "b1p", "w2p": "w2p", "b2r": "b2r",
                                  "lng": "lng", "lnb": "lnb"}[k]] = v
    shared["fl_msk"] = local_masks()

    shared["b1_wq"] = pack_weight(g("L0_attn_b_Wq"))
    wkm = g("L0_attn_b_Wk")
    shared["b1_wk"] = pack_weight(wkm)
    shared["b1_wkh"] = np.ascontiguousarray(
        wkm.reshape(8, 128, 1024).transpose(1, 0, 2))
    shared["b2_wo"] = pack_weight(g("L0_attn_b_Wo"))
    shared["b2_bo"] = g("L0_attn_b_bo").reshape(1, D)

    shared["c1_wkp"] = pack_weight(g("L1_attn_a_Wk"))
    shared["c1_wvp"] = pack_weight(g("L1_attn_a_Wv"))
    shared["c2_wqp"] = pack_weight(g("L1_attn_a_Wq"))
    shared["c2_wcp"] = pack_weight(g("L1_attn_a_Wc"))

    shared["l1f_w1t"] = pack_weight(g("L1_ffn_w1_w").T)
    shared["l1f_w2t"] = pack_weight(g("L1_ffn_w2_w").T)
    shared["l1f_h1t"] = pack_weight(g("L1_ffn_h1_w").T)
    shared["l1f_h2t"] = pack_weight(g("L1_ffn_h2_w").T)
    shared["l1f_w3t"] = pack_weight(g("L1_ffn_w3_w").T)
    shared["l1f_s1"] = np.ascontiguousarray(g("L1_ffn_w1_s").reshape(32, 128).T)
    shared["l1f_s2"] = np.ascontiguousarray(g("L1_ffn_w2_s").reshape(32, 128).T)
    shared["l1f_sh1"] = np.ascontiguousarray(g("L1_ffn_h1_s").reshape(32, 128).T)
    shared["l1f_sh2"] = np.ascontiguousarray(g("L1_ffn_h2_s").reshape(32, 128).T)
    shared["l1f_s3"] = np.ascontiguousarray(g("L1_ffn_w3_s").reshape(8, 128).T)

    shared["sm1_wip"] = pack_weight(np.ascontiguousarray(g("L1_attn_b_in_proj_w").T))
    shared["sm2_wxp"] = pack_weight(np.ascontiguousarray(g("L1_attn_b_x_proj_w").T), blk=96)
    shared["sm2_wdt"] = pack_weight(np.ascontiguousarray(g("L1_attn_b_dt_proj_w").T))
    shared["sm2_cw"] = np.ascontiguousarray(
        g("L1_attn_b_conv_w")[:, 0, :].reshape(16, 128, 4).transpose(1, 0, 2))
    shared["sm2_cb"] = np.ascontiguousarray(g("L1_attn_b_conv_b").reshape(16, 128).T)
    shared["sm2_dtb"] = np.ascontiguousarray(g("L1_attn_b_dt_proj_b").reshape(16, 128).T)
    shared["sm4_wop"] = pack_weight(np.ascontiguousarray(g("L1_attn_b_out_proj_w").T))
    alog = g("L1_attn_b_A_log")
    ddv = g("L1_attn_b_D")

    shared["cf_consts"] = ctx.cf_array()
    shared.update(ctx.consts)

    maps = []
    for c in range(NCORES):
        m = dict(shared)
        m["xT"] = _shard_cols(xT_full, c)
        r = c % 4
        rm1 = (r + 3) % 4
        m["ridx"] = np.array([[r * 512, r, rm1, r * TC, 0, 0, 0, 0]],
                             np.int32)
        pos = (r * TC + np.arange(TC)).astype(np.float32)
        qc, qn = rope_tables_attn_a(pos)
        m["a1_qcos"], m["a1_qsin"] = qc, qn
        tabs, m0 = l0ffn_core_tables(c)
        m["fl_qcos"], m["fl_qsin"] = tabs["qcos"], tabs["qsin"]
        m["fl_kcos"], m["fl_ksin"] = tabs["kcos"], tabs["ksin"]
        m["fl_mv"] = tabs["mv"]
        m["fl_m0"] = m0
        hmask = np.full((128, 1), 0.0 if r == 0 else 1.0, np.float32)
        for l in range(2):
            for ci in range(2):
                m[f"f{ci}{l}_h_hmask"] = hmask
        m["smh_hmask"] = hmask
        cmask = np.zeros((128, 4), np.float32)
        for cp in range(4):
            if cp < r:
                cmask[:, cp] = 1.0
        m["c2_cmask"] = cmask
        d0 = r * 512
        m["sm3_alog"] = np.ascontiguousarray(
            alog[d0:d0 + 512].reshape(4, 128, 16).transpose(1, 0, 2))
        m["sm3_dd"] = np.ascontiguousarray(ddv[d0:d0 + 512].reshape(4, 128).T)
        maps.append(m)
    return maps, xT_full


_FUSED_CACHE = {}


def kernel_fused(**inputs):
    if "nc" not in _FUSED_CACHE:
        nc = bass.Bass("TRN2", target_bir_lowering=False)
        ctx = build_full(nc)
        _FUSED_CACHE["nc"] = nc
        _FUSED_CACHE["ctx"] = ctx
    nc, ctx = _FUSED_CACHE["nc"], _FUSED_CACHE["ctx"]
    maps, _ = fused_maps(inputs, ctx)
    res = run_bass_kernel_spmd(nc, maps, core_ids=list(range(NCORES)))
    out = np.concatenate([res.results[c]["out"] for c in range(NCORES)], axis=1)
    return np.ascontiguousarray(out.T).reshape(2, 8192, D)


def kernel(**inputs):
    return kernel_fused(**inputs)


def kernel_unfused(**inputs):
    x = np.asarray(inputs["L0_attn_a_x"], np.float32)
    xT = np.ascontiguousarray(x.reshape(2 * 8192, D).T)
    xT = attn_a_host(inputs, xT)
    xT = np.ascontiguousarray(xT)
    xT = l0ffn_call_host(inputs, xT)
    xT = np.ascontiguousarray(xT)
    xT = attn_b_host(inputs, xT)
    xT = np.ascontiguousarray(xT)
    xT = l0ffn_call_host(inputs, xT)
    xT = np.ascontiguousarray(xT)
    xT = cum_host(inputs, xT)
    xT = np.ascontiguousarray(xT)
    xT = run_l1_ffn(inputs, xT)
    xT = np.ascontiguousarray(xT)
    xT = ssm_host(inputs, xT)
    xT = np.ascontiguousarray(xT)
    xT = run_l1_ffn(inputs, xT)
    return np.ascontiguousarray(xT.T).reshape(2, 8192, D)


# version 07
# speedup vs baseline: 1.3139x; 1.0406x over previous
import contextlib
import numpy as np
import ml_dtypes
import concourse.bass as bass
import concourse.mybir as mybir
from concourse.bass_utils import run_bass_kernel_spmd

F32 = mybir.dt.float32
BF16 = mybir.dt.bfloat16
AF = mybir.ActivationFunctionType
ALU = mybir.AluOpType
BF = ml_dtypes.bfloat16

NCORES = 8
TC = 2048
D = 1024
NCH = D // 128


STREAMS = ("T", "V", "S", "G", "Y")


class Op:
    __slots__ = ("stream", "fn", "reads", "writes", "inc", "tag", "idx",
                 "sidx", "deps", "need_inc", "cum", "waits", "role")

    def __init__(self, stream, fn, reads, writes, inc, tag, role=None):
        self.stream = stream
        self.fn = fn
        self.reads = reads
        self.writes = writes
        self.inc = inc
        self.tag = tag
        self.role = role
        self.deps = []
        self.need_inc = False
        self.cum = None
        self.waits = []


class Prog:
    def __init__(self):
        self.ops = []
        self.last_writer = {}
        self.readers = {}

    PSUM_PREFIXES = ("ps", "pb", "pot", "pst", "ppv", "pbc", "pstat")

    def add(self, stream, fn, reads=(), writes=(), inc=1, tag="", role=None):
        assert not getattr(self, "frozen", False), f"op added during finalize: {tag}"
        op = Op(stream, fn, tuple(reads), tuple(writes), inc, tag, role)
        op.idx = len(self.ops)
        self.ops.append(op)
        deps = set()
        for k in op.reads:
            w = self.last_writer.get(k)
            if w is not None:
                deps.add(w.idx)
            if k.startswith(self.PSUM_PREFIXES) and stream in ("V", "S"):
                for r in self.readers.get(k, ()):
                    if r.stream in ("V", "S") and r.stream != stream:
                        raise AssertionError(
                            f"PSUM bank {k} read by both {r.stream}({r.tag}) and "
                            f"{stream}({tag}) without intervening write - fatal "
                            f"bank conflict on hardware")
            self.readers.setdefault(k, []).append(op)
        for k in op.writes:
            w = self.last_writer.get(k)
            if w is not None:
                deps.add(w.idx)
            for r in self.readers.get(k, ()):
                deps.add(r.idx)
            self.readers[k] = []
            self.last_writer[k] = op
        deps.discard(op.idx)
        op.deps = sorted(deps)
        return op

    def T(self, fn, reads=(), writes=(), tag=""):
        return self.add("T", fn, reads, writes, 1, tag)

    def V(self, fn, reads=(), writes=(), tag=""):
        return self.add("V", fn, reads, writes, 1, tag)

    def S(self, fn, reads=(), writes=(), tag=""):
        return self.add("S", fn, reads, writes, 1, tag)

    def G(self, fn, reads=(), writes=(), tag=""):
        return self.add("G", fn, reads, writes, 1, tag)

    def dma(self, stream, fn, reads=(), writes=(), tag="", role=None):
        return self.add(stream, fn, reads, writes, 16, tag,
                        role=role or (stream + "d"))

    def cc(self, fn, reads=(), writes=(), tag=""):
        op = self.add("G", fn, reads, writes, 1, tag, role="cc")
        op.inc = 101
        return op

    def barrier_all(self, keys, tag="final"):
        self.add("Y", None, tuple(keys), (), 0, tag)

    def barrier(self, tag="phase"):
        last = {}
        for op in self.ops:
            if op.inc == 0:
                continue
            if op.role is not None:
                last[("R", op.role)] = op
            else:
                last[(op.stream, "c")] = op
        deps = sorted({o.idx for o in last.values()})
        for o in last.values():
            o.need_inc = True
        for st in STREAMS:
            bop = self.add(st, None, (), (), 0, f"bar_{tag}_{st}")
            bop.deps = [d for d in deps if d != bop.idx]

    def finalize(self, nc):
        self.frozen = True
        ops = self.ops
        INORDER = {"T", "V", "S", "Y"}
        for op in ops:
            best = {}
            keep = []
            for d in op.deps:
                st = ops[d].stream
                kind = ops[d].role or "c"
                k = (st, kind)
                if st in INORDER or (st == "G" and kind != "c"):
                    if k not in best or d > best[k]:
                        best[k] = d
                else:
                    keep.append(d)
            op.deps = sorted(set(keep) | set(best.values()))
        for op in ops:
            if op.inc in (16, 101):
                op.need_inc = True
            for d in op.deps:
                p = ops[d]
                if p.stream == "T" and op.stream == "T":
                    continue
                p.need_inc = True

        roles = sorted({op.role for op in ops if op.role})
        sems = {}
        es = contextlib.ExitStack()
        for st in STREAMS:
            sems[(st, "c")] = es.enter_context(nc.semaphore(f"sem_{st}c"))
        for r in roles:
            sems[("R", r)] = es.enter_context(nc.semaphore(f"sem_{r}"))
        self._sem_es = es

        counts = {}
        role_chain = {}
        for op in ops:
            if op.inc == 0:
                continue
            if op.role is not None:
                op.need_inc = True
                key = ("R", op.role)
                amt = 16 if op.inc == 16 else 1
                prev = role_chain.get(key, 0)
                if prev:
                    op.waits.append((key, prev))
                counts[key] = counts.get(key, 0) + amt
                role_chain[key] = counts[key]
                op.cum = (key, counts[key])
            elif op.need_inc:
                key = (op.stream, "c")
                counts[key] = counts.get(key, 0) + 1
                op.cum = (key, counts[key])

        seen = {st: {} for st in STREAMS}
        for op in ops:
            for d in op.deps:
                p = ops[d]
                if p.cum is None:
                    if p.stream == op.stream and p.stream == "T":
                        continue
                    if p.inc == 0:
                        continue
                    raise AssertionError(f"dep without inc: {p.tag} <- {op.tag}")
                key, val = p.cum
                cur = seen[op.stream].get(key, 0)
                if val > cur:
                    op.waits.append((key, val))
                    seen[op.stream][key] = val
            if op.waits:
                best = {}
                for key, val in op.waits:
                    if val > best.get(key, 0):
                        best[key] = val
                op.waits = [(k, v) for k, v in best.items()]

        stream_ops = {st: [o for o in ops if o.stream == st] for st in STREAMS}

        def run_stream(eng, st):
            for op in stream_ops[st]:
                for key, val in op.waits:
                    eng.wait_ge(sems[key], val)
                if op.fn is None:
                    continue
                inst = op.fn(eng)
                if op.cum is not None:
                    key, _ = op.cum
                    amt = 16 if (op.role and op.inc == 16) else 1
                    inst.then_inc(sems[key], amt)

        with nc.Block() as block:
            @block.tensor
            def _(e):
                run_stream(e, "T")

            @block.vector
            def _(e):
                run_stream(e, "V")

            @block.scalar
            def _(e):
                run_stream(e, "S")

            @block.gpsimd
            def _(e):
                run_stream(e, "G")

            @block.sync
            def _(e):
                run_stream(e, "Y")



class Ctx:
    def __init__(self, nc, prog, es):
        self.nc = nc
        self.p = prog
        self.es = es
        self.root_es = es
        self.consts = {}
        self.uid = 0
        self.ns = ""

    def begin_phase(self, ns):
        self.ns = ns
        self.es = contextlib.ExitStack()
        for attr in ("wpool", "wpool_keys", "wpool_cfg"):
            if hasattr(self, attr):
                delattr(self, attr)

    def end_phase(self):
        self.es.close()
        self.es = self.root_es
        self.ns = ""
        self.p.barrier()

    def dram(self, name, shape, dtype, kind="ExternalInput"):
        return self.nc.dram_tensor(self.ns + name, shape, dtype, kind=kind)

    def add_const(self, name, arr):
        self.consts[self.ns + name] = np.ascontiguousarray(arr)

    def name(self, base):
        self.uid += 1
        return f"{base}_{self.uid}"

    def sbuf(self, base, shape, dtype):
        return self.es.enter_context(self.nc.sbuf_tensor(self.name(base), shape, dtype))

    def sbuf_root(self, base, shape, dtype):
        return self.root_es.enter_context(
            self.nc.sbuf_tensor(self.name(base), shape, dtype))

    def psum(self, base, shape, dtype=F32):
        return self.es.enter_context(self.nc.psum_tensor(self.name(base), shape, dtype))

    def cf(self, val, rows=128):
        if not hasattr(self, "cf_tile"):
            self.cf_tile = self.sbuf_root("cft", [128, 64], F32)
            self.cf_map = {}
            self.cf_dram = self.nc.dram_tensor("cf_consts", [128, 64], F32,
                                               kind="ExternalInput")
            load_const(self, self.cf_tile, self.cf_dram, "cft")
        v = float(val)
        if v not in self.cf_map:
            col = len(self.cf_map)
            assert col < 64
            self.cf_map[v] = col
        col = self.cf_map[v]
        return self.cf_tile[0:rows, col:col + 1]

    def cf_array(self):
        arr = np.zeros((128, 64), np.float32)
        if hasattr(self, "cf_map"):
            for v, col in self.cf_map.items():
                arr[:, col] = v
        return arr

    def const_input(self, name, arr):
        arr = np.ascontiguousarray(arr)
        self.consts[name] = arr
        dt = {np.dtype(np.float32): F32, np.dtype(BF): BF16}[arr.dtype]
        return self.nc.dram_tensor(name, list(arr.shape), dt, kind="ExternalInput")


def make_ctx(nc):
    es = contextlib.ExitStack()
    return Ctx(nc, Prog(), es)


def iod(ctx, io, name, shape, dtype, kind="ExternalInput"):
    if io and name in io:
        return io[name]
    return ctx.dram(name, shape, dtype, kind=kind)



def load_xT(ctx, src_dram, dst, key, t0=0, tlen=TC, nch=NCH, stream="Y"):
    p = ctx.p
    ap = src_dram[:, t0:t0 + tlen].rearrange("(c p) t -> p c t", p=128)

    def fn(e, ap=ap, dst=dst, tlen=tlen):
        return e.dma_start(out=dst[:, :, 0:tlen], in_=ap)
    p.dma(stream, fn, reads=(key + "@hbm",), writes=(key,), tag=f"load_{key}")


def store_xT(ctx, dst_dram, src, key, outkey, t0=0, tlen=TC, nch=NCH,
             stream="Y"):
    p = ctx.p
    ap = dst_dram[:, t0:t0 + tlen].rearrange("(c p) t -> p c t", p=128)

    def fn(e, ap=ap, src=src):
        return e.dma_start(out=ap, in_=src[:, :, :])
    p.dma(stream, fn, reads=(key,), writes=(outkey,), tag=f"store_{outkey}")


class LNHelper:

    def __init__(self, ctx, tchunk=512, nch=NCH):
        self.ctx = ctx
        self.tchunk = tchunk
        self.nch = nch
        self.sq_s = [ctx.sbuf(f"ln_sq_{i}", [128, tchunk], F32) for i in range(2)]
        self.sq2_s = None
        self.rows_s = [ctx.sbuf("ln_rows_0", [1, 4 * tchunk], F32)] * 2
        self.cnt = 0
        self.ones_sb = get_ones(ctx)

    def emit(self, x, xkey, out, outkey, ps_stat, ps_stat2, ps_b0, ps_b1,
             eps=1e-5, out2=None, out2key=None, gamma=None, beta=None,
             tlen=None):
        ctx, p = self.ctx, self.ctx.p
        nch, tc = self.nch, (tlen if tlen is not None else self.tchunk)
        nf = nch * 128
        si = self.cnt % 2
        self.cnt += 1
        if gamma is not None and self.sq2_s is None:
            self.sq2_s = [ctx.sbuf(f"ln_sq2_{i}", [128, self.tchunk], F32)
                          for i in range(2)]
        sq = self.sq_s[si]
        sq2 = self.sq2_s[si] if self.sq2_s else None
        kSQ, kSQ2, kRW = f"ln_sq_{si}", f"ln_sq2_{si}", "ln_rows_0"
        stat, stat_k = ps_stat
        stat2, stat2_k = ps_stat2
        b0, b0_k = ps_b0
        b1, b1_k = ps_b1
        for c in range(nch):
            def fn(e, c=c, x=x, stat=stat, tc=tc):
                return e.matmul(stat[0:1, 0:tc], self.ones_sb[:, 0:1],
                                x[:, c, 0:tc], start=(c == 0), stop=(c == nch - 1))
            p.T(fn, reads=(xkey, "ones"), writes=(stat_k,), tag="ln_mean")
        for c in range(nch):
            def fnsq(e, c=c, x=x, tc=tc):
                return e.activation(sq[:, 0:tc], x[:, c, 0:tc], AF.Square)
            p.S(fnsq, reads=(xkey,), writes=(kSQ,), tag=kSQ)

            def fnmm(e, c=c, stat2=stat2, tc=tc):
                return e.matmul(stat2[0:1, 0:tc], self.ones_sb[:, 0:1],
                                sq[:, 0:tc], start=(c == 0), stop=(c == nch - 1))
            p.T(fnmm, reads=(kSQ, "ones"), writes=(stat2_k,), tag="ln_sumsq")
        r = self.rows_s[si]

        def f1(e):
            return e.tensor_scalar_mul(r[0:1, 0:tc], stat[0:1, 0:tc], -1.0 / nf)
        p.V(f1, reads=(stat_k,), writes=(kRW + "0",), tag=kRW + "0")

        def f2(e):
            return e.activation(r[0:1, 2 * tc:3 * tc], r[0:1, 0:tc], AF.Square)
        p.S(f2, reads=(kRW + "0",), writes=(kRW + "2",), tag=kRW + "2")

        def f3(e):
            return e.tensor_scalar_mul(r[0:1, 3 * tc:4 * tc], stat2[0:1, 0:tc],
                                       1.0 / nf)
        p.V(f3, reads=(stat2_k,), writes=(kRW + "3",), tag=kRW + "3")

        def f4(e):
            return e.tensor_tensor(r[0:1, 3 * tc:4 * tc], r[0:1, 3 * tc:4 * tc],
                                   r[0:1, 2 * tc:3 * tc], op=ALU.subtract)
        p.V(f4, reads=(kRW + "3", kRW + "2"), writes=(kRW + "3",), tag="ln_var")

        epsap = ctx.cf(eps, rows=1)

        def f5(e):
            return e.activation(r[0:1, 2 * tc:3 * tc], r[0:1, 3 * tc:4 * tc],
                                AF.Sqrt, bias=epsap)
        p.S(f5, reads=(kRW + "3", "cft"), writes=(kRW + "2",), tag="ln_sqrt")

        def f6(e):
            return e.reciprocal(r[0:1, tc:2 * tc], r[0:1, 2 * tc:3 * tc])
        p.V(f6, reads=(kRW + "2",), writes=(kRW + "1",), tag="ln_rstd")

        def b0fn(e):
            return e.matmul(b0[:, 0:tc], self.ones_sb[0:1, 0:128], r[0:1, 0:tc],
                            start=True, stop=True)
        p.T(b0fn, reads=(kRW + "0", "ones"), writes=(b0_k,), tag="ln_bc0")

        def b1fn(e):
            return e.matmul(b1[:, 0:tc], self.ones_sb[0:1, 0:128], r[0:1, tc:2 * tc],
                            start=True, stop=True)
        p.T(b1fn, reads=(kRW + "1", "ones"), writes=(b1_k,), tag="ln_bc1")

        for c in range(nch):
            def a1(e, c=c, x=x, tc=tc):
                return e.tensor_tensor(sq[:, 0:tc], x[:, c, 0:tc],
                                       b0[:, 0:tc], op=ALU.add)
            p.V(a1, reads=(xkey, b0_k), writes=(kSQ,), tag="ln_apply0")

            if gamma is None:
                def a2(e, c=c, out=out, tc=tc):
                    return e.tensor_tensor(out[:, c, 0:tc], sq[:, 0:tc],
                                           b1[:, 0:tc], op=ALU.mult)
                p.V(a2, reads=(kSQ, b1_k), writes=(outkey,), tag="ln_apply1")
            else:
                def a2(e, c=c, tc=tc):
                    return e.tensor_tensor(sq2[:, 0:tc], sq[:, 0:tc],
                                           b1[:, 0:tc], op=ALU.mult)
                p.V(a2, reads=(kSQ, b1_k), writes=(kSQ2,), tag="ln_apply1")

                def a2b(e, c=c, out=out, tc=tc):
                    return e.tensor_scalar(out[:, c, 0:tc], sq2[:, 0:tc],
                                           gamma(c), beta(c),
                                           op0=ALU.mult, op1=ALU.add)
                p.V(a2b, reads=(kSQ2, "lnp"), writes=(outkey,), tag="ln_gb")
            if out2 is not None:
                w0, w1_ = out2[1]
                if w1_ > w0:
                    def a3(e, c=c, w0=w0, w1_=w1_):
                        return e.tensor_tensor(out2[0](c, w0, w1_),
                                               sq[:, w0:w1_],
                                               b1[:, w0:w1_], op=ALU.mult)
                    p.V(a3, reads=(kSQ, b1_k), writes=(out2key,),
                        tag="ln_apply2")


def get_ones(ctx):
    if not hasattr(ctx, "ones_sb"):
        ctx.ones_sb = ctx.sbuf_root("ones", [128, 512], F32)

        def fn(e):
            return e.memset(ctx.ones_sb[:, :], 1.0)
        ctx.p.G(fn, writes=("ones",), tag="ones")
    return ctx.ones_sb


def load_const(ctx, sb, dram, key, stream="Y"):
    p = ctx.p

    def fn(e, sb=sb, dram=dram):
        shape = dram.shape
        if len(shape) == 2:
            return e.dma_start(out=sb[:, :], in_=dram[:, :])
        elif len(shape) == 3:
            return e.dma_start(out=sb[:, :, :], in_=dram[:, :, :])
        raise ValueError
    p.dma(stream, fn, reads=(), writes=(key,), tag=f"ldc_{key}")


def pack_weight(wt, blk=512):
    d_in, d_out = wt.shape
    nin = (d_in + 127) // 128
    nb = (d_out + blk - 1) // blk
    out = np.zeros((nb, 128, nin, blk), np.float32)
    for b in range(nb):
        o0 = b * blk
        olen = min(blk, d_out - o0)
        w = wt[:, o0:o0 + olen]
        wpad = np.zeros((nin * 128, olen), np.float32)
        wpad[:d_in] = w
        out[b, :, :, :olen] = wpad.reshape(nin, 128, olen).transpose(1, 0, 2)
    return np.ascontiguousarray(out)


def get_wpool(ctx):
    if not hasattr(ctx, "wpool"):
        nin_max, blk = getattr(ctx, "wpool_cfg", (32, 512))
        ctx.wpool = [ctx.sbuf(f"wp{i}", [128, nin_max, blk], BF16)
                     for i in range(2)]
        ctx.wpool_keys = ["wp0", "wp1"]
    return ctx.wpool, ctx.wpool_keys


class WStream:

    def __init__(self, ctx, wname, dram, d_in, d_out, blk=512, tag=None):
        self.ctx = ctx
        self.dram = dram
        self.d_in = d_in
        self.d_out = d_out
        self.blk = blk
        self.nin = (d_in + 127) // 128
        pool, keys = get_wpool(ctx)
        assert self.nin <= pool[0].shape[1] and blk <= pool[0].shape[2], (self.nin, blk)
        self.bufs = [pool[0], pool[1]]
        self.keys = list(keys)
        self.tag = tag or wname

    def load_block(self, bi, b):
        ctx, p = self.ctx, self.ctx.p
        buf = self.bufs[bi]
        nin = self.nin

        def fn(e, b=b, buf=buf, nin=nin):
            return e.dma_start(out=buf[:, 0:nin, 0:self.blk],
                               in_=self.dram[b, :, :, :])
        p.dma("G", fn, reads=(), writes=(self.keys[bi],),
              tag=f"wload_{self.tag}", role=f"w{bi}")


def emit_proj(ctx, ws, rhs_tiles, rhs_key, T_len, evac, psum_banks,
              rhs_rows=None, extra_mm=None, t_outer=False, blk_range=None):
    p = ctx.p
    d_out = ws.d_out
    blk = ws.blk
    nin = ws.nin
    d_in = ws.d_in
    rem = d_in - (d_in // 128) * 128
    nocs = (d_out + 127) // 128
    tch = 512
    nts = (T_len + tch - 1) // tch

    nblocks = (d_out + blk - 1) // blk
    pi = 0

    def inner(b, bi, ti):
        nonlocal pi
        o0 = b * blk
        olen = min(blk, d_out - o0)
        t0 = ti * tch
        tlen = min(tch, T_len - t0)
        for oc in range(0, olen, 128):
            ocw = min(128, olen - oc)
            ps, ps_key = psum_banks[pi % len(psum_banks)]
            pi += 1
            for c in range(nin):
                kw = 128 if (c < nin - 1 or rem == 0) else rem

                def fn(e, c=c, kw=kw, bi=bi, oc=oc, ocw=ocw, t0=t0,
                       tlen=tlen, ps=ps, start=(c == 0)):
                    if callable(rhs_tiles):
                        rhs = rhs_tiles(c)[0:kw, t0:t0 + tlen]
                    else:
                        rhs = rhs_tiles[0:kw, c, t0:t0 + tlen]
                    return e.matmul(ps[0:ocw, 0:tlen],
                                    ws.bufs[bi][0:kw, c, oc:oc + ocw],
                                    rhs,
                                    start=start,
                                    stop=(c == nin - 1 and extra_mm is None))
                p.T(fn, reads=(ws.keys[bi], rhs_key), writes=(ps_key,),
                    tag=f"mm_{ws.tag}")
            if extra_mm is not None:
                def fne(e, oc=oc + o0, ocw=ocw, t0=t0, tlen=tlen, ps=ps):
                    return extra_mm(e, oc, ocw, t0, tlen, ps)
                p.T(fne, reads=(ws.keys[bi], rhs_key) + getattr(extra_mm, "reads", ()),
                    writes=(ps_key,), tag=f"mmx_{ws.tag}")
            evac(o0 + oc, ocw, t0, tlen, ps, ps_key)

    b_lo, b_hi = (0, nblocks) if blk_range is None else blk_range
    if not t_outer:
        ws.load_block(0, b_lo)
        for b in range(b_lo, b_hi):
            bi = (b - b_lo) % 2
            if b + 1 < b_hi:
                ws.load_block((b + 1 - b_lo) % 2, b + 1)
            for ti in range(nts):
                inner(b, bi, ti)
    else:
        li = 0
        ws.load_block(0, 0)
        for ti in range(nts):
            for b in range(nblocks):
                bi = li % 2
                if not (ti == nts - 1 and b == nblocks - 1):
                    nb = (b + 1) % nblocks
                    ws.load_block((li + 1) % 2, nb)
                inner(b, bi, ti)
                li += 1



def build_l1_ffn(nc, Ttile=512, fctx=None, io=None):
    local = fctx is None
    ctx = make_ctx(nc) if local else fctx
    nc = ctx.nc
    p = ctx.p
    xT_h = iod(ctx, io, "xT", [D, TC], F32, kind="ExternalInput")
    w1t = iod(ctx, io, "w1t", [8, 128, 8, 512], F32, kind="ExternalInput")
    w2t = iod(ctx, io, "w2t", [8, 128, 8, 512], F32, kind="ExternalInput")
    h1t = iod(ctx, io, "h1t", [8, 128, 32, 512], F32, kind="ExternalInput")
    h2t = iod(ctx, io, "h2t", [8, 128, 32, 512], F32, kind="ExternalInput")
    w3t = iod(ctx, io, "w3t", [2, 128, 32, 512], F32, kind="ExternalInput")
    s1 = iod(ctx, io, "s1", [128, 32], F32, kind="ExternalInput")
    s2 = iod(ctx, io, "s2", [128, 32], F32, kind="ExternalInput")
    sh1 = iod(ctx, io, "sh1", [128, 32], F32, kind="ExternalInput")
    sh2 = iod(ctx, io, "sh2", [128, 32], F32, kind="ExternalInput")
    s3 = iod(ctx, io, "s3", [128, 8], F32, kind="ExternalInput")
    yT_h = iod(ctx, io, "yT", [D, TC], F32, kind="ExternalOutput")

    with (ctx.es if local else contextlib.nullcontext()):
        emit_l1_ffn_body(ctx, xT_h, yT_h,
                         dict(w1t=w1t, w2t=w2t, h1t=h1t, h2t=h2t, w3t=w3t,
                              s1=s1, s2=s2, sh1=sh1, sh2=sh2, s3=s3),
                         Ttile)
        p.barrier_all([f"yT@out{i}" for i in range(TC // Ttile)])
        if local:
            p.finalize(nc)
    return ctx


def emit_l1_ffn_body(ctx, xT_h, yT_h, wh, Ttile):
    nc, p = ctx.nc, ctx.p
    nt = TC // Ttile

    xtile = ctx.sbuf("x", [128, NCH, Ttile], F32)
    xln = ctx.sbuf("xln", [128, NCH, Ttile], BF16)
    t1 = ctx.sbuf("t1", [128, 32, Ttile], BF16)
    t2 = ctx.sbuf("t2", [128, 32, Ttile], BF16)
    scl = ctx.sbuf("scl", [128, 32 * 4 + 8], F32)

    ln = LNHelper(ctx, tchunk=Ttile)

    banks = [(ctx.psum(f"pb{i}", [128, 512], F32), f"pb{i}") for i in range(4)]
    stat = (ctx.psum("pstat", [1, Ttile], F32), "pstat")
    stat2 = (ctx.psum("pstat2", [1, Ttile], F32), "pstat2")
    bc0 = (ctx.psum("pbc0", [128, Ttile], F32), "pbc0")
    bc1 = (ctx.psum("pbc1", [128, Ttile], F32), "pbc1")

    def lds(e):
        return e.dma_start(out=scl[:, 0:32], in_=wh["s1"][:, :])
    p.dma("Y", lds, writes=("scl",), tag="lds1")
    for nm, off, w in (("s2", 32, 32), ("sh1", 64, 32), ("sh2", 96, 32), ("s3", 128, 8)):
        def ld(e, nm=nm, off=off, w=w):
            return e.dma_start(out=scl[:, off:off + w], in_=wh[nm][:, :])
        p.dma("Y", ld, writes=("scl",), tag=f"ld{nm}")

    ws1 = WStream(ctx, "w1", wh["w1t"], D, 4096, tag="w1")
    ws2 = WStream(ctx, "w2", wh["w2t"], D, 4096, tag="w2")
    wsh1 = WStream(ctx, "h1", wh["h1t"], 4096, 4096, tag="h1")
    wsh2 = WStream(ctx, "h2", wh["h2t"], 4096, 4096, tag="h2")
    ws3 = WStream(ctx, "w3", wh["w3t"], 4096, D, tag="w3")

    for it in range(nt):
        t0 = it * Ttile
        load_xT(ctx, xT_h, xtile, "x", t0=t0, tlen=Ttile)
        ln.emit(xtile, "x", xln, "xln", stat, stat2, bc0, bc1)

        def evac_w1(oc, ocw, tt0, tlen, ps, ps_key):
            def fn(e, oc=oc, ocw=ocw, tt0=tt0, tlen=tlen, ps=ps):
                return e.activation(t1[0:ocw, oc // 128, tt0:tt0 + tlen],
                                    ps[0:ocw, 0:tlen], AF.Silu,
                                    scale=scl[0:ocw, oc // 128:oc // 128 + 1])
            p.S(fn, reads=(ps_key, "scl"), writes=("t1",), tag="ev_w1")
        emit_proj(ctx, ws1, xln, "xln", Ttile, evac_w1, banks[0:2])

        def evac_w2(oc, ocw, tt0, tlen, ps, ps_key):
            def fn(e, oc=oc, ocw=ocw, tt0=tt0, tlen=tlen, ps=ps):
                return e.scalar_tensor_tensor(
                    t2[0:ocw, oc // 128, tt0:tt0 + tlen],
                    ps[0:ocw, 0:tlen],
                    scl[0:ocw, 32 + oc // 128:32 + oc // 128 + 1],
                    t1[0:ocw, oc // 128, tt0:tt0 + tlen],
                    op0=ALU.mult, op1=ALU.mult)
            p.V(fn, reads=(ps_key, "scl", "t1"), writes=("t2",), tag="ev_w2")
        emit_proj(ctx, ws2, xln, "xln", Ttile, evac_w2, banks[2:4])

        def evac_h1(oc, ocw, tt0, tlen, ps, ps_key):
            def fn(e, oc=oc, ocw=ocw, tt0=tt0, tlen=tlen, ps=ps):
                return e.activation(t1[0:ocw, oc // 128, tt0:tt0 + tlen],
                                    ps[0:ocw, 0:tlen], AF.Gelu,
                                    scale=scl[0:ocw, 64 + oc // 128:64 + oc // 128 + 1])
            p.S(fn, reads=(ps_key, "scl"), writes=("t1",), tag="ev_h1")
        emit_proj(ctx, wsh1, t2, "t2", Ttile, evac_h1, banks[0:2])

        def evac_h2(oc, ocw, tt0, tlen, ps, ps_key):
            def fn(e, oc=oc, ocw=ocw, tt0=tt0, tlen=tlen, ps=ps):
                return e.activation(t2[0:ocw, oc // 128, tt0:tt0 + tlen],
                                    ps[0:ocw, 0:tlen], AF.Gelu,
                                    scale=scl[0:ocw, 96 + oc // 128:96 + oc // 128 + 1])
            p.S(fn, reads=(ps_key, "scl"), writes=("t2",), tag="ev_h2")
        emit_proj(ctx, wsh2, t1, "t1", Ttile, evac_h2, banks[2:4])

        def evac_w3(oc, ocw, tt0, tlen, ps, ps_key):
            def fn(e, oc=oc, ocw=ocw, tt0=tt0, tlen=tlen, ps=ps):
                return e.scalar_tensor_tensor(
                    xtile[0:ocw, oc // 128, tt0:tt0 + tlen],
                    ps[0:ocw, 0:tlen],
                    scl[0:ocw, 128 + oc // 128:128 + oc // 128 + 1],
                    xtile[0:ocw, oc // 128, tt0:tt0 + tlen],
                    op0=ALU.mult, op1=ALU.add)
            p.V(fn, reads=(ps_key, "scl", "x"), writes=("x",), tag="ev_w3")
        emit_proj(ctx, ws3, t2, "t2", Ttile, evac_w3, banks[0:2])

        store_xT(ctx, yT_h, xtile, "x", f"yT@out{it}", t0=t0, tlen=Ttile)



class SubView:

    def __init__(self, base, t0, tlen):
        self.base = base
        self.t0 = t0
        self.tlen = tlen

    def __getitem__(self, sl):
        par, ch, t = sl
        a = self.t0 + (t.start or 0)
        b = self.t0 + (t.stop if t.stop is not None else self.tlen)
        return self.base[par, ch, a:b]





A_HEADS_LO = [0, 1, 2, 3, 8, 9, 10, 11]
A_HEADS_HI = [4, 5, 6, 7, 12, 13, 14, 15]


def qperm_attn_a():
    perm = []
    for j in range(8):
        perm += list(range(A_HEADS_LO[j] * 64, A_HEADS_LO[j] * 64 + 64))
        perm += list(range(A_HEADS_HI[j] * 64, A_HEADS_HI[j] * 64 + 64))
    return np.array(perm)


def rope_tables_attn_a(pos):
    ch = np.arange(64)
    inv_freq = 1.0 / 10000.0 ** ((ch % 32) * 2.0 / 64.0)
    f = pos[None, :] * inv_freq[:, None]
    cos = np.cos(f)
    sin = np.sin(f)
    sgn = np.where(ch < 32, -1.0, 1.0)[:, None]
    sinS = sin * sgn
    cosT = np.tile(cos, (2, 1)).astype(np.float32)
    sinT = np.tile(sinS, (2, 1)).astype(np.float32)
    return np.ascontiguousarray(cosT), np.ascontiguousarray(sinT)


def emit_rope(ctx, ps, ps_key, dst, dst_slice, dst_key, cosT, cosT_key,
              sinT, sinT_key, tsl, rbuf, tmp, nrows=128):
    p = ctx.p
    tlen = tsl.stop - tsl.start
    for g in range(nrows // 32):
        src_g = g + 1 if g % 2 == 0 else g - 1

        def fc(e, g=g, src_g=src_g):
            return e.tensor_copy(rbuf[g * 32:(g + 1) * 32, 0:tlen],
                                 ps[src_g * 32:(src_g + 1) * 32, 0:tlen])
        p.V(fc, reads=(ps_key,), writes=("rope_rbuf",), tag="rope_rot")

    def f1(e):
        return e.tensor_tensor(tmp[0:nrows, 0:tlen], ps[0:nrows, 0:tlen],
                               cosT[0:nrows, tsl], op=ALU.mult)
    p.V(f1, reads=(ps_key, cosT_key), writes=("rope_tmp",), tag="rope_cos")

    def f2(e):
        return e.tensor_tensor(rbuf[0:nrows, 0:tlen], rbuf[0:nrows, 0:tlen],
                               sinT[0:nrows, tsl], op=ALU.mult)
    p.V(f2, reads=("rope_rbuf", sinT_key), writes=("rope_rbuf",), tag="rope_sin")

    def f3(e):
        return e.tensor_tensor(dst[dst_slice], tmp[0:nrows, 0:tlen],
                               rbuf[0:nrows, 0:tlen], op=ALU.add)
    p.V(f3, reads=("rope_tmp", "rope_rbuf"), writes=(dst_key,), tag="rope_add")


def build_attn_a1(nc, fctx=None, io=None):
    local = fctx is None
    ctx = make_ctx(nc) if local else fctx
    nc = ctx.nc
    p = ctx.p
    xT_h = iod(ctx, io, "xT", [D, TC], F32, kind="ExternalInput")
    wq = iod(ctx, io, "wq", [2, 128, 8, 512], F32, kind="ExternalInput")
    wk = iod(ctx, io, "wk", [1, 128, 8, 256], F32, kind="ExternalInput")
    wv = iod(ctx, io, "wv", [128, 8, 256], F32, kind="ExternalInput")
    qcos_h = iod(ctx, io, "qcos", [128, TC], F32, kind="ExternalInput")
    qsin_h = iod(ctx, io, "qsin", [128, TC], F32, kind="ExternalInput")
    qT_o = iod(ctx, io, "qT", [D, TC], BF16, kind="ExternalOutput")
    kT_o = iod(ctx, io, "kT", [256, TC], BF16, kind="ExternalOutput")
    va_o = iod(ctx, io, "va", [TC, 260], BF16, kind="ExternalOutput")

    ctx.wpool_cfg = (8, 512)
    with (ctx.es if local else contextlib.nullcontext()):
        xtile = ctx.sbuf("x", [128, NCH, TC], F32)
        xln = ctx.sbuf("xln", [128, NCH, TC], BF16)
        qT = ctx.sbuf("qT", [128, NCH, TC], BF16)
        kT = ctx.sbuf("kT", [128, 2, TC], BF16)
        va = ctx.sbuf("va", [128, TC // 128, 260], BF16)
        cosT = ctx.sbuf("cosT", [128, TC], F32)
        sinT = ctx.sbuf("sinT", [128, TC], F32)
        rbuf = ctx.sbuf("rbuf", [128, 512], F32)
        tmp = ctx.sbuf("tmp", [128, 512], F32)
        wvs = ctx.sbuf("wvs", [128, 8, 256], BF16)

        ln = LNHelper(ctx, tchunk=512)
        banks = [(ctx.psum(f"pb{i}", [128, 512], F32), f"pb{i}") for i in range(2)]
        stat = (ctx.psum("pstat", [1, 512], F32), "pstat")
        stat2 = (ctx.psum("pstat2", [1, 512], F32), "pstat2")
        bc0 = (ctx.psum("pbc0", [128, 512], F32), "pbc0")
        bc1 = (ctx.psum("pbc1", [128, 512], F32), "pbc1")
        pv = (ctx.psum("ppv", [128, 512], F32), "ppv")

        load_xT(ctx, xT_h, xtile, "x")
        for nm, sb in (("qcos", cosT), ("qsin", sinT)):
            def ld(e, nm=nm, sb=sb):
                return e.dma_start(out=sb[:, :], in_={"qcos": qcos_h, "qsin": qsin_h}[nm][:, :])
            p.dma("Y", ld, writes=(nm,), tag=f"ld_{nm}")

        def ldwv(e):
            return e.dma_start(out=wvs[:, :, :], in_=wv[:, :, :])
        p.dma("G", ldwv, writes=("wvs",), tag="ld_wvs")

        for tch in range(TC // 512):
            t0 = tch * 512
            xv = SubView(xtile, t0, 512)
            ov = SubView(xln, t0, 512)
            ln.emit(xv, "x", ov, "xln", stat, stat2, bc0, bc1)

        ws_q = WStream(ctx, "wq", wq, D, 1024, tag="wq")

        def evac_q(oc, ocw, tt0, tlen, ps, ps_key):
            j = oc // 128
            emit_rope(ctx, ps, ps_key, qT, (slice(0, 128), j, slice(tt0, tt0 + tlen)),
                      "qT", cosT, "qcos", sinT, "qsin", slice(tt0, tt0 + tlen),
                      rbuf, tmp)
        emit_proj(ctx, ws_q, xln, "xln", TC, evac_q, banks)

        ws_k = WStream(ctx, "wk", wk, D, 256, blk=256, tag="wk")

        def evac_k(oc, ocw, tt0, tlen, ps, ps_key):
            j = oc // 128
            emit_rope(ctx, ps, ps_key, kT, (slice(0, 128), j, slice(tt0, tt0 + tlen)),
                      "kT", cosT, "qcos", sinT, "qsin", slice(tt0, tt0 + tlen),
                      rbuf, tmp)
        emit_proj(ctx, ws_k, xln, "xln", TC, evac_k, banks)

        ones = get_ones(ctx)
        for tq in range(TC // 128):
            pvp, pv_key = pv
            for e_ in range(NCH):
                def fmm(e, e_=e_, tq=tq, pvp=pvp):
                    return e.matmul(pvp[:, 0:256], xln[:, e_, tq * 128:(tq + 1) * 128],
                                    wvs[:, e_, :], start=(e_ == 0), stop=(e_ == NCH - 1))
                p.T(fmm, reads=("xln", "wvs"), writes=(pv_key,), tag="mm_v")
            for h in range(4):
                def fcp(e, h=h, tq=tq, pvp=pvp):
                    return e.tensor_copy(va[:, tq, h * 65:h * 65 + 64],
                                         pvp[:, h * 64:(h + 1) * 64])
                p.V(fcp, reads=(pv_key,), writes=("va",), tag="v_cp")

        def fones(e):
            return e.memset(va[:, :, 64:260:65], 1.0)
        p.G(fones, writes=("va",), tag="va_ones")

        store_xT(ctx, qT_o, qT, "qT", "qT@out", tlen=TC)
        def stk(e):
            return e.dma_start(out=kT_o[:, :].rearrange("(c p) t -> p c t", p=128),
                               in_=kT[:, :, :])
        p.dma("Y", stk, reads=("kT",), writes=("kT@out",), tag="st_kT")

        def stv(e):
            return e.dma_start(out=va_o[:, :].rearrange("(c p) v -> p c v", p=128),
                               in_=va[:, :, :])
        p.dma("Y", stv, reads=("va",), writes=("va@out",), tag="st_va")
        p.barrier_all(["qT@out", "kT@out", "va@out"])
        if local:
            p.finalize(nc)
    return ctx


def build_attn_a2(nc, NB=8192, fctx=None, io=None):
    local = fctx is None
    ctx = make_ctx(nc) if local else fctx
    nc = ctx.nc
    p = ctx.p
    xT_h = iod(ctx, io, "xT", [D, TC], F32, kind="ExternalInput")
    qT_h = iod(ctx, io, "qT", [D, TC], BF16, kind="ExternalInput")
    kT_h = iod(ctx, io, "kT", [NB // TC, 256, TC], BF16, kind="ExternalInput")
    va_h = iod(ctx, io, "va", [NB // TC, TC, 260], BF16, kind="ExternalInput")
    wo = iod(ctx, io, "wo", [2, 128, 8, 512], F32, kind="ExternalInput")
    bo = iod(ctx, io, "bo", [1, D], F32, kind="ExternalInput")
    yT_h = iod(ctx, io, "yT", [D, TC], F32, kind="ExternalOutput")


    ctx.wpool_cfg = (8, 512)
    with (ctx.es if local else contextlib.nullcontext()):
        qT = ctx.sbuf("qT", [128, NCH, TC], BF16)
        kT = ctx.sbuf("kT", [128, 2, NB], BF16)
        va = ctx.sbuf("va", [128, NB // 128, 260], BF16)
        OT = ctx.sbuf("OT", [128, NCH, TC], BF16)
        pT = [ctx.sbuf(f"pT{i}", [128, 512], BF16) for i in range(3)]
        lrow = ctx.sbuf("lrow", [1, 512], F32)
        lbc = ctx.sbuf("lbc", [64, 512], F32)
        xtile = ctx.sbuf("x", [128, NCH, 512], F32)
        bos = ctx.sbuf("bos", [1, D], F32)
        ones = get_ones(ctx)

        def ldq(e):
            return e.dma_start(out=qT[:, :, :], in_=qT_h[:, :].rearrange("(c p) t -> p c t", p=128))
        p.dma("Y", ldq, writes=("qT",), tag="ld_qT")

        for r in range(NB // TC):
            def ldk(e, r=r):
                return e.dma_start(out=kT[:, :, r * TC:(r + 1) * TC],
                                   in_=kT_h[r, :, :].rearrange("(c p) t -> p c t", p=128))
            p.dma("Y", ldk, writes=("kT",), tag="ld_kT")

            def ldv(e, r=r):
                return e.dma_start(out=va[:, r * (TC // 128):(r + 1) * (TC // 128), :],
                                   in_=va_h[r, :, :].rearrange("(c p) v -> p c v", p=128))
            p.dma("Y", ldv, writes=("va",), tag="ld_va")

        def ldb(e):
            return e.dma_start(out=bos[:, :], in_=bo[:, :])
        p.dma("Y", ldb, writes=("bos",), tag="ld_bo")

        pst = [(ctx.psum(f"pst{i}", [128, 512], F32), f"pst{i}") for i in range(3)]
        pot = [(ctx.psum(f"pot{i}", [128, 512], F32), f"pot{i}") for i in range(2)]
        pbc = (ctx.psum("pbc", [64, 512], F32), "pbc")
        banks = [(ctx.psum(f"pb{i}", [128, 512], F32), f"pb{i}") for i in range(2)]

        scale_ap = ctx.cf(64.0 ** -0.5)
        nkt = NB // 128

        for qc in range(TC // 512):
            qsl = slice(qc * 512, (qc + 1) * 512)
            for j in range(NCH):
                for half in range(2):
                    base = half * 64
                    h = (A_HEADS_LO[j] if half == 0 else A_HEADS_HI[j])
                    kh = h // 4
                    kch, kbase = kh // 2, (kh % 2) * 64
                    po, po_key = pot[(j * 2 + half) % 2]

                    def emit_st(kt, kch=kch, kbase=kbase, base=base, qsl=qsl,
                                j=j):
                        ps, ps_key = pst[kt % 3]

                        def fst(e, kch=kch, kbase=kbase, kt=kt, base=base,
                                qsl=qsl, ps=ps, j=j):
                            return e.matmul(
                                ps[:, 0:512],
                                kT[kbase:kbase + 64, kch, kt * 128:(kt + 1) * 128],
                                qT[base:base + 64, j, qsl],
                                start=True, stop=True)
                        p.T(fst, reads=("kT", "qT"), writes=(ps_key,), tag="mm_st")

                    emit_st(0)
                    emit_st(1)
                    for kt in range(nkt):
                        if kt + 2 < nkt:
                            emit_st(kt + 2)
                        ps, ps_key = pst[kt % 3]
                        pb = pT[kt % 3]

                        def fexp(e, ps=ps, pb=pb):
                            return e.activation(pb[:, :], ps[:, 0:512], AF.Exp,
                                                scale=scale_ap)
                        p.S(fexp, reads=(ps_key, "cft"), writes=(f"pT{kt % 3}",),
                            tag="exp")

                        def fpv(e, kt=kt, pb=pb, po=po, kh=kh):
                            return e.matmul(po[0:65, 0:512],
                                            va[:, kt, kh * 65:kh * 65 + 65],
                                            pb[:, :],
                                            start=(kt == 0), stop=(kt == nkt - 1))
                        p.T(fpv, reads=("va", f"pT{kt % 3}"), writes=(po_key,),
                            tag="mm_pv")
                    def frc(e, po=po):
                        return e.reciprocal(lrow[0:1, :], po[64:65, 0:512])
                    p.V(frc, reads=(po_key,), writes=("lrow",), tag="recip")

                    def fbc(e):
                        return e.matmul(pbc[0][0:64, 0:512], ones[0:1, 0:64],
                                        lrow[0:1, :], start=True, stop=True)
                    p.T(fbc, reads=("lrow", "ones"), writes=(pbc[1],), tag="mm_bc")

                    def fcl(e):
                        return e.copy(lbc[:, :], pbc[0][0:64, 0:512])
                    p.S(fcl, reads=(pbc[1],), writes=("lbc",), tag="cp_lbc")

                    def fdv(e, po=po, base=base, j=j, qsl=qsl):
                        return e.tensor_tensor(OT[base:base + 64, j, qsl],
                                               po[0:64, 0:512], lbc[:, :],
                                               op=ALU.mult)
                    p.V(fdv, reads=(po_key, "lbc"), writes=("OT",), tag="div")

        ws_o = WStream(ctx, "wo", wo, D, D, tag="wo")

        def extra_bias(e, oc, ocw, t0, tlen, ps):
            return e.matmul(ps[0:ocw, 0:tlen], bos[0:1, oc:oc + ocw],
                            ones[0:1, 0:tlen], start=False, stop=True)
        extra_bias.reads = ("bos", "ones")

        tload = {}

        def evac_o(oc, ocw, t0, tlen, ps, ps_key):
            j = oc // 128
            if t0 not in tload:
                load_xT(ctx, xT_h, xtile, "xt", t0=t0, tlen=512)
                tload[t0] = True

            def fn(e, j=j, tlen=tlen, ps=ps, ocw=ocw):
                return e.tensor_tensor(xtile[0:ocw, j, 0:tlen], ps[0:ocw, 0:tlen],
                                       xtile[0:ocw, j, 0:tlen], op=ALU.add)
            p.V(fn, reads=(ps_key, "xt"), writes=("xt",), tag="resid")
            if j == NCH - 1:
                store_xT(ctx, yT_h, xtile, "xt", f"yT@o{t0}", t0=t0, tlen=512)
        emit_proj(ctx, ws_o, OT, "OT", TC, evac_o, banks, extra_mm=extra_bias,
                  t_outer=True)

        p.barrier_all([f"yT@o{t0}" for t0 in range(0, TC, 512)])
        if local:
            p.finalize(nc)
    return ctx



def build_attn_b1(nc, fctx=None, io=None):
    local = fctx is None
    ctx = make_ctx(nc) if local else fctx
    nc = ctx.nc
    ctx.wpool_cfg = (8, 512)
    p = ctx.p
    xT_h = iod(ctx, io, "xT", [D, TC], F32, kind="ExternalInput")
    wq = iod(ctx, io, "wq", [2, 128, 8, 512], F32, kind="ExternalInput")
    wk = iod(ctx, io, "wk", [2, 128, 8, 512], F32, kind="ExternalInput")
    wkh = iod(ctx, io, "wkh", [128, 8, 1024], F32, kind="ExternalInput")
    qT_o = iod(ctx, io, "qT", [64, 16, TC], BF16, kind="ExternalOutput")
    kT_o = iod(ctx, io, "kT", [D, TC], BF16, kind="ExternalOutput")
    va_o = iod(ctx, io, "va", [16, TC, 65], BF16, kind="ExternalOutput")

    with (ctx.es if local else contextlib.nullcontext()):
        xtile = ctx.sbuf("x", [128, NCH, 512], F32)
        xln = ctx.sbuf("xln", [128, NCH, TC], BF16)
        qT = ctx.sbuf("qT", [64, 16, TC], BF16)
        kst = [ctx.sbuf(f"kst{i}", [128, 512], BF16) for i in range(2)]
        va = ctx.sbuf("va", [128, TC // 128, 1040], BF16)
        wkhs = ctx.sbuf("wkhs", [128, 8, 1024], BF16)

        ln = LNHelper(ctx, tchunk=512)
        banks = [(ctx.psum(f"pb{i}", [128, 512], F32), f"pb{i}") for i in range(2)]
        stat = (ctx.psum("pstat", [1, 512], F32), "pstat")
        stat2 = (ctx.psum("pstat2", [1, 512], F32), "pstat2")
        bc0 = (ctx.psum("pbc0", [128, 512], F32), "pbc0")
        bc1 = (ctx.psum("pbc1", [128, 512], F32), "pbc1")
        pv = (ctx.psum("ppv", [128, 512], F32), "ppv")

        def ldwkh(e):
            return e.dma_start(out=wkhs[:, :, :], in_=wkh[:, :, :])
        p.dma("G", ldwkh, writes=("wkhs",), tag="ld_wkh")

        for tch in range(TC // 512):
            t0 = tch * 512
            load_xT(ctx, xT_h, xtile, "x", t0=t0, tlen=512)
            ln.emit(xtile, "x", SubView(xln, t0, 512), "xln",
                    stat, stat2, bc0, bc1)

        ws_q = WStream(ctx, "wq", wq, D, 1024, tag="wq")

        def evac_q(oc, ocw, tt0, tlen, ps, ps_key):
            j = oc // 128
            for half in range(2):
                def fc(e, j=j, half=half, tt0=tt0, tlen=tlen, ps=ps):
                    return e.tensor_copy(qT[0:64, 2 * j + half, tt0:tt0 + tlen],
                                         ps[half * 64:half * 64 + 64, 0:tlen])
                p.V(fc, reads=(ps_key,), writes=("qT",), tag="ev_q")
        emit_proj(ctx, ws_q, xln, "xln", TC, evac_q, banks)

        ws_k = WStream(ctx, "wk", wk, D, 1024, tag="wk")
        kc = [0]

        def evac_k(oc, ocw, tt0, tlen, ps, ps_key):
            j = oc // 128
            si = kc[0] % 2
            kc[0] += 1

            def fc(e, si=si, tlen=tlen, ps=ps, ocw=ocw):
                return e.tensor_copy(kst[si][0:ocw, 0:tlen], ps[0:ocw, 0:tlen])
            p.V(fc, reads=(ps_key,), writes=(f"kst{si}",), tag="ev_k")

            def fs(e, si=si, j=j, tt0=tt0, tlen=tlen, ocw=ocw):
                return e.dma_start(out=kT_o[j * 128:j * 128 + ocw, tt0:tt0 + tlen],
                                   in_=kst[si][0:ocw, 0:tlen])
            p.dma("Y", fs, reads=(f"kst{si}",), writes=(f"kT@out{j}_{tt0}",),
                  tag="st_k")
        emit_proj(ctx, ws_k, xln, "xln", TC, evac_k, banks)

        pvp, pv_key = pv
        for tq in range(TC // 128):
            for hg in range(2):
                for e_ in range(NCH):
                    def fmm(e, e_=e_, tq=tq, hg=hg):
                        return e.matmul(pvp[:, 0:512],
                                        xln[:, e_, tq * 128:(tq + 1) * 128],
                                        wkhs[:, e_, hg * 512:(hg + 1) * 512],
                                        start=(e_ == 0), stop=(e_ == NCH - 1))
                    p.T(fmm, reads=("xln", "wkhs"), writes=(pv_key,), tag="mm_v")
                for hh in range(8):
                    h = hg * 8 + hh

                    def fcp(e, h=h, hh=hh, tq=tq):
                        return e.tensor_copy(va[:, tq, h * 65:h * 65 + 64],
                                             pvp[:, hh * 64:(hh + 1) * 64])
                    p.V(fcp, reads=(pv_key,), writes=("va",), tag="v_cp")

        def fones(e):
            return e.memset(va[:, :, 64:1040:65], 1.0)
        p.G(fones, writes=("va",), tag="va_ones")

        def stq(e):
            return e.dma_start(out=qT_o[:, :, :], in_=qT[:, :, :])
        p.dma("Y", stq, reads=("qT",), writes=("qT@out",), tag="st_qT")

        for h in range(16):
            def stv(e, h=h):
                return e.dma_start(out=va_o[h, :, :].rearrange("(q p) v -> p q v", p=128),
                                   in_=va[:, :, h * 65:(h + 1) * 65])
            p.dma("Y", stv, reads=("va",), writes=(f"va@out{h}",), tag="st_va")
        p.barrier_all([f"va@out{h}" for h in range(16)] + ["qT@out"]
                      + [f"kT@out{j}_{t0}" for j in range(8) for t0 in range(0, TC, 512)])
        if local:
            p.finalize(nc)
    return ctx


def build_attn_b2(nc, NB=8192, fctx=None, io=None):
    local = fctx is None
    ctx = make_ctx(nc) if local else fctx
    nc = ctx.nc
    ctx.wpool_cfg = (8, 512)
    p = ctx.p
    xT_h = iod(ctx, io, "xT", [D, TC], F32, kind="ExternalInput")
    qT_h = iod(ctx, io, "qT", [64, 16, TC], BF16, kind="ExternalInput")
    kT_h = iod(ctx, io, "kT", [NB // TC, D, TC], BF16, kind="ExternalInput")
    va_h = iod(ctx, io, "va", [NB // TC, 16, TC, 65], BF16, kind="ExternalInput")
    wo = iod(ctx, io, "wo", [2, 128, 8, 512], F32, kind="ExternalInput")
    bo = iod(ctx, io, "bo", [1, D], F32, kind="ExternalInput")
    yT_h = iod(ctx, io, "yT", [D, TC], F32, kind="ExternalOutput")

    with (ctx.es if local else contextlib.nullcontext()):
        qT = ctx.sbuf("qT", [64, 16, TC], BF16)
        kth = [ctx.sbuf(f"kth{i}", [64, NB], BF16) for i in range(2)]
        vah = [ctx.sbuf(f"vah{i}", [128, NB // 128, 65], BF16) for i in range(2)]
        OT = ctx.sbuf("OT", [128, NCH, TC], BF16)
        pT = [ctx.sbuf(f"pT{i}", [128, 512], BF16) for i in range(3)]
        lrow = ctx.sbuf("lrow", [1, 512], F32)
        lbc = ctx.sbuf("lbc", [64, 512], F32)
        xtile = ctx.sbuf("x", [128, NCH, 512], F32)
        bos = ctx.sbuf("bos", [1, D], F32)
        ones = get_ones(ctx)

        def ldq(e):
            return e.dma_start(out=qT[:, :, :], in_=qT_h[:, :, :])
        p.dma("Y", ldq, writes=("qT",), tag="ld_qT")

        def ldb(e):
            return e.dma_start(out=bos[:, :], in_=bo[:, :])
        p.dma("Y", ldb, writes=("bos",), tag="ld_bo")

        pst = [(ctx.psum(f"pst{i}", [128, 512], F32), f"pst{i}") for i in range(3)]
        pot = [(ctx.psum(f"pot{i}", [128, 512], F32), f"pot{i}") for i in range(2)]
        pbc = (ctx.psum("pbc", [64, 512], F32), "pbc")
        banks = [(ctx.psum(f"pb{i}", [128, 512], F32), f"pb{i}") for i in range(2)]

        scale_ap = ctx.cf(1.0 / 32.0)
        nkt = NB // 128

        def load_head(h):
            bi = h % 2
            for r in range(NB // TC):
                def fk(e, h=h, bi=bi, r=r):
                    return e.dma_start(out=kth[bi][:, r * TC:(r + 1) * TC],
                                       in_=kT_h[r, h * 64:(h + 1) * 64, :])
                p.dma("Y", fk, reads=(), writes=(f"kth{bi}",), tag="ld_kth",
                      role=f"kv{bi}")

                def fv(e, h=h, bi=bi, r=r):
                    return e.dma_start(
                        out=vah[bi][:, r * (TC // 128):(r + 1) * (TC // 128), :],
                        in_=va_h[r, h, :, :].rearrange("(c p) v -> p c v", p=128))
                p.dma("Y", fv, reads=(), writes=(f"vah{bi}",), tag="ld_vah",
                      role=f"kv{bi}")

        load_head(0)
        for h in range(16):
            bi = h % 2
            if h + 1 < 16:
                load_head(h + 1)
            for qc in range(TC // 512):
                qsl = slice(qc * 512, (qc + 1) * 512)
                po, po_key = pot[qc % 2]

                def emit_st(kt, h=h, bi=bi, qsl=qsl):
                    ps, ps_key = pst[kt % 3]

                    def fst(e, h=h, bi=bi, kt=kt, qsl=qsl, ps=ps):
                        return e.matmul(ps[:, 0:512],
                                        kth[bi][0:64, kt * 128:(kt + 1) * 128],
                                        qT[0:64, h, qsl], start=True, stop=True)
                    p.T(fst, reads=(f"kth{bi}", "qT"), writes=(ps_key,), tag="mm_st")

                emit_st(0)
                emit_st(1)
                for kt in range(nkt):
                    if kt + 2 < nkt:
                        emit_st(kt + 2)
                    ps, ps_key = pst[kt % 3]
                    pb = pT[kt % 3]

                    def fexp(e, ps=ps, pb=pb):
                        return e.activation(pb[:, :], ps[:, 0:512], AF.Exp,
                                            scale=scale_ap)
                    p.S(fexp, reads=(ps_key, "cft"), writes=(f"pT{kt % 3}",), tag="exp")

                    def fpv(e, kt=kt, pb=pb, po=po, bi=bi):
                        return e.matmul(po[0:65, 0:512], vah[bi][:, kt, :], pb[:, :],
                                        start=(kt == 0), stop=(kt == nkt - 1))
                    p.T(fpv, reads=(f"vah{bi}", f"pT{kt % 3}"), writes=(po_key,),
                        tag="mm_pv")

                def frc(e, po=po):
                    return e.reciprocal(lrow[0:1, :], po[64:65, 0:512])
                p.V(frc, reads=(po_key,), writes=("lrow",), tag="recip")

                def fbc(e):
                    return e.matmul(pbc[0][0:64, 0:512], ones[0:1, 0:64],
                                    lrow[0:1, :], start=True, stop=True)
                p.T(fbc, reads=("lrow", "ones"), writes=(pbc[1],), tag="mm_bc")

                def fcl(e):
                    return e.copy(lbc[:, :], pbc[0][0:64, 0:512])
                p.S(fcl, reads=(pbc[1],), writes=("lbc",), tag="cp_lbc")

                def fdv(e, po=po, h=h, qsl=qsl):
                    return e.tensor_tensor(OT[(h % 2) * 64:(h % 2) * 64 + 64,
                                              h // 2, qsl],
                                           po[0:64, 0:512], lbc[:, :], op=ALU.mult)
                p.V(fdv, reads=(po_key, "lbc"), writes=("OT",), tag="div")

        ws_o = WStream(ctx, "wo", wo, D, D, tag="wo")

        def extra_bias(e, oc, ocw, t0, tlen, ps):
            return e.matmul(ps[0:ocw, 0:tlen], bos[0:1, oc:oc + ocw],
                            ones[0:1, 0:tlen], start=False, stop=True)
        extra_bias.reads = ("bos", "ones")

        tload = {}

        def evac_o(oc, ocw, t0, tlen, ps, ps_key):
            j = oc // 128
            if t0 not in tload:
                load_xT(ctx, xT_h, xtile, "xt", t0=t0, tlen=512)
                tload[t0] = True

            def fn(e, j=j, tlen=tlen, ps=ps, ocw=ocw):
                return e.tensor_tensor(xtile[0:ocw, j, 0:tlen], ps[0:ocw, 0:tlen],
                                       xtile[0:ocw, j, 0:tlen], op=ALU.add)
            p.V(fn, reads=(ps_key, "xt"), writes=("xt",), tag="resid")
            if j == NCH - 1:
                store_xT(ctx, yT_h, xtile, "xt", f"yT@o{t0}", t0=t0, tlen=512)
        emit_proj(ctx, ws_o, OT, "OT", TC, evac_o, banks, extra_mm=extra_bias,
                  t_outer=True)

        p.barrier_all([f"yT@o{t0}" for t0 in range(0, TC, 512)])
        if local:
            p.finalize(nc)
    return ctx


def attn_b_host(inputs, xT_full):
    g = lambda k: np.ascontiguousarray(np.asarray(inputs[k], np.float32))
    wq = pack_weight(g("L0_attn_b_Wq"))
    wkm = g("L0_attn_b_Wk")
    wk = pack_weight(wkm)
    wkh = np.ascontiguousarray(wkm.reshape(8, 128, 1024).transpose(1, 0, 2))
    wo = pack_weight(g("L0_attn_b_Wo"))
    bo = g("L0_attn_b_bo").reshape(1, D)

    nc1 = bass.Bass("TRN2", target_bir_lowering=False)
    c1 = build_attn_b1(nc1)
    maps = []
    for c in range(NCORES):
        m = dict(xT=_shard_cols(xT_full, c), wq=wq, wk=wk, wkh=wkh)
        m.update(c1.consts)
        if hasattr(c1, "cf_map"):
            m["cf_consts"] = c1.cf_array()
        maps.append(m)
    r1 = run_bass_kernel_spmd(nc1, maps, core_ids=list(range(NCORES)))

    kT_full = [np.stack([r1.results[b * 4 + i]["kT"] for i in range(4)])
               for b in range(2)]
    va_full = [np.stack([r1.results[b * 4 + i]["va"] for i in range(4)])
               for b in range(2)]

    nc2 = bass.Bass("TRN2", target_bir_lowering=False)
    c2 = build_attn_b2(nc2)
    maps2 = []
    for c in range(NCORES):
        b = c // 4
        m = dict(xT=_shard_cols(xT_full, c), qT=r1.results[c]["qT"],
                 kT=kT_full[b], va=va_full[b], wo=wo, bo=bo)
        m.update(c2.consts)
        m["cf_consts"] = c2.cf_array()
        maps2.append(m)
    r2 = run_bass_kernel_spmd(nc2, maps2, core_ids=list(range(NCORES)))
    return np.concatenate([r2.results[c]["yT"] for c in range(NCORES)], axis=1)





FXE = 3200
FKE = 3072
FQE = 2560
FOWN = 1152


class ShiftView:

    def __init__(self, base, nch_split=4):
        self.base = base
        self.split = nch_split

    def __getitem__(self, sl):
        par, c, t = sl
        off = 1 if c < self.split else 0
        a = (t.start or 0) + off
        b = t.stop + off
        return self.base[par, c, a:b]


def rope_tables_xpos(pos, invert):
    DH, WINDOW, N = 64, 512, 8192
    ch = np.arange(64)
    half = (ch % 32).astype(np.float64)
    inv_freq = 1.0 / 10000.0 ** (half * 2.0 / DH)
    posc = np.clip(pos, 0, None).astype(np.float64)
    f = posc[None, :] * inv_freq[:, None]
    base = (2.0 * half + 0.4 * DH) / (1.4 * DH)
    power = (posc - N // 2) / float(WINDOW // 2)
    scale = base[:, None] ** (power[None, :] * (-1.0 if invert else 1.0))
    cos = np.cos(f) * scale
    sin = np.sin(f) * scale
    sgn = np.where(ch < 32, -1.0, 1.0)[:, None]
    cosT = np.tile(cos, (2, 1)) * 1.0
    sinT = np.tile(sin, (2, 1)) * np.concatenate([sgn[:64]] * 2, 0)
    return (np.ascontiguousarray(cosT.astype(np.float32)),
            np.ascontiguousarray(sinT.astype(np.float32)))


def local_masks():
    M = np.zeros((128, 8, 512), np.float32)
    for di, d in enumerate(range(-4, 4)):
        kp = np.arange(128)[:, None]
        qp = np.arange(512)[None, :]
        imj = qp - (d * 128 + kp)
        M[:, di, :] = np.where((imj >= 0) & (imj <= 512), 0.0, -30000.0)
    return np.ascontiguousarray(M.astype(BF))


def build_l0ffn_attn(nc, do_ln, fctx=None, io=None):
    local = fctx is None
    ctx = make_ctx(nc) if local else fctx
    nc = ctx.nc
    ctx.wpool_cfg = (8, 512)
    p = ctx.p
    xin_h = iod(ctx, io, "xin", [D, FXE], F32, kind="ExternalInput")
    wk_h = iod(ctx, io, "wkp", [1, 128, 8, 512], F32, kind="ExternalInput")
    wq_h = iod(ctx, io, "wqp", [1, 128, 8, 512], F32, kind="ExternalInput")
    wvt_h = iod(ctx, io, "wvt", [128, 8, 512], F32, kind="ExternalInput")
    wo_h = iod(ctx, io, "wop", [2, 128, 4, 512], F32, kind="ExternalInput")
    qs_h = iod(ctx, io, "qs", [128, 2], F32, kind="ExternalInput")
    qcos_h = iod(ctx, io, "qcos", [128, FQE], BF16, kind="ExternalInput")
    qsin_h = iod(ctx, io, "qsin", [128, FQE], BF16, kind="ExternalInput")
    kcos_h = iod(ctx, io, "kcos", [128, FKE], BF16, kind="ExternalInput")
    ksin_h = iod(ctx, io, "ksin", [128, FKE], BF16, kind="ExternalInput")
    msk_h = iod(ctx, io, "msk", [128, 8, 512], BF16, kind="ExternalInput")
    mv_h = iod(ctx, io, "mv", [128, 12], F32, kind="ExternalInput")
    xp_h = iod(ctx, io, "xp", [D, 2560], F32,
                          kind="ExternalOutput" if do_ln else "Internal")
    x1_h = iod(ctx, io, "x1", [D, 2560], F32, kind="ExternalOutput")
    if do_ln:
        xlnx_h = iod(ctx, io, "xlnx", [D, FXE], BF16, kind="Internal")

    with (ctx.es if local else contextlib.nullcontext()):
        B = [(ctx.psum(f"ps{i}", [128, 512], F32), f"ps{i}") for i in range(8)]
        xsm = ctx.sbuf("xsm", [128, NCH, 520], F32)
        xlt = [ctx.sbuf("xlt0", [128, NCH, 520], BF16)]
        kT = ctx.sbuf("kT", [128, 4, FKE], BF16)
        qT = ctx.sbuf("qT", [128, 4, FQE], BF16)
        va = ctx.sbuf("va", [128, FKE // 128, 520], BF16)
        OT = ctx.sbuf("OT", [128, 4, FQE], BF16)
        tmpA_s = [ctx.sbuf(f"tmpA{i}", [128, 512], F32) for i in range(2)]
        tmpB_s = [ctx.sbuf("tmpB0", [128, 512], F32)] * 2
        rbuf_s = [ctx.sbuf("rbufX", [128, 512], F32)] * 2
        rn_s = [ctx.sbuf("rn0", [1, 1024], F32)] * 2
        nr_cnt = [0]
        tcos = ctx.sbuf("tcos", [128, FKE], BF16)
        tsin = ctx.sbuf("tsin", [128, FKE], BF16)
        qcos = kcos = tcos
        qsin = ksin = tsin
        msk = ctx.sbuf("msk", [128, 8, 512], BF16)
        mv = ctx.sbuf("mv", [128, 12], F32)
        qss = ctx.sbuf("qss", [128, 2], F32)
        pT = [ctx.sbuf(f"pTl{i}", [128, 512], BF16) for i in range(3)]
        lrow = ctx.sbuf("lrow", [1, 512], F32)
        lbc = ctx.sbuf("lbc", [64, 512], F32)
        xres = xsm
        ones = get_ones(ctx)
        ln = LNHelper(ctx, tchunk=512) if do_ln else None

        for nm, sb in (("mv", mv), ("qss", qss)):
            def ld(e, nm=nm, sb=sb):
                h = {"mv": mv_h, "qss": qs_h}[nm]
                return e.dma_start(out=sb[:, :], in_=h[:, :])
            p.dma("Y", ld, writes=(nm,), tag=f"ld_{nm}")

        def load_tables(which):
            for nm, sb, h, w in ((which + "cos", tcos,
                                  kcos_h if which == "k" else qcos_h,
                                  FKE if which == "k" else FQE),
                                 (which + "sin", tsin,
                                  ksin_h if which == "k" else qsin_h,
                                  FKE if which == "k" else FQE)):
                def ld(e, sb=sb, h=h, w=w):
                    return e.dma_start(out=sb[:, 0:w], in_=h[:, :])
                p.dma("Y", ld, writes=("tcos" if "cos" in nm else "tsin",),
                      tag=f"ld_{nm}")

        def ldm(e):
            return e.dma_start(out=msk[:, :, :], in_=msk_h[:, :, :])
        p.dma("Y", ldm, writes=("msk",), tag="ld_msk")

        if do_ln:
            nchk = (FXE + 511) // 512
            xpt = ctx.sbuf("xpt", [128, NCH, 512], F32)
            for ic in range(nchk):
                t0 = ic * 512
                tl = min(512, FXE - t0)
                load_xT(ctx, xin_h, xsm, "xsm", t0=t0, tlen=tl)
                w0 = max(0, 640 - t0)
                w1 = max(w0, min(tl, FXE - t0))
                pass

                def out2fn(c, a, b, xpt=xpt, w0=w0):
                    return xpt[:, c, a - w0:b - w0]
                ln.emit(xsm, "xsm", xlt[0], "xlt0", B[0], B[1], B[2], B[3],
                        tlen=tl,
                        out2=((out2fn, (w0, w1)) if w1 > w0 else None),
                        out2key="xpt")
                st = xlnx_h[:, t0:t0 + tl].rearrange("(c p) t -> p c t", p=128)

                def fst(e, st=st, tl=tl):
                    return e.dma_start(out=st, in_=xlt[0][:, :, 0:tl])
                p.dma("Y", fst, reads=("xlt0",), writes=(f"xlnx{ic}",), tag="st_xln")
                if w1 > w0:
                    dst = xp_h[:, t0 + w0 - 640:t0 + w1 - 640]

                    def fsp(e, dst=dst, xpt=xpt, w0=w0, w1=w1):
                        return e.dma_start(
                            out=dst.rearrange("(c p) t -> p c t", p=128),
                            in_=xpt[:, :, 0:w1 - w0])
                    p.dma("Y", fsp, reads=("xpt",), writes=(f"xp{ic}", "xpt"), tag="st_xp")
            xsrc_h = xlnx_h
            src_keys = [f"xlnx{i}" for i in range(nchk)]
        else:
            xsrc_h = xin_h
            src_keys = []

        def load_xtile(bi, c0, width, key):
            ap = xsrc_h[:, c0 - 1:c0 + width].rearrange("(c p) t -> p c t", p=128)

            def fn(e, ap=ap, bi=bi, width=width):
                return e.dma_start(out=xlt[bi][:, :, 0:width + 1], in_=ap)
            p.dma("G" if not do_ln else "Y", fn, reads=tuple(src_keys),
                  writes=(key,), tag="ld_xt")

        load_tables("k")
        ws_k = WStream(ctx, "wk", wk_h, D, 512, tag="wk")
        wvs = ctx.sbuf("wvs", [128, 8, 512], BF16)

        def ldwv(e):
            return e.dma_start(out=wvs[:, :, :], in_=wvt_h[:, :, :])
        p.dma("G", ldwv, writes=("wvs",), tag="ld_wvs")

        def norm_rope_evac(dst, dst_key, cosT, cosK, sinT, sinK, scol,
                           oc, tt0, tlen, ps, ps_key, tabsl):
            j = oc // 128
            si = nr_cnt[0] % 2
            nr_cnt[0] += 1
            tmpA, tmpB, rbuf, rn = tmpA_s[si], tmpB_s[si], rbuf_s[si], rn_s[si]
            kA, kB, kR, kN = f"tmpA{si}", "tmpB0", "rbufX", "rn0"
            def fsq(e, ps=ps, tlen=tlen):
                return e.activation(tmpA[:, 0:tlen], ps[0:128, 0:tlen], AF.Square)
            p.S(fsq, reads=(ps_key,), writes=(kA,), tag="nr_sq")
            prs = [B[6], B[5]] if si == 0 else [B[7], B[6]]
            for half in range(2):
                pr, pr_key = prs[half]

                def fmm(e, half=half, tlen=tlen, pr=pr):
                    return e.matmul(pr[0:1, 0:tlen],
                                    ones[half * 64:half * 64 + 64, 0:1],
                                    tmpA[half * 64:half * 64 + 64, 0:tlen],
                                    start=True, stop=True)
                p.T(fmm, reads=(kA, "ones"), writes=(pr_key,), tag="nr_mm")
            for half in range(2):
                o_ = half * 512
                pr, pr_key = prs[half]

                def fmx(e, o_=o_, tlen=tlen, pr=pr):
                    return e.tensor_scalar_max(rn[0:1, o_:o_ + tlen],
                                               pr[0:1, 0:tlen], 1e-24)
                p.V(fmx, reads=(pr_key,), writes=(kN,), tag="nr_max")

                def fsqr(e, o_=o_, tlen=tlen):
                    return e.activation(rn[0:1, o_:o_ + tlen], rn[0:1, o_:o_ + tlen],
                                        AF.Sqrt)
                p.S(fsqr, reads=(kN,), writes=(kN,), tag="nr_sqrt")

                def frec(e, o_=o_, tlen=tlen):
                    return e.reciprocal(rn[0:1, o_:o_ + tlen], rn[0:1, o_:o_ + tlen])
                p.V(frec, reads=(kN,), writes=(kN,), tag="nr_rec")
            pb_, pb_key = B[7] if si == 0 else B[5]
            for half in range(2):
                def fbc(e, half=half, tlen=tlen, pb_=pb_):
                    return e.matmul(pb_[half * 64:half * 64 + 64, 0:tlen],
                                    ones[0:1, 0:64],
                                    rn[0:1, half * 512:half * 512 + tlen],
                                    start=True, stop=True)
                p.T(fbc, reads=(kN, "ones"), writes=(pb_key,), tag="nr_bc")
            def fsc(e, ps=ps, tlen=tlen, scol=scol):
                return e.activation(tmpA[:, 0:tlen], ps[0:128, 0:tlen], AF.Copy,
                                    scale=qss[:, scol:scol + 1])
            p.S(fsc, reads=(ps_key, "qss"), writes=(kA,), tag="nr_scale")

            def fnm(e, tlen=tlen, pb_=pb_):
                return e.tensor_tensor(tmpB[:, 0:tlen], tmpA[:, 0:tlen],
                                       pb_[0:128, 0:tlen], op=ALU.mult)
            p.V(fnm, reads=(kA, pb_key), writes=(kB,), tag="nr_norm")
            for gq in range(4):
                src_g = gq + 1 if gq % 2 == 0 else gq - 1

                def fc(e, gq=gq, src_g=src_g, tlen=tlen):
                    return e.tensor_copy(rbuf[gq * 32:(gq + 1) * 32, 0:tlen],
                                         tmpB[src_g * 32:(src_g + 1) * 32, 0:tlen])
                p.V(fc, reads=(kB,), writes=(kR,), tag="nr_rot")

            def f1(e, tlen=tlen, tabsl=tabsl):
                return e.tensor_tensor(tmpA[:, 0:tlen], tmpB[:, 0:tlen],
                                       cosT[:, tabsl], op=ALU.mult)
            p.V(f1, reads=(kB, cosK), writes=(kA,), tag="nr_cos")

            def f2(e, tlen=tlen, tabsl=tabsl):
                return e.tensor_tensor(rbuf[:, 0:tlen], rbuf[:, 0:tlen],
                                       sinT[:, tabsl], op=ALU.mult)
            p.V(f2, reads=(kR, sinK), writes=(kR,), tag="nr_sin")

            def f3(e, j=j, tt0=tt0, tlen=tlen):
                return e.tensor_tensor(dst[:, j, tt0:tt0 + tlen], tmpA[:, 0:tlen],
                                       rbuf[:, 0:tlen], op=ALU.add)
            p.V(f3, reads=(kA, kR), writes=(dst_key,), tag="nr_add")

        for kt6 in range(FKE // 512):
            c0 = 128 + kt6 * 512
            load_xtile(0, c0, 512, "xlt0")
            sv = ShiftView(xlt[0])

            def rhs_fn(c, sv=sv):
                return sv[slice(0, 128), c, slice(0, 512)]
            def evac_k(oc, ocw, tt0, tlen, ps, ps_key, kt6=kt6):
                norm_rope_evac(kT, "kT", kcos, "tcos", ksin, "tsin", 1,
                               oc, kt6 * 512 + tt0, tlen, ps, ps_key,
                               slice(kt6 * 512 + tt0, kt6 * 512 + tt0 + tlen))
            emit_proj(ctx, ws_k, rhs_fn, "xlt0", 512, evac_k, B[0:2])
            for tq in range(4):
                pvp, pv_key = B[2 + tq % 2]
                for e_ in range(NCH):
                    def fmm(e, e_=e_, tq=tq, sv=sv, pvp=pvp):
                        return e.matmul(pvp[:, 0:512],
                                        sv[slice(0, 128), e_,
                                           slice(tq * 128, (tq + 1) * 128)],
                                        wvs[:, e_, :],
                                        start=(e_ == 0), stop=(e_ == NCH - 1))
                    p.T(fmm, reads=("xlt0", "wvs"), writes=(pv_key,), tag="mm_v")
                kti = kt6 * 4 + tq
                for h in range(8):
                    def fcp(e, h=h, kti=kti, pvp=pvp):
                        return e.tensor_copy(va[:, kti, h * 65:h * 65 + 64],
                                             pvp[:, h * 64:(h + 1) * 64])
                    p.V(fcp, reads=(pv_key,), writes=("va",), tag="v_cp")

        def fones(e):
            return e.memset(va[:, :, 64:520:65], 1.0)
        p.G(fones, writes=("va",), tag="va_ones")

        load_tables("q")
        ws_q = WStream(ctx, "wq", wq_h, D, 512, tag="wq")
        for qt5 in range(FQE // 512):
            c0 = 640 + qt5 * 512
            load_xtile(0, c0, 512, "xlt0")
            sv = ShiftView(xlt[0])

            def rhs_fn(c, sv=sv):
                return sv[slice(0, 128), c, slice(0, 512)]

            def evac_q(oc, ocw, tt0, tlen, ps, ps_key, qt5=qt5):
                norm_rope_evac(qT, "qT", qcos, "tcos", qsin, "tsin", 0,
                               oc, qt5 * 512 + tt0, tlen, ps, ps_key,
                               slice(qt5 * 512 + tt0, qt5 * 512 + tt0 + tlen))
            emit_proj(ctx, ws_q, rhs_fn, "xlt0", 512, evac_q, B[0:2])

        exp8 = ctx.cf(8.0)
        for qt in range(5):
            qsl = slice(qt * 512, (qt + 1) * 512)
            for h in range(8):
                base = (h % 2) * 64
                ch = h // 2
                po, po_key = B[4 + (qt * 8 + h) % 2]
                STB = [B[2], B[3], B[0]]

                def emit_st(di, base=base, ch=ch, qt=qt, qsl=qsl):
                    kcol = qt * 512 + di * 128
                    ps, ps_key = STB[di % 3]

                    def fst(e, base=base, ch=ch, kcol=kcol, qsl=qsl, ps=ps):
                        return e.matmul(ps[:, 0:512],
                                        kT[base:base + 64, ch, kcol:kcol + 128],
                                        qT[base:base + 64, ch, qsl],
                                        start=True, stop=True)
                    p.T(fst, reads=("kT", "qT"), writes=(ps_key,), tag="lmm_st")

                emit_st(0)
                emit_st(1)
                for di in range(8):
                    if di + 2 < 8:
                        emit_st(di + 2)
                    kti = (qt * 512 + di * 128) // 128
                    ps, ps_key = STB[di % 3]
                    pb = pT[di % 3]

                    def fmsk(e, ps=ps, di=di):
                        return e.tensor_tensor(ps[:, 0:512], ps[:, 0:512],
                                               msk[:, di, :], op=ALU.add)
                    p.V(fmsk, reads=(ps_key, "msk"), writes=(ps_key,), tag="lmsk")

                    def fexp(e, ps=ps, pb=pb):
                        return e.activation(pb[:, :], ps[:, 0:512], AF.Exp,
                                            scale=exp8)
                    p.S(fexp, reads=(ps_key, "cft"), writes=(f"pTl{di % 3}",),
                        tag="lexp")
                    mvi = None
                    if qt * 512 + di * 128 < 1024:
                        mvi = qt * 8 + di if qt == 0 else 8 + di - 4
                    if mvi is not None:
                        def fmv(e, mvi=mvi, pb=pb):
                            return e.tensor_scalar_mul(pb[:, :], pb[:, :],
                                                       mv[:, mvi:mvi + 1])
                        p.V(fmv, reads=("mv", f"pTl{di % 3}"),
                            writes=(f"pTl{di % 3}",), tag="lmv")

                    def fpv(e, kti=kti, h=h, pb=pb, po=po, di=di):
                        return e.matmul(po[0:65, 0:512],
                                        va[:, kti, h * 65:h * 65 + 65],
                                        pb[:, :], start=(di == 0), stop=(di == 7))
                    p.T(fpv, reads=("va", f"pTl{di % 3}"), writes=(po_key,),
                        tag="lmm_pv")
                def fre(e, po=po):
                    return e.tensor_scalar_add(lrow[0:1, :], po[64:65, 0:512],
                                               1e-9)
                p.V(fre, reads=(po_key,), writes=("lrow",), tag="leps")

                def frc(e):
                    return e.reciprocal(lrow[0:1, :], lrow[0:1, :])
                p.V(frc, reads=("lrow",), writes=("lrow",), tag="lrecip")
                pb2, pb2_key = B[6]

                def fbc(e, pb2=pb2):
                    return e.matmul(pb2[0:64, 0:512], ones[0:1, 0:64],
                                    lrow[0:1, :], start=True, stop=True)
                p.T(fbc, reads=("lrow", "ones"), writes=(pb2_key,), tag="lmm_bc")

                def fcl(e, pb2=pb2):
                    return e.copy(lbc[:, :], pb2[0:64, 0:512])
                p.S(fcl, reads=(pb2_key,), writes=("lbc",), tag="lcp")

                def fdv(e, po=po, base=base, ch=ch, qsl=qsl):
                    return e.tensor_tensor(OT[base:base + 64, ch, qsl],
                                           po[0:64, 0:512], lbc[:, :], op=ALU.mult)
                p.V(fdv, reads=(po_key, "lbc"), writes=("OT",), tag="ldiv")

        ws_o = WStream(ctx, "wo", wo_h, 512, D, tag="wo")
        xp_src = xp_h if do_ln else None

        def rhs_o(c):
            return OTview[c]

        class OView:
            def __getitem__(self, c):
                return None
        OTs = [None]

        def rhs_o2(c):
            return OT[:, c, 0:2560]

        tload = {}

        def evac_o(oc, ocw, t0, tlen, ps, ps_key):
            j = oc // 128
            if t0 not in tload:
                tload[t0] = True
                if do_ln:
                    ap = xp_h[:, t0:t0 + tlen]
                else:
                    ap = xin_h[:, 640 + t0:640 + t0 + tlen]

                def fl(e, ap=ap, tlen=tlen):
                    return e.dma_start(out=xres[:, :, 0:tlen],
                                       in_=ap.rearrange("(c p) t -> p c t", p=128))
                p.dma("Y", fl, reads=tuple(f"xp{i}" for i in
                                           range((FXE + 511) // 512)) if do_ln
                      else (), writes=("xsm",), tag="ld_xres")

            def fn(e, j=j, tlen=tlen, ps=ps, ocw=ocw):
                return e.tensor_tensor(xres[0:ocw, j, 0:tlen], ps[0:ocw, 0:tlen],
                                       xres[0:ocw, j, 0:tlen], op=ALU.add)
            p.V(fn, reads=(ps_key, "xsm"), writes=("xsm",), tag="o_resid")
            if j == NCH - 1:
                dst = x1_h[:, t0:t0 + tlen]

                def fs(e, dst=dst, tlen=tlen):
                    return e.dma_start(out=dst.rearrange("(c p) t -> p c t", p=128),
                                       in_=xres[:, :, 0:tlen])
                p.dma("Y", fs, reads=("xsm",), writes=(f"x1@o{t0}",), tag="st_x1")
        emit_proj(ctx, ws_o, rhs_o2, "OT", 2560, evac_o, B[0:2], t_outer=True)

        p.barrier_all([f"x1@o{t0}" for t0 in range(0, 2560, 512)]
                      + ([f"xp{i}" for i in range((FXE + 511) // 512)] if do_ln else []))
        if local:
            p.finalize(nc)
    return ctx


def build_l0ffn_ffn(nc, add_module, fctx=None, io=None):
    local = fctx is None
    ctx = make_ctx(nc) if local else fctx
    nc = ctx.nc
    ctx.wpool_cfg = (22, 512)
    p = ctx.p
    x1_h = iod(ctx, io, "x1", [D, 2560], F32, kind="ExternalInput")
    w1_h = iod(ctx, io, "w1p", [11, 128, 8, 512], F32, kind="ExternalInput")
    w2_h = iod(ctx, io, "w2p", [2, 128, 22, 512], F32, kind="ExternalInput")
    lng_h = iod(ctx, io, "lng", [128, 8], F32, kind="ExternalInput")
    lnb_h = iod(ctx, io, "lnb", [128, 8], F32, kind="ExternalInput")
    b1_h = iod(ctx, io, "b1p", [128, 44], F32, kind="ExternalInput")
    b2_h = iod(ctx, io, "b2r", [1, D], F32, kind="ExternalInput")
    m0_h = iod(ctx, io, "m0", [128, 1], F32, kind="ExternalInput")
    if add_module:
        xm_h = iod(ctx, io, "xmod", [D, TC], F32, kind="ExternalInput")
    x2_h = iod(ctx, io, "x2", [D, TC], F32, kind="ExternalOutput")

    with (ctx.es if local else contextlib.nullcontext()):
        B = [(ctx.psum(f"ps{i}", [128, 512], F32), f"ps{i}") for i in range(8)]
        x1t = ctx.sbuf("x1t", [128, NCH, 513], F32)
        xln2 = ctx.sbuf("xln2", [128, NCH, 512], BF16)
        t1 = ctx.sbuf("t1", [128, 44, 512], BF16)
        t2 = ctx.sbuf("t2", [128, 22, 512], BF16)
        lng = ctx.sbuf("lngs", [128, 8], F32)
        lnb = ctx.sbuf("lnbs", [128, 8], F32)
        b1 = ctx.sbuf("b1s", [128, 44], F32)
        b2r = ctx.sbuf("b2rs", [1, D], F32)
        xmt = ctx.sbuf("xmt", [128, NCH, 512], F32)
        m0s = ctx.sbuf("m0s", [128, 1], F32)
        ones = get_ones(ctx)
        ln = LNHelper(ctx, tchunk=512)

        def ldm0(e):
            return e.dma_start(out=m0s[:, :], in_=m0_h[:, :])
        p.dma("Y", ldm0, writes=("m0s",), tag="ld_m0")

        for nm, sb, h in (("lng", lng, lng_h), ("lnb", lnb, lnb_h),
                          ("b1", b1, b1_h), ("b2r", b2r, b2_h)):
            def ld(e, sb=sb, h=h):
                return e.dma_start(out=sb[:, :], in_=h[:, :])
            p.dma("Y", ld, writes=("lnp" if nm in ("lng", "lnb") else nm,),
                  tag=f"ld_{nm}")

        ws1 = WStream(ctx, "w1", w1_h, D, 5632, tag="w1")
        ws2 = WStream(ctx, "w2", w2_h, 2816, D, tag="w2")

        for it in range(TC // 512):
            t0 = it * 512

            def flx(e, t0=t0):
                return e.dma_start(out=x1t[:, :, :],
                                   in_=x1_h[:, 511 + t0:511 + t0 + 513]
                                   .rearrange("(c p) t -> p c t", p=128))
            p.dma("Y", flx, writes=("x1t",), tag="ld_x1t")
            if it == 0:
                for c_ in range(NCH):
                    def fz(e, c_=c_):
                        return e.tensor_scalar_mul(x1t[:, c_, 0:1],
                                                   x1t[:, c_, 0:1], m0s[:, 0:1])
                    p.V(fz, reads=("x1t", "m0s"), writes=("x1t",), tag="zcol0")
            sv = ShiftView(x1t)
            ln.emit(sv, "x1t", xln2, "xln2", B[4], B[5], B[6], B[7],
                    gamma=lambda c: lng[:, c:c + 1],
                    beta=lambda c: lnb[:, c:c + 1])

            def evac_w1(oc, ocw, tt0, tlen, ps, ps_key):
                j = oc // 128
                func = AF.Gelu if j >= 22 else AF.Identity

                def fn(e, j=j, tt0=tt0, tlen=tlen, ps=ps, ocw=ocw, func=func):
                    return e.activation(t1[0:ocw, j, tt0:tt0 + tlen],
                                        ps[0:ocw, 0:tlen], func,
                                        bias=b1[0:ocw, j:j + 1])
                p.S(fn, reads=(ps_key, "b1"), writes=("t1",), tag="ev_w1")
            emit_proj(ctx, ws1, xln2, "xln2", 512, evac_w1, B[0:2])

            for j in range(22):
                def fg(e, j=j):
                    return e.tensor_tensor(t2[:, j, :], t1[:, j, :],
                                           t1[:, 22 + j, :], op=ALU.mult)
                p.V(fg, reads=("t1",), writes=("t2",), tag="geglu")

            if add_module:
                def flm(e, t0=t0):
                    return e.dma_start(out=xmt[:, :, :],
                                       in_=xm_h[:, t0:t0 + 512]
                                       .rearrange("(c p) t -> p c t", p=128))
                p.dma("Y", flm, writes=("xmt",), tag="ld_xm")

            def extra_b2(e, oc, ocw, tt0, tlen, ps):
                return e.matmul(ps[0:ocw, 0:tlen], b2r[0:1, oc:oc + ocw],
                                ones[0:1, 0:tlen], start=False, stop=True)
            extra_b2.reads = ("b2r", "ones")

            def evac_w2(oc, ocw, tt0, tlen, ps, ps_key):
                j = oc // 128

                def fn(e, j=j, tt0=tt0, tlen=tlen, ps=ps, ocw=ocw):
                    return e.tensor_tensor(x1t[0:ocw, j, 1 + tt0:1 + tt0 + tlen],
                                           ps[0:ocw, 0:tlen],
                                           x1t[0:ocw, j, 1 + tt0:1 + tt0 + tlen],
                                           op=ALU.add)
                p.V(fn, reads=(ps_key, "x1t"), writes=("x1t",), tag="w2_resid")
                if add_module:
                    def fm(e, j=j, tt0=tt0, tlen=tlen, ocw=ocw):
                        return e.tensor_tensor(
                            x1t[0:ocw, j, 1 + tt0:1 + tt0 + tlen],
                            x1t[0:ocw, j, 1 + tt0:1 + tt0 + tlen],
                            xmt[0:ocw, j, tt0:tt0 + tlen], op=ALU.add)
                    p.V(fm, reads=("xmt", "x1t"), writes=("x1t",), tag="w2_mod")

                def fs(e, j=j, t0=t0, tt0=tt0, tlen=tlen, ocw=ocw):
                    return e.dma_start(
                        out=x2_h[j * 128:j * 128 + ocw, t0 + tt0:t0 + tt0 + tlen],
                        in_=x1t[0:ocw, j, 1 + tt0:1 + tt0 + tlen])
                p.dma("Y", fs, reads=("x1t",), writes=(f"x2@o{t0}_{oc}",),
                      tag="st_x2")
            emit_proj(ctx, ws2, t2, "t2", 512, evac_w2, B[2:4],
                      extra_mm=extra_b2)

        p.barrier_all([f"x2@o{t0}_{oc}" for t0 in range(0, TC, 512)
                       for oc in range(0, D, 128)])
        if local:
            p.finalize(nc)
    return ctx




def build_cum1(nc, fctx=None, io=None):
    local = fctx is None
    ctx = make_ctx(nc) if local else fctx
    nc = ctx.nc
    ctx.wpool_cfg = (8, 512)
    p = ctx.p
    xT_h = iod(ctx, io, "xT", [D, TC], F32, kind="ExternalInput")
    wk_h = iod(ctx, io, "wkp", [2, 128, 8, 512], F32, kind="ExternalInput")
    wv_h = iod(ctx, io, "wvp", [2, 128, 8, 512], F32, kind="ExternalInput")
    kc_o = iod(ctx, io, "kc", [D, TC], F32, kind="ExternalOutput")
    kvc_o = iod(ctx, io, "kvc", [D, TC], F32, kind="ExternalOutput")
    tot_o = iod(ctx, io, "tot", [D, 2], F32, kind="ExternalOutput")

    with (ctx.es if local else contextlib.nullcontext()):
        B = [(ctx.psum(f"ps{i}", [128, 512], F32), f"ps{i}") for i in range(8)]
        xtile = ctx.sbuf("x", [128, NCH, 512], F32)
        xln = ctx.sbuf("xln", [128, NCH, TC], BF16)
        kraw = ctx.sbuf("kraw", [128, 4, TC], BF16)
        kcb = ctx.sbuf("kcb", [128, 4, TC], F32)
        kvcb = ctx.sbuf("kvcb", [128, 4, TC], F32)
        kvt = ctx.sbuf("kvt", [128, 512], F32)
        tot = ctx.sbuf("tot", [128, NCH, 2], F32)
        ones = get_ones(ctx)
        ln = LNHelper(ctx, tchunk=512)

        for tch in range(TC // 512):
            t0 = tch * 512
            load_xT(ctx, xT_h, xtile, "x", t0=t0, tlen=512)
            ln.emit(xtile, "x", SubView(xln, t0, 512), "xln",
                    B[4], B[5], B[6], B[7])

        ws_k = WStream(ctx, "wk", wk_h, D, D, tag="wk")
        ws_v = WStream(ctx, "wv", wv_h, D, D, tag="wv")

        for blk in range(2):
            def evac_k(oc, ocw, tt0, tlen, ps, ps_key):
                j4 = (oc // 128) % 4
                oc8 = oc // 128

                def fe(e, j4=j4, tt0=tt0, tlen=tlen, ps=ps):
                    return e.activation(kraw[:, j4, tt0:tt0 + tlen],
                                        ps[:, 0:tlen], AF.Exp)
                p.S(fe, reads=(ps_key,), writes=("kraw",), tag="c_exp")

                init = 0.0 if tt0 == 0 else kcb[:, j4, tt0 - 1:tt0]

                def fs(e, j4=j4, tt0=tt0, tlen=tlen, init=init):
                    return e.tensor_tensor_scan(
                        kcb[:, j4, tt0:tt0 + tlen], ones[:, 0:tlen],
                        kraw[:, j4, tt0:tt0 + tlen], init,
                        op0=ALU.mult, op1=ALU.add)
                p.V(fs, reads=("kraw", "ones", "kcb"), writes=("kcb",),
                    tag="c_scan")
                def fst(e, j4=j4, tt0=tt0, tlen=tlen, oc8=oc8):
                    return e.dma_start(
                        out=kc_o[oc8 * 128:(oc8 + 1) * 128, tt0:tt0 + tlen],
                        in_=kcb[:, j4, tt0:tt0 + tlen])
                p.dma("Y", fst, reads=("kcb",), writes=(f"kc@o{oc8}_{tt0}",),
                      tag="st_kc")
                if tt0 + tlen == TC:
                    def ftot(e, j4=j4, oc8=oc8):
                        return e.tensor_copy(tot[:, oc8, 0:1],
                                             kcb[:, j4, TC - 1:TC])
                    p.V(ftot, reads=("kcb",), writes=("tot",), tag="c_tot")
            emit_proj(ctx, ws_k, xln, "xln", TC, evac_k, B[0:2],
                      blk_range=(blk, blk + 1))

            def evac_v(oc, ocw, tt0, tlen, ps, ps_key):
                j4 = (oc // 128) % 4
                oc8 = oc // 128

                def fm(e, j4=j4, tt0=tt0, tlen=tlen, ps=ps):
                    return e.tensor_tensor(kvt[:, 0:tlen], ps[:, 0:tlen],
                                           kraw[:, j4, tt0:tt0 + tlen],
                                           op=ALU.mult)
                p.V(fm, reads=(ps_key, "kraw"), writes=("kvt",), tag="c_kv")

                init = 0.0 if tt0 == 0 else kvcb[:, j4, tt0 - 1:tt0]

                def fs(e, j4=j4, tt0=tt0, tlen=tlen, init=init):
                    return e.tensor_tensor_scan(
                        kvcb[:, j4, tt0:tt0 + tlen], ones[:, 0:tlen],
                        kvt[:, 0:tlen], init, op0=ALU.mult, op1=ALU.add)
                p.V(fs, reads=("kvt", "ones", "kvcb"), writes=("kvcb",),
                    tag="c_scan2")

                def fst(e, j4=j4, tt0=tt0, tlen=tlen, oc8=oc8):
                    return e.dma_start(
                        out=kvc_o[oc8 * 128:(oc8 + 1) * 128, tt0:tt0 + tlen],
                        in_=kvcb[:, j4, tt0:tt0 + tlen])
                p.dma("Y", fst, reads=("kvcb",), writes=(f"kvc@o{oc8}_{tt0}",),
                      tag="st_kvc")
                if tt0 + tlen == TC:
                    def ftot(e, j4=j4, oc8=oc8):
                        return e.tensor_copy(tot[:, oc8, 1:2],
                                             kvcb[:, j4, TC - 1:TC])
                    p.V(ftot, reads=("kvcb",), writes=("tot",), tag="c_tot2")
            emit_proj(ctx, ws_v, xln, "xln", TC, evac_v, B[2:4],
                      blk_range=(blk, blk + 1))

        def fstt(e):
            return e.dma_start(out=tot_o[:, :].rearrange("(c p) q -> p c q", p=128),
                               in_=tot[:, :, :])
        p.dma("Y", fstt, reads=("tot",), writes=("tot@o",), tag="st_tot")
        p.barrier_all(["tot@o"] + [f"kc@o{j}_{t}" for j in range(8)
                                   for t in range(0, TC, 512)]
                      + [f"kvc@o{j}_{t}" for j in range(8)
                         for t in range(0, TC, 512)])
        if local:
            p.finalize(nc)
    return ctx


def build_cum2(nc, fctx=None, io=None):
    local = fctx is None
    ctx = make_ctx(nc) if local else fctx
    nc = ctx.nc
    ctx.wpool_cfg = (8, 512)
    p = ctx.p
    xT_h = iod(ctx, io, "xT", [D, TC], F32, kind="ExternalInput")
    wq_h = iod(ctx, io, "wqp", [2, 128, 8, 512], F32, kind="ExternalInput")
    wc_h = iod(ctx, io, "wcp", [2, 128, 8, 512], F32, kind="ExternalInput")
    kc_h = iod(ctx, io, "kc", [D, TC], F32, kind="ExternalInput")
    kvc_h = iod(ctx, io, "kvc", [D, TC], F32, kind="ExternalInput")
    tots_h = iod(ctx, io, "tots", [4, D, 2], F32, kind="ExternalInput")
    cm_h = iod(ctx, io, "cmask", [128, 4], F32, kind="ExternalInput")
    yT_h = iod(ctx, io, "yT", [D, TC], F32, kind="ExternalOutput")

    with (ctx.es if local else contextlib.nullcontext()):
        B = [(ctx.psum(f"ps{i}", [128, 512], F32), f"ps{i}") for i in range(8)]
        xtile = ctx.sbuf("x", [128, NCH, 512], F32)
        xln = ctx.sbuf("xln", [128, NCH, TC], BF16)
        Ob = ctx.sbuf("Ob", [128, NCH, TC], BF16)
        tots = ctx.sbuf("tots", [128, NCH, 8], F32)
        offs = ctx.sbuf("offs", [128, NCH, 2], F32)
        cm = ctx.sbuf("cm", [128, 4], F32)
        sct = ctx.sbuf("sct", [128, 512], F32)
        sct2 = ctx.sbuf("sct2", [128, 512], F32)
        xres = ctx.sbuf("xres", [128, NCH, 512], F32)
        ones = get_ones(ctx)
        ln = LNHelper(ctx, tchunk=512)

        for r in range(4):
            def ldt(e, r=r):
                return e.dma_start(out=tots[:, :, r * 2:(r + 1) * 2],
                                   in_=tots_h[r, :, :].rearrange("(c p) q -> p c q", p=128))
            p.dma("Y", ldt, writes=("tots",), tag="ld_tots")

        def ldc(e):
            return e.dma_start(out=cm[:, :], in_=cm_h[:, :])
        p.dma("Y", ldc, writes=("cm",), tag="ld_cm")

        def fz(e):
            return e.memset(offs[:, :, :], 0.0)
        p.G(fz, writes=("offs",), tag="off_z")
        for q in range(2):
            for cp in range(4):
                for j in range(NCH):
                    def fa(e, q=q, cp=cp, j=j):
                        return e.scalar_tensor_tensor(
                            offs[:, j, q:q + 1], tots[:, j, cp * 2 + q:cp * 2 + q + 1],
                            cm[:, cp:cp + 1], offs[:, j, q:q + 1],
                            op0=ALU.mult, op1=ALU.add)
                    p.V(fa, reads=("tots", "cm", "offs"), writes=("offs",),
                        tag="off_acc")

        for tch in range(TC // 512):
            t0 = tch * 512
            load_xT(ctx, xT_h, xtile, "x", t0=t0, tlen=512)
            ln.emit(xtile, "x", SubView(xln, t0, 512), "xln",
                    B[4], B[5], B[6], B[7])

        ws_q = WStream(ctx, "wq", wq_h, D, D, tag="wq")
        kct = ctx.sbuf("kct", [128, 512], F32)

        def evac_q(oc, ocw, tt0, tlen, ps, ps_key):
            j = oc // 128

            def flk(e, j=j, tt0=tt0, tlen=tlen):
                return e.dma_start(out=kct[:, 0:tlen],
                                   in_=kc_h[j * 128:(j + 1) * 128, tt0:tt0 + tlen])
            p.dma("G", flk, writes=("kct",), tag="ld_kc", role="kcl")

            def f1(e, j=j, tlen=tlen):
                return e.tensor_scalar_add(kct[:, 0:tlen], kct[:, 0:tlen],
                                           offs[:, j, 0:1])
            p.V(f1, reads=("kct", "offs"), writes=("kct",), tag="q_offk")

            def f2(e, tlen=tlen):
                return e.tensor_scalar_add(kct[:, 0:tlen], kct[:, 0:tlen], 1e-6)
            p.V(f2, reads=("kct",), writes=("kct",), tag="q_eps")

            def f3(e, tlen=tlen):
                return e.reciprocal(kct[:, 0:tlen], kct[:, 0:tlen])
            p.V(f3, reads=("kct",), writes=("kct",), tag="q_rec")

            def flv(e, j=j, tt0=tt0, tlen=tlen):
                return e.dma_start(out=sct[:, 0:tlen],
                                   in_=kvc_h[j * 128:(j + 1) * 128, tt0:tt0 + tlen])
            p.dma("G", flv, writes=("sct",), tag="ld_kvc", role="kcl")

            def f4(e, j=j, tlen=tlen):
                return e.tensor_scalar_add(sct[:, 0:tlen], sct[:, 0:tlen],
                                           offs[:, j, 1:2])
            p.V(f4, reads=("sct", "offs"), writes=("sct",), tag="q_offv")

            def f5(e, tlen=tlen):
                return e.tensor_tensor(sct[:, 0:tlen], sct[:, 0:tlen],
                                       kct[:, 0:tlen], op=ALU.mult)
            p.V(f5, reads=("sct", "kct"), writes=("sct",), tag="q_s")

            def f6(e, tlen=tlen, ps=ps):
                return e.activation(sct2[:, 0:tlen], ps[:, 0:tlen], AF.Sigmoid)
            p.S(f6, reads=(ps_key,), writes=("sct2",), tag="q_sig")

            def f7(e, j=j, tt0=tt0, tlen=tlen):
                return e.tensor_tensor(Ob[:, j, tt0:tt0 + tlen], sct[:, 0:tlen],
                                       sct2[:, 0:tlen], op=ALU.mult)
            p.V(f7, reads=("sct", "sct2"), writes=("Ob",), tag="q_o")
        emit_proj(ctx, ws_q, xln, "xln", TC, evac_q, B[0:2])

        ws_c = WStream(ctx, "wc", wc_h, D, D, tag="wc")
        tload = {}

        def evac_o(oc, ocw, t0, tlen, ps, ps_key):
            j = oc // 128
            if t0 not in tload:
                tload[t0] = True
                load_xT(ctx, xT_h, xres, "xres", t0=t0, tlen=512)

            def fn(e, j=j, tlen=tlen, ps=ps, ocw=ocw):
                return e.tensor_tensor(xres[0:ocw, j, 0:tlen], ps[0:ocw, 0:tlen],
                                       xres[0:ocw, j, 0:tlen], op=ALU.add)
            p.V(fn, reads=(ps_key, "xres"), writes=("xres",), tag="resid")
            if j == NCH - 1:
                store_xT(ctx, yT_h, xres, "xres", f"yT@o{t0}", t0=t0, tlen=512)
        emit_proj(ctx, ws_c, Ob, "Ob", TC, evac_o, B[2:4], t_outer=True)

        p.barrier_all([f"yT@o{t0}" for t0 in range(0, TC, 512)])
        if local:
            p.finalize(nc)
    return ctx


def cum_host(inputs, xT_full):
    g = lambda k: np.ascontiguousarray(np.asarray(inputs[k], np.float32))
    wk = pack_weight(g("L1_attn_a_Wk"))
    wv = pack_weight(g("L1_attn_a_Wv"))
    wq = pack_weight(g("L1_attn_a_Wq"))
    wc = pack_weight(g("L1_attn_a_Wc"))

    nc1 = bass.Bass("TRN2", target_bir_lowering=False)
    c1 = build_cum1(nc1)
    maps = []
    for c in range(NCORES):
        m = dict(xT=_shard_cols(xT_full, c), wkp=wk, wvp=wv)
        m.update(c1.consts)
        if hasattr(c1, "cf_map"):
            m["cf_consts"] = c1.cf_array()
        maps.append(m)
    r1 = run_bass_kernel_spmd(nc1, maps, core_ids=list(range(NCORES)))

    tots_by_batch = [np.stack([r1.results[b * 4 + i]["tot"] for i in range(4)])
                     for b in range(2)]

    nc2 = bass.Bass("TRN2", target_bir_lowering=False)
    c2 = build_cum2(nc2)
    maps2 = []
    for c in range(NCORES):
        cmask = np.zeros((128, 4), np.float32)
        for cp in range(4):
            if cp < c % 4:
                cmask[:, cp] = 1.0
        m = dict(xT=_shard_cols(xT_full, c), wqp=wq, wcp=wc,
                 kc=r1.results[c]["kc"], kvc=r1.results[c]["kvc"],
                 tots=tots_by_batch[c // 4], cmask=cmask)
        m.update(c2.consts)
        m["cf_consts"] = c2.cf_array()
        maps2.append(m)
    r2 = run_bass_kernel_spmd(nc2, maps2, core_ids=list(range(NCORES)))
    return np.concatenate([r2.results[c]["yT"] for c in range(NCORES)], axis=1)




def build_ssm1(nc, fctx=None, io=None):
    local = fctx is None
    ctx = make_ctx(nc) if local else fctx
    nc = ctx.nc
    ctx.wpool_cfg = (8, 512)
    p = ctx.p
    xT_h = iod(ctx, io, "xT", [D, TC], F32, kind="ExternalInput")
    wi_h = iod(ctx, io, "wip", [8, 128, 8, 512], F32, kind="ExternalInput")
    xp_o = iod(ctx, io, "xpart", [2048, TC], F32, kind="ExternalOutput")
    z_o = iod(ctx, io, "z", [2048, TC], F32, kind="ExternalOutput")

    with (ctx.es if local else contextlib.nullcontext()):
        B = [(ctx.psum(f"ps{i}", [128, 512], F32), f"ps{i}") for i in range(8)]
        xtile = ctx.sbuf("x", [128, NCH, 512], F32)
        xln = ctx.sbuf("xln", [128, NCH, TC], BF16)
        stg = [ctx.sbuf(f"stg{i}", [128, 512], F32) for i in range(2)]
        ln = LNHelper(ctx, tchunk=512)
        get_ones(ctx)

        for tch in range(TC // 512):
            t0 = tch * 512
            load_xT(ctx, xT_h, xtile, "x", t0=t0, tlen=512)
            ln.emit(xtile, "x", SubView(xln, t0, 512), "xln",
                    B[4], B[5], B[6], B[7])

        ws = WStream(ctx, "wi", wi_h, D, 4096, tag="wi")
        cnt = [0]

        def evac(oc, ocw, tt0, tlen, ps, ps_key):
            si = cnt[0] % 2
            cnt[0] += 1
            dst = xp_o if oc < 2048 else z_o
            ro = oc if oc < 2048 else oc - 2048

            def fc(e, si=si, tlen=tlen, ps=ps):
                return e.copy(stg[si][:, 0:tlen], ps[:, 0:tlen])
            p.S(fc, reads=(ps_key,), writes=(f"stg{si}",), tag="s1_cp")

            def fs(e, si=si, dst=dst, ro=ro, tt0=tt0, tlen=tlen):
                return e.dma_start(out=dst[ro:ro + 128, tt0:tt0 + tlen],
                                   in_=stg[si][:, 0:tlen])
            p.dma("Y", fs, reads=(f"stg{si}",), writes=(f"s1@o{ro}_{tt0}_{dst.name}",),
                  tag="s1_st")
        emit_proj(ctx, ws, xln, "xln", TC, evac, B[0:2])
        p.barrier_all([f"s1@o{ro}_{tt0}_{nm}" for ro in range(0, 2048, 128)
                       for tt0 in range(0, TC, 512) for nm in ("xpart", "z")])
        if local:
            p.finalize(nc)
    return ctx


def build_ssm2(nc, fctx=None, io=None, xin_pad=3):
    local = fctx is None
    ctx = make_ctx(nc) if local else fctx
    nc = ctx.nc
    ctx.wpool_cfg = (16, 512)
    p = ctx.p
    xin_h = iod(ctx, io, "xin", [2048, TC + xin_pad], F32, kind="ExternalInput")
    wxp_h = iod(ctx, io, "wxp", [1, 128, 16, 96], F32, kind="ExternalInput")
    wdt_h = iod(ctx, io, "wdt", [4, 128, 1, 512], F32, kind="ExternalInput")
    cw_h = iod(ctx, io, "cw", [128, 16, 4], F32, kind="ExternalInput")
    cb_h = iod(ctx, io, "cb", [128, 16], F32, kind="ExternalInput")
    dtb_h = iod(ctx, io, "dtb", [128, 16], F32, kind="ExternalInput")
    xc_o = iod(ctx, io, "xc", [2048, TC], BF16, kind="ExternalOutput")
    dtt_o = iod(ctx, io, "dtt", [2048, TC], BF16, kind="ExternalOutput")
    bc_o = iod(ctx, io, "bc", [32, TC], F32, kind="ExternalOutput")

    with (ctx.es if local else contextlib.nullcontext()):
        B = [(ctx.psum(f"ps{i}", [128, 512], F32), f"ps{i}") for i in range(8)]
        xint = ctx.sbuf("xint", [128, 16, 515], F32)
        xc = ctx.sbuf("xc", [128, 16, TC], BF16)
        xdb = ctx.sbuf("xdb", [96, TC], F32)
        xdbb = ctx.sbuf("xdbb", [64, TC], BF16)
        cw = ctx.sbuf("cws", [128, 16, 4], F32)
        cb = ctx.sbuf("cbs", [128, 16], F32)
        dtb = ctx.sbuf("dtbs", [128, 16], F32)
        ta = ctx.sbuf("ta", [128, 512], F32)
        tb = ctx.sbuf("tb", [128, 512], F32)
        stg = [ctx.sbuf(f"stg{i}", [128, 512], BF16) for i in range(2)]
        get_ones(ctx)

        for nm, sb, h in (("cw", cw, cw_h), ("cb", cb, cb_h), ("dtb", dtb, dtb_h)):
            def ld(e, sb=sb, h=h, nm=nm):
                if nm == "cw":
                    return e.dma_start(out=sb[:, :, :], in_=h[:, :, :])
                return e.dma_start(out=sb[:, :], in_=h[:, :])
            p.dma("Y", ld, writes=(nm,), tag=f"ld_{nm}")

        for tch in range(TC // 512):
            t0 = tch * 512

            def fl(e, t0=t0):
                return e.dma_start(out=xint[:, :, :],
                                   in_=xin_h[:, t0 + xin_pad - 3:t0 + xin_pad + 512]
                                   .rearrange("(c p) t -> p c t", p=128))
            p.dma("Y", fl, writes=("xint",), tag="ld_xin")
            for j in range(16):
                def f0(e, j=j):
                    return e.tensor_scalar_mul(ta[:, :], xint[:, j, 0:512],
                                               cw[:, j, 0:1])
                p.V(f0, reads=("xint", "cw"), writes=("ta",), tag="cv0")
                for k in range(1, 4):
                    def fk(e, j=j, k=k):
                        return e.scalar_tensor_tensor(ta[:, :], xint[:, j, k:k + 512],
                                                      cw[:, j, k:k + 1], ta[:, :],
                                                      op0=ALU.mult, op1=ALU.add)
                    p.V(fk, reads=("xint", "cw", "ta"), writes=("ta",), tag="cvk")

                def fsl(e, j=j, t0=t0):
                    return e.activation(xc[:, j, t0:t0 + 512], ta[:, :], AF.Silu,
                                        bias=cb[:, j:j + 1])
                p.S(fsl, reads=("ta", "cb"), writes=("xc",), tag="cv_silu")

                def fst(e, j=j, t0=t0):
                    return e.dma_start(out=xc_o[j * 128:(j + 1) * 128, t0:t0 + 512],
                                       in_=xc[:, j, t0:t0 + 512])
                p.dma("Y", fst, reads=("xc",), writes=(f"xc@o{j}_{t0}",), tag="st_xc")

        ws_xp = WStream(ctx, "wxp", wxp_h, 2048, 96, blk=96, tag="wxp")

        def evac_xdb(oc, ocw, tt0, tlen, ps, ps_key):
            def fc(e, tt0=tt0, tlen=tlen, ps=ps, ocw=ocw):
                return e.copy(xdb[0:ocw, tt0:tt0 + tlen], ps[0:ocw, 0:tlen])
            p.S(fc, reads=(ps_key,), writes=("xdb",), tag="xdb_cp")

            def fcb(e, tt0=tt0, tlen=tlen):
                return e.tensor_copy(xdbb[0:64, tt0:tt0 + tlen],
                                     xdb[0:64, tt0:tt0 + tlen])
            p.V(fcb, reads=("xdb",), writes=("xdbb",), tag="xdbb_cp")
        emit_proj(ctx, ws_xp, xc, "xc", TC, evac_xdb, B[0:2])

        def fbc_st(e):
            return e.dma_start(out=bc_o[:, :], in_=xdb[64:96, :])
        p.dma("Y", fbc_st, reads=("xdb",), writes=("bc@o",), tag="st_bc")

        ws_dt = WStream(ctx, "wdt", wdt_h, 64, 2048, tag="wdt")

        def rhs_dt(c):
            return xdbb[0:64, :]

        cnt = [0]

        def evac_dt(oc, ocw, tt0, tlen, ps, ps_key):
            j = oc // 128
            si = cnt[0] % 2
            cnt[0] += 1

            def f1(e, j=j, tlen=tlen, ps=ps):
                return e.activation(ta[:, 0:tlen], ps[:, 0:tlen], AF.Abs,
                                    bias=dtb[:, j:j + 1])
            p.S(f1, reads=(ps_key, "dtb"), writes=("ta",), tag="dt_abs")

            def f2(e, tlen=tlen):
                return e.activation(ta[:, 0:tlen], ta[:, 0:tlen], AF.Exp,
                                    scale=-1.0)
            p.S(f2, reads=("ta",), writes=("ta",), tag="dt_exp")

            def f3(e, tlen=tlen):
                return e.activation(ta[:, 0:tlen], ta[:, 0:tlen], AF.Ln, bias=1.0)
            p.S(f3, reads=("ta",), writes=("ta",), tag="dt_ln")

            def f4(e, j=j, tlen=tlen, ps=ps):
                return e.activation(tb[:, 0:tlen], ps[:, 0:tlen], AF.Relu,
                                    bias=dtb[:, j:j + 1])
            p.S(f4, reads=(ps_key, "dtb"), writes=("tb",), tag="dt_relu")

            def f5(e, si=si, tlen=tlen):
                return e.tensor_tensor(stg[si][:, 0:tlen], ta[:, 0:tlen],
                                       tb[:, 0:tlen], op=ALU.add)
            p.V(f5, reads=("ta", "tb"), writes=(f"stg{si}",), tag="dt_add")

            def fs(e, si=si, j=j, tt0=tt0, tlen=tlen):
                return e.dma_start(out=dtt_o[j * 128:(j + 1) * 128, tt0:tt0 + tlen],
                                   in_=stg[si][:, 0:tlen])
            p.dma("Y", fs, reads=(f"stg{si}",), writes=(f"dt@o{j}_{tt0}",),
                  tag="st_dt")
        emit_proj(ctx, ws_dt, rhs_dt, "xdbb", TC, evac_dt, B[2:4])

        p.barrier_all(["bc@o"] + [f"xc@o{j}_{t}" for j in range(16)
                                  for t in range(0, TC, 512)]
                      + [f"dt@o{j}_{t}" for j in range(16)
                         for t in range(0, TC, 512)])
        if local:
            p.finalize(nc)
    return ctx


def build_ssm3(nc, TFULL=8192, fctx=None, io=None, dyn=False):
    local = fctx is None
    ctx = make_ctx(nc) if local else fctx
    nc = ctx.nc
    p = ctx.p
    NDC = 4
    dts_h = iod(ctx, io, "dts", [4, 512, TC], BF16, kind="ExternalInput")
    xcs_h = iod(ctx, io, "xcs", [4, 512, TC], BF16, kind="ExternalInput")
    if dyn:
        ys_o = iod(ctx, io, "ys", [512, TFULL], BF16, kind="Internal")
    else:
        ys_o = iod(ctx, io, "ys", [4, 512, TC], BF16, kind="ExternalOutput")
    bf_h = iod(ctx, io, "bcf", [4, 32, TC], F32, kind="ExternalInput")
    al_h = iod(ctx, io, "alog", [128, 4, 16], F32, kind="ExternalInput")
    dd_h = iod(ctx, io, "dd", [128, 4], F32, kind="ExternalInput")

    T2 = 2048
    ntc = TFULL // T2
    with (ctx.es if local else contextlib.nullcontext()):
        B = [(ctx.psum(f"ps{i}", [128, 512], F32), f"ps{i}") for i in range(8)]
        dts = ctx.sbuf("dts", [128, NDC, T2], BF16)
        xcs = ctx.sbuf("xcs", [128, NDC, T2], BF16)
        an = ctx.sbuf("an", [128, NDC, 16], F32)
        dd = ctx.sbuf("dd", [128, NDC], F32)
        carr = ctx.sbuf("carr", [128, NDC, 16], F32)
        us = [ctx.sbuf(f"u{i}", [128, T2], BF16) for i in range(NDC)]
        Bb = [ctx.sbuf(f"Bb{i}", [128, T2], BF16) for i in range(2)]
        Cb = [ctx.sbuf(f"Cb{i}", [128, T2], BF16) for i in range(2)]
        dat = [ctx.sbuf(f"dat{i}", [128, T2], BF16) for i in range(2)]
        dbu = [ctx.sbuf(f"dbu{i}", [128, T2], BF16) for i in range(2)]
        ht = [ctx.sbuf(f"ht{i}", [128, T2], BF16) for i in range(2)]
        tm = [ctx.sbuf(f"tm{i}", [128, T2], BF16) for i in range(2)]
        yt = [ctx.sbuf(f"yt{i}", [128, T2], F32) for i in range(NDC)]
        ysb = None
        brow = [ctx.sbuf(f"brow{i}", [1, 2048], F32) for i in range(2)]
        ones = get_ones(ctx)

        def lda(e):
            return e.dma_start(out=an[:, :, :], in_=al_h[:, :, :])
        p.dma("Y", lda, writes=("an",), tag="ld_al")

        def ldd(e):
            return e.dma_start(out=dd[:, :], in_=dd_h[:, :])
        p.dma("Y", ldd, writes=("dd",), tag="ld_dd")

        def fea(e):
            return e.activation(an[:, :, :], an[:, :, :], AF.Exp)
        p.S(fea, reads=("an",), writes=("an",), tag="a_exp")

        def fng(e):
            return e.tensor_scalar_mul(an[:, :, :], an[:, :, :], -1.0)
        p.V(fng, reads=("an",), writes=("an",), tag="a_neg")

        def fcz(e):
            return e.memset(carr[:, :, :], 0.0)
        p.G(fcz, writes=("carr",), tag="carr_z")

        for tci in range(ntc):
            t0 = tci * T2

            def ltd(e, tci=tci):
                return e.dma_start(out=dts[:, :, :],
                                   in_=dts_h[tci, :, :].rearrange("(c p) t -> p c t", p=128))
            p.dma("Y", ltd, writes=("dts",), tag="ld_dts")

            def ltx(e, tci=tci):
                return e.dma_start(out=xcs[:, :, :],
                                   in_=xcs_h[tci, :, :].rearrange("(c p) t -> p c t", p=128))
            p.dma("Y", ltx, writes=("xcs",), tag="ld_xcs")
            for dch in range(NDC):
                def fu(e, dch=dch):
                    return e.tensor_tensor(us[dch][:, :], dts[:, dch, :],
                                           xcs[:, dch, :], op=ALU.mult)
                p.V(fu, reads=("dts", "xcs"), writes=(f"u{dch}",), tag="s_u")
            for n in range(16):
                bi = n % 2
                browB = brow[bi]

                def flb(e, n=n, t0=t0, bi=bi):
                    rk, tt = t0 // TC, t0 % TC
                    return e.dma_start(out=brow[bi][0:1, :],
                                       in_=bf_h[rk, n:n + 1, tt:tt + T2])
                p.dma("Y", flb, writes=(f"browB{bi}",), tag="ld_b", role="br")
                for sc in range(T2 // 512):
                    pb_, pb_key = B[sc % 2]

                    def fb(e, sc=sc, pb_=pb_, bi=bi):
                        return e.matmul(pb_[:, 0:512], ones[0:1, 0:128],
                                        brow[bi][0:1, sc * 512:(sc + 1) * 512],
                                        start=True, stop=True)
                    p.T(fb, reads=(f"browB{bi}", "ones"), writes=(pb_key,), tag="s_bb")

                    def fcpB(e, sc=sc, pb_=pb_, bi=bi):
                        return e.copy(Bb[bi][:, sc * 512:(sc + 1) * 512],
                                      pb_[:, 0:512])
                    p.S(fcpB, reads=(pb_key,), writes=(f"Bb{bi}",), tag="s_bcp")

                def flc(e, n=n, t0=t0, bi=bi):
                    rk, tt = t0 // TC, t0 % TC
                    return e.dma_start(out=brow[bi][0:1, :],
                                       in_=bf_h[rk, 16 + n:17 + n, tt:tt + T2])
                p.dma("Y", flc, reads=(f"browB{bi}",), writes=(f"browB{bi}",),
                      tag="ld_c", role="br")
                for sc in range(T2 // 512):
                    pc_, pc_key = B[2 + sc % 2]

                    def fc2(e, sc=sc, pc_=pc_, bi=bi):
                        return e.matmul(pc_[:, 0:512], ones[0:1, 0:128],
                                        brow[bi][0:1, sc * 512:(sc + 1) * 512],
                                        start=True, stop=True)
                    p.T(fc2, reads=(f"browB{bi}", "ones"), writes=(pc_key,), tag="s_cb")

                    def fcpC(e, sc=sc, pc_=pc_, bi=bi):
                        return e.copy(Cb[bi][:, sc * 512:(sc + 1) * 512],
                                      pc_[:, 0:512])
                    p.S(fcpC, reads=(pc_key,), writes=(f"Cb{bi}",), tag="s_ccp")

                for dch in range(NDC):
                    pi = (n * NDC + dch) % 2
                    eng = "G" if dch == 3 else "V"

                    def fda(e, dch=dch, n=n, pi=pi):
                        return e.activation(dat[pi][:, :], dts[:, dch, :],
                                            AF.Exp, scale=an[:, dch, n:n + 1])
                    p.S(fda, reads=("dts", "an"), writes=(f"dat{pi}",), tag="s_da")

                    def fdb(e, dch=dch, pi=pi, bi=bi):
                        return e.tensor_tensor(dbu[pi][:, :], us[dch][:, :],
                                               Bb[bi][:, :], op=ALU.mult)
                    p.add(eng, fdb, reads=(f"u{dch}", f"Bb{bi}"),
                          writes=(f"dbu{pi}",), tag="s_db")

                    def fsc(e, dch=dch, n=n, pi=pi):
                        return e.tensor_tensor_scan(ht[pi][:, :], dat[pi][:, :],
                                                    dbu[pi][:, :],
                                                    carr[:, dch, n:n + 1],
                                                    op0=ALU.mult, op1=ALU.add)
                    p.V(fsc, reads=(f"dat{pi}", f"dbu{pi}", "carr"),
                        writes=(f"ht{pi}",), tag="s_scan")

                    def fcu(e, dch=dch, n=n, pi=pi):
                        return e.tensor_copy(carr[:, dch, n:n + 1],
                                             ht[pi][:, T2 - 1:T2])
                    p.V(fcu, reads=(f"ht{pi}",), writes=("carr",), tag="s_carr")

                    def fm1(e, pi=pi, bi=bi):
                        return e.tensor_tensor(tm[pi][:, :], ht[pi][:, :],
                                               Cb[bi][:, :], op=ALU.mult)
                    p.add(eng, fm1, reads=(f"ht{pi}", f"Cb{bi}"),
                          writes=(f"tm{pi}",), tag="s_yc")

                    if n == 0:
                        def fy0(e, dch=dch, pi=pi):
                            return e.tensor_copy(yt[dch][:, :], tm[pi][:, :])
                        p.add(eng, fy0, reads=(f"tm{pi}",),
                              writes=(f"yt{dch}",), tag="s_y0")
                    else:
                        def fya(e, dch=dch, pi=pi):
                            return e.tensor_tensor(yt[dch][:, :], yt[dch][:, :],
                                                   tm[pi][:, :], op=ALU.add)
                        p.add(eng, fya, reads=(f"yt{dch}", f"tm{pi}"),
                              writes=(f"yt{dch}",), tag="s_ya")
            for dch in range(NDC):
                yo = tm[dch % 2]

                def fdt(e, dch=dch, yo=yo):
                    return e.scalar_tensor_tensor(yo[:, :], xcs[:, dch, :],
                                                  dd[:, dch:dch + 1], yt[dch][:, :],
                                                  op0=ALU.mult, op1=ALU.add)
                p.V(fdt, reads=("xcs", "dd", f"yt{dch}"),
                    writes=(f"tm{dch % 2}",), tag="s_d")

                def fst(e, dch=dch, t0=t0, yo=yo):
                    if dyn:
                        return e.dma_start(out=ys_o[dch * 128:(dch + 1) * 128,
                                                    t0:t0 + T2], in_=yo[:, :])
                    rk, tt = t0 // TC, t0 % TC
                    return e.dma_start(out=ys_o[rk, dch * 128:(dch + 1) * 128,
                                                tt:tt + T2],
                                       in_=yo[:, :])
                p.dma("Y", fst, reads=(f"tm{dch % 2}",), writes=(f"ys@o{dch}_{t0}",),
                      tag="st_ys")
        p.barrier_all([f"ys@o{d}_{t}" for d in range(NDC)
                       for t in range(0, TFULL, T2)])
        if local:
            p.finalize(nc)
    return ctx


def build_ssm4(nc, fctx=None, io=None, dyn=False):
    local = fctx is None
    ctx = make_ctx(nc) if local else fctx
    nc = ctx.nc
    ctx.wpool_cfg = (16, 512)
    p = ctx.p
    xT_h = iod(ctx, io, "xT", [D, TC], F32, kind="ExternalInput")
    if dyn:
        y_h = io["y"]
    else:
        y_h = iod(ctx, io, "y", [4, 512, TC], BF16, kind="ExternalInput")
    z_h = iod(ctx, io, "z", [2048, TC], F32, kind="ExternalInput")
    wo_h = iod(ctx, io, "wop", [2, 128, 16, 512], F32, kind="ExternalInput")
    yT_o = iod(ctx, io, "yT", [D, TC], F32, kind="ExternalOutput")

    with (ctx.es if local else contextlib.nullcontext()):
        B = [(ctx.psum(f"ps{i}", [128, 512], F32), f"ps{i}") for i in range(8)]
        rhs = ctx.sbuf("rhs", [128, 16, TC], BF16)
        zt = ctx.sbuf("zt", [128, 512], F32)
        zs = ctx.sbuf("zs", [128, 512], F32)
        xres = ctx.sbuf("xres", [128, NCH, 512], F32)
        get_ones(ctx)

        if dyn:
            for r in range(4):
                for dc in range(4):
                    def ldy(e, r=r, dc=dc):
                        return e.dma_start(
                            out=rhs[:, 4 * r + dc, :],
                            in_=y_h[r, dc * 128:(dc + 1) * 128,
                                    bass.ds(ctx.snaps["t0"], TC)])
                    p.add("G", ldy, reads=("ridx_regs",), writes=("rhs",),
                          inc=16, tag="ld_y", role="Gd")
        else:
            for r in range(4):
                def ldy(e, r=r):
                    return e.dma_start(out=rhs[:, 4 * r:4 * (r + 1), :],
                                       in_=y_h[r, :, :].rearrange("(c p) t -> p c t", p=128))
                p.dma("Y", ldy, writes=("rhs",), tag="ld_y")

        for j in range(16):
            for tch in range(TC // 512):
                t0 = tch * 512

                def flz(e, j=j, t0=t0):
                    return e.dma_start(out=zt[:, :],
                                       in_=z_h[j * 128:(j + 1) * 128, t0:t0 + 512])
                p.dma("G", flz, writes=("zt",), tag="ld_z", role="zl")

                def fsl(e):
                    return e.activation(zs[:, :], zt[:, :], AF.Silu)
                p.S(fsl, reads=("zt",), writes=("zs",), tag="z_silu")

                def fm(e, j=j, t0=t0):
                    return e.tensor_tensor(rhs[:, j, t0:t0 + 512],
                                           rhs[:, j, t0:t0 + 512], zs[:, :],
                                           op=ALU.mult)
                p.V(fm, reads=("rhs", "zs"), writes=("rhs",), tag="yz")

        ws = WStream(ctx, "wo", wo_h, 2048, D, tag="wo")
        tload = {}

        def evac(oc, ocw, t0, tlen, ps, ps_key):
            j = oc // 128
            if t0 not in tload:
                tload[t0] = True
                load_xT(ctx, xT_h, xres, "xres", t0=t0, tlen=512)

            def fn(e, j=j, tlen=tlen, ps=ps, ocw=ocw):
                return e.tensor_tensor(xres[0:ocw, j, 0:tlen], ps[0:ocw, 0:tlen],
                                       xres[0:ocw, j, 0:tlen], op=ALU.add)
            p.V(fn, reads=(ps_key, "xres"), writes=("xres",), tag="resid")
            if j == NCH - 1:
                store_xT(ctx, yT_o, xres, "xres", f"yT@o{t0}", t0=t0, tlen=512)
        emit_proj(ctx, ws, rhs, "rhs", TC, evac, B[0:2], t_outer=True)
        p.barrier_all([f"yT@o{t0}" for t0 in range(0, TC, 512)])
        if local:
            p.finalize(nc)
    return ctx


def ssm_host(inputs, xT_full):
    g = lambda k: np.ascontiguousarray(np.asarray(inputs[k], np.float32))
    wi = pack_weight(np.ascontiguousarray(g("L1_attn_b_in_proj_w").T))
    wxp = pack_weight(np.ascontiguousarray(g("L1_attn_b_x_proj_w").T), blk=96)
    wdtT = np.ascontiguousarray(g("L1_attn_b_dt_proj_w").T)
    wdt = pack_weight(wdtT)
    cw = np.ascontiguousarray(g("L1_attn_b_conv_w")[:, 0, :]
                              .reshape(16, 128, 4).transpose(1, 0, 2))
    cb = np.ascontiguousarray(g("L1_attn_b_conv_b").reshape(16, 128).T)
    dtb = np.ascontiguousarray(g("L1_attn_b_dt_proj_b").reshape(16, 128).T)
    wo = pack_weight(np.ascontiguousarray(g("L1_attn_b_out_proj_w").T))
    alog = g("L1_attn_b_A_log")
    ddv = g("L1_attn_b_D")

    nc1 = bass.Bass("TRN2", target_bir_lowering=False)
    c1 = build_ssm1(nc1)
    maps = []
    for c in range(NCORES):
        m = dict(xT=_shard_cols(xT_full, c), wip=wi)
        m.update(c1.consts)
        if hasattr(c1, "cf_map"):
            m["cf_consts"] = c1.cf_array()
        maps.append(m)
    r1 = run_bass_kernel_spmd(nc1, maps, core_ids=list(range(NCORES)))
    xpart = [r1.results[c]["xpart"] for c in range(NCORES)]
    zs = [r1.results[c]["z"] for c in range(NCORES)]

    nc2 = bass.Bass("TRN2", target_bir_lowering=False)
    c2 = build_ssm2(nc2)
    maps2 = []
    for c in range(NCORES):
        xin = np.zeros((2048, TC + 3), np.float32)
        xin[:, 3:] = xpart[c]
        if c % 4 != 0:
            xin[:, 0:3] = xpart[c - 1][:, -3:]
        m = dict(xin=xin, wxp=wxp, wdt=wdt, cw=cw, cb=cb, dtb=dtb)
        m.update(c2.consts)
        if hasattr(c2, "cf_map"):
            m["cf_consts"] = c2.cf_array()
        maps2.append(m)
    r2 = run_bass_kernel_spmd(nc2, maps2, core_ids=list(range(NCORES)))

    TF = 4 * TC
    maps3 = []
    nc3 = bass.Bass("TRN2", target_bir_lowering=False)
    c3 = build_ssm3(nc3, TFULL=TF)
    xc_full = [[r2.results[b * 4 + i]["xc"] for i in range(4)] for b in range(2)]
    dt_full = [[r2.results[b * 4 + i]["dtt"] for i in range(4)] for b in range(2)]
    bc_full = [np.stack([r2.results[b * 4 + i]["bc"] for i in range(4)])
               for b in range(2)]
    for c in range(NCORES):
        b = c // 4
        d0 = (c % 4) * 512
        m = dict(dts=np.stack([dt_full[b][r][d0:d0 + 512] for r in range(4)]),
                 xcs=np.stack([xc_full[b][r][d0:d0 + 512] for r in range(4)]),
                 bcf=bc_full[b],
                 alog=np.ascontiguousarray(
                     alog[d0:d0 + 512].reshape(4, 128, 16).transpose(1, 0, 2)),
                 dd=np.ascontiguousarray(ddv[d0:d0 + 512].reshape(4, 128).T))
        m.update(c3.consts)
        if hasattr(c3, "cf_map"):
            m["cf_consts"] = c3.cf_array()
        maps3.append(m)
    r3 = run_bass_kernel_spmd(nc3, maps3, core_ids=list(range(NCORES)))

    y_full = [[r3.results[b * 4 + i]["ys"] for i in range(4)] for b in range(2)]
    maps4 = []
    nc4 = bass.Bass("TRN2", target_bir_lowering=False)
    c4 = build_ssm4(nc4)
    for c in range(NCORES):
        b = c // 4
        t0 = (c % 4) * TC
        m = dict(xT=_shard_cols(xT_full, c),
                 y=np.stack([y_full[b][r][c % 4] for r in range(4)]),
                 z=zs[c], wop=wo)
        m.update(c4.consts)
        if hasattr(c4, "cf_map"):
            m["cf_consts"] = c4.cf_array()
        maps4.append(m)
    r4 = run_bass_kernel_spmd(nc4, maps4, core_ids=list(range(NCORES)))
    return np.concatenate([r4.results[c]["yT"] for c in range(NCORES)], axis=1)


def _ext_slice(stream_full, c, lo, width):
    b = c // 4
    gs = (c % 4) * TC
    out = np.zeros((D, width), np.float32)
    a = gs + lo
    bcols = stream_full[:, b * 8192:(b + 1) * 8192]
    s0 = max(0, a)
    s1 = min(8192, a + width)
    if s1 > s0:
        out[:, s0 - a:s1 - a] = bcols[:, s0:s1]
    return np.ascontiguousarray(out)


def l0ffn_prep_layer(inputs, l):
    g = lambda k: np.ascontiguousarray(np.asarray(inputs[k], np.float32))
    wqkv = g("L0_ffn_wqkv")[l]
    wq = pack_weight(wqkv[:, 0:512], blk=512)
    wk = pack_weight(wqkv[:, 512:1024], blk=512)
    wvt = np.ascontiguousarray(wqkv[:, 1024:1536].reshape(8, 128, 512).transpose(1, 0, 2))
    wo = pack_weight(g("L0_ffn_wo")[l])
    qs = np.zeros((128, 2), np.float32)
    qs[:, 0] = np.tile(g("L0_ffn_q_scale")[l], 2)
    qs[:, 1] = np.tile(g("L0_ffn_k_scale")[l], 2)
    w1 = g("L0_ffn_w1")[l]
    w1p = np.zeros((D, 5632), np.float32)
    w1p[:, 0:2730] = w1[:, 0:2730]
    w1p[:, 2816:2816 + 2730] = w1[:, 2730:5460]
    w1p = pack_weight(w1p)
    b1 = g("L0_ffn_b1")[l]
    b1p = np.zeros(5632, np.float32)
    b1p[0:2730] = b1[0:2730]
    b1p[2816:2816 + 2730] = b1[2730:5460]
    b1p = np.ascontiguousarray(b1p.reshape(44, 128).T)
    w2 = g("L0_ffn_w2")[l]
    w2p = np.zeros((2816, D), np.float32)
    w2p[0:2730] = w2
    w2p = pack_weight(w2p)
    b2r = g("L0_ffn_b2")[l].reshape(1, D)
    lng = np.ascontiguousarray(g("L0_ffn_ln_g")[l].reshape(8, 128).T)
    lnb = np.ascontiguousarray(g("L0_ffn_ln_b")[l].reshape(8, 128).T)
    return dict(wqp=wq, wkp=wk, wvt=wvt, wop=wo, qs=qs,
                w1p=w1p, b1p=b1p, w2p=w2p, b2r=b2r, lng=lng, lnb=lnb)


def l0ffn_core_tables(c):
    ps0 = (c % 4) * TC
    qpos = np.arange(ps0 - 512, ps0 + TC)
    kpos = np.arange(ps0 - 1024, ps0 + TC)
    qcos, qsin = rope_tables_xpos(qpos, False)
    kcos, ksin = rope_tables_xpos(kpos, True)
    mv = np.ones((128, 12), np.float32)
    if c % 4 == 0:
        for qt in range(2):
            for di in range(8):
                if qt == 1 and di >= 4:
                    continue
                kabs = ps0 - 1024 + qt * 512 + di * 128
                mvi = di if qt == 0 else 8 + di - 4
                if kabs < 0:
                    mv[:, mvi] = 0.0
    m0 = np.full((128, 1), 0.0 if c % 4 == 0 else 1.0, np.float32)
    return dict(qcos=qcos.astype(BF), qsin=qsin.astype(BF),
                kcos=kcos.astype(BF), ksin=ksin.astype(BF),
                msk=local_masks(), mv=mv), m0


def l0ffn_call_host(inputs, xT_full, cores=range(NCORES), runner=None):
    runner = runner or (lambda nc, maps: run_bass_kernel_spmd(
        nc, maps, core_ids=list(range(NCORES))).results)
    stream = xT_full
    for l in range(2):
        wl = l0ffn_prep_layer(inputs, l)
        nc = bass.Bass("TRN2", target_bir_lowering=False)
        ca = build_l0ffn_attn(nc, do_ln=(l == 0))
        maps = []
        for c in range(NCORES):
            tabs, m0 = l0ffn_core_tables(c)
            m = dict(xin=_ext_slice(stream, c, -1152, FXE),
                     wkp=wl["wkp"], wqp=wl["wqp"], wvt=wl["wvt"],
                     wop=wl["wop"], qs=wl["qs"], **tabs)
            m.update(ca.consts)
            m["cf_consts"] = ca.cf_array()
            maps.append(m)
        r1 = runner(nc, maps)

        nc2 = bass.Bass("TRN2", target_bir_lowering=False)
        cf = build_l0ffn_ffn(nc2, add_module=(l == 1))
        maps2 = []
        for c in range(NCORES):
            tabs, m0 = l0ffn_core_tables(c)
            m = dict(x1=r1[c]["x1"], lng=wl["lng"], lnb=wl["lnb"],
                     b1p=wl["b1p"], b2r=wl["b2r"], w1p=wl["w1p"],
                     w2p=wl["w2p"], m0=m0)
            if l == 1:
                m["xmod"] = _shard_cols(xT_full, c)
            m.update(cf.consts)
            m["cf_consts"] = cf.cf_array()
            maps2.append(m)
        r2 = runner(nc2, maps2)
        stream = np.concatenate([r2[c]["x2"] for c in range(NCORES)], axis=1)
    return stream



def _shard_cols(a, c):
    return np.ascontiguousarray(a[:, c * TC:(c + 1) * TC])


def l1_ffn_inputs(inputs, xT_full):
    g = lambda k: np.ascontiguousarray(inputs[k])
    w1t = pack_weight(g("L1_ffn_w1_w").T)
    w2t = pack_weight(g("L1_ffn_w2_w").T)
    h1t = pack_weight(g("L1_ffn_h1_w").T)
    h2t = pack_weight(g("L1_ffn_h2_w").T)
    w3t = pack_weight(g("L1_ffn_w3_w").T)
    s1 = np.ascontiguousarray(g("L1_ffn_w1_s").reshape(32, 128).T)
    s2 = np.ascontiguousarray(g("L1_ffn_w2_s").reshape(32, 128).T)
    sh1 = np.ascontiguousarray(g("L1_ffn_h1_s").reshape(32, 128).T)
    sh2 = np.ascontiguousarray(g("L1_ffn_h2_s").reshape(32, 128).T)
    s3 = np.ascontiguousarray(g("L1_ffn_w3_s").reshape(8, 128).T)
    maps = []
    for c in range(NCORES):
        maps.append(dict(xT=_shard_cols(xT_full, c), w1t=w1t, w2t=w2t,
                         h1t=h1t, h2t=h2t, w3t=w3t, s1=s1, s2=s2, sh1=sh1,
                         sh2=sh2, s3=s3))
    return maps


def run_l1_ffn(inputs, xT_full):
    nc = bass.Bass("TRN2", target_bir_lowering=False)
    ctx = build_l1_ffn(nc)
    maps = l1_ffn_inputs(inputs, xT_full)
    extra = dict(ctx.consts)
    extra["cf_consts"] = ctx.cf_array()
    for m in maps:
        m.update(extra)
    res = run_bass_kernel_spmd(nc, maps, core_ids=list(range(NCORES)))
    out = np.concatenate([res.results[c]["yT"] for c in range(NCORES)], axis=1)
    return out



def attn_a_host(inputs, xT_full):
    g = lambda k: np.ascontiguousarray(np.asarray(inputs[k], np.float32))
    perm = qperm_attn_a()
    wq = pack_weight(g("L0_attn_a_Wq")[:, perm])
    wkv = g("L0_attn_a_Wkv")
    wk = pack_weight(wkv[:, 0:256], blk=256)
    wv = np.ascontiguousarray(wkv[:, 256:512].reshape(8, 128, 256).transpose(1, 0, 2))
    wo = pack_weight(g("L0_attn_a_Wo")[perm, :])
    bo = g("L0_attn_a_bo").reshape(1, D)

    nc1 = bass.Bass("TRN2", target_bir_lowering=False)
    c1 = build_attn_a1(nc1)
    maps = []
    for c in range(NCORES):
        pos = ((c % 4) * TC + np.arange(TC)).astype(np.float32)
        qcos, qsin = rope_tables_attn_a(pos)
        m = dict(xT=_shard_cols(xT_full, c), wq=wq, wk=wk, wv=wv,
                 qcos=qcos, qsin=qsin)
        m.update(c1.consts)
        if hasattr(c1, "cf_map"):
            m["cf_consts"] = c1.cf_array()
        maps.append(m)
    r1 = run_bass_kernel_spmd(nc1, maps, core_ids=list(range(NCORES)))

    kT_full = [np.stack([r1.results[b * 4 + i]["kT"] for i in range(4)])
               for b in range(2)]
    va_full = [np.stack([r1.results[b * 4 + i]["va"] for i in range(4)])
               for b in range(2)]

    nc2 = bass.Bass("TRN2", target_bir_lowering=False)
    c2 = build_attn_a2(nc2)
    maps2 = []
    for c in range(NCORES):
        b = c // 4
        m = dict(xT=_shard_cols(xT_full, c), qT=r1.results[c]["qT"],
                 kT=kT_full[b], va=va_full[b], wo=wo, bo=bo)
        m.update(c2.consts)
        m["cf_consts"] = c2.cf_array()
        maps2.append(m)
    r2 = run_bass_kernel_spmd(nc2, maps2, core_ids=list(range(NCORES)))
    return np.concatenate([r2.results[c]["yT"] for c in range(NCORES)], axis=1)




GROUPS4 = [[0, 1, 2, 3], [4, 5, 6, 7]]


def emit_cc(ctx, kind, src, dst, tag):
    p = ctx.p

    def fn(e, src=src, dst=dst, kind=kind):
        return e.collective_compute(kind, mybir.AluOpType.bypass,
                                    replica_groups=GROUPS4,
                                    ins=[src.ap().opt()], outs=[dst.ap().opt()])
    p.cc(fn, reads=(), writes=(tag,), tag=tag)


def emit_ridx_setup(ctx, ridx_h, nvals=8):
    p = ctx.p
    ri = ctx.sbuf_root("ri", [1, nvals], mybir.dt.int32)

    def fl(e):
        return e.dma_start(out=ri[:, :], in_=ridx_h[:, :])
    p.dma("G", fl, writes=("ri",), tag="ld_ridx")

    names = ["d0", "rk", "rm1", "t0", "x4", "x5", "x6", "x7"]
    ctx.snaps = {}

    def fs(e):
        last = None
        for k in range(nvals):
            reg = e.alloc_register()
            last = e.reg_load(reg, ri[0:1, k:k + 1])
            ctx.snaps[names[k]] = e.snap(reg)
        return last
    p.G(fs, reads=("ri",), writes=("ridx_regs",), tag="ridx_regs")


def emit_halo_asm(ctx, stream, xin, hw, nrows, maskname, ns):
    p = ctx.p
    nch = nrows // 128
    hin = ctx.dram(ns + "hin", [nrows, hw], F32, kind="Internal")
    hout = ctx.dram(ns + "hout", [4, nrows, hw], F32, kind="Internal")
    mask_h = ctx.dram(ns + "hmask", [128, 1], F32, kind="ExternalInput")
    ctx.begin_phase(ns)
    skey = "rm1"

    def f1(e):
        return e.dma_start(out=hin[:, :], in_=stream[:, TC - hw:TC])
    p.dma("Y", f1, writes=(ns + "hin",), tag="hin")

    def f2(e):
        return e.dma_start(out=xin[:, hw:hw + TC], in_=stream[:, :])
    p.dma("Y", f2, writes=(ns + "own",), tag="own")
    ctx.p.barrier(tag=ns + "pre")
    emit_cc(ctx, "AllGather", hin, hout, ns + "ag")
    ctx.p.barrier(tag=ns + "cc")

    def f25(e):
        return e.dma_start(out=xin[:, 0:hw],
                           in_=hout[bass.ds(ctx.snaps[skey], 1), :, :])
    p.add("G", f25, reads=("ridx_regs",), writes=(ns + "hx",), inc=16,
          tag="halo_d2d", role="Gd")

    tile = ctx.sbuf("halo", [128, nch, hw], F32)
    msk = ctx.sbuf("hmask", [128, 1], F32)

    def f3(e):
        return e.dma_start(out=msk[:, :], in_=mask_h[:, :])
    p.dma("Y", f3, writes=("hmask",), tag="hmask")

    def f4(e):
        return e.dma_start(out=tile[:, :, :],
                           in_=xin[:, 0:hw].rearrange("(c p) t -> p c t", p=128))
    p.dma("Y", f4, reads=(ns + "hx",), writes=("halo",), tag="halo_ld")
    for ch in range(nch):
        def f5(e, ch=ch):
            return e.tensor_scalar_mul(tile[:, ch, :], tile[:, ch, :],
                                       msk[:, 0:1])
        p.V(f5, reads=("halo", "hmask"), writes=("halo",), tag="halo_msk")

    def f6(e):
        return e.dma_start(out=xin[:, 0:hw].rearrange("(c p) t -> p c t", p=128),
                           in_=tile[:, :, :])
    p.dma("Y", f6, reads=("halo",), writes=(ns + "halo@o",), tag="halo_st")
    ctx.end_phase()


def emit_dyn_extract(ctx, src, dst, which, tag):
    p = ctx.p
    n = dst.shape[1]
    for r in range(4):
        if which == "rows":
            def fn(e, r=r, n=n):
                return e.dma_start(out=dst[r, :, :],
                                   in_=src[r, bass.ds(ctx.snaps["d0"], n), :])
        else:
            def fn(e, r=r):
                return e.dma_start(out=dst[r, :, :],
                                   in_=src[r, bass.ds(ctx.snaps["rk"], 1), :, :])
        p.add("G", fn, reads=("ridx_regs",), writes=(tag,), inc=16,
              tag=tag, role="Gd")


def build_full(nc):
    ctx = make_ctx(nc)
    p = ctx.p
    xT_in = nc.dram_tensor("xT", [D, TC], F32, kind="ExternalInput")
    xT_out = nc.dram_tensor("out", [D, TC], F32, kind="ExternalOutput")
    ridx_h = nc.dram_tensor("ridx", [1, 8], mybir.dt.int32, kind="ExternalInput")

    st = {i: nc.dram_tensor(f"s{i}", [D, TC], F32) for i in range(1, 8)}

    with ctx.es:
        get_ones(ctx)
        ctx.cf(0.0)
        emit_ridx_setup(ctx, ridx_h)
        p.barrier(tag="setup")

        aa_qT = nc.dram_tensor("aa_qT", [D, TC], BF16)
        aa_kT = nc.dram_tensor("aa_kT", [256, TC], BF16)
        aa_va = nc.dram_tensor("aa_va", [TC, 260], BF16)
        aag_kT = nc.dram_tensor("aag_kT", [4, 256, TC], BF16)
        aag_va = nc.dram_tensor("aag_va", [4, TC, 260], BF16)
        ctx.begin_phase("a1_")
        build_attn_a1(None, fctx=ctx, io=dict(xT=xT_in, qT=aa_qT, kT=aa_kT,
                                              va=aa_va))
        ctx.end_phase()
        emit_cc(ctx, "AllGather", aa_kT, aag_kT, "cc_aakT")
        emit_cc(ctx, "AllGather", aa_va, aag_va, "cc_aava")
        p.barrier(tag="cc_aa")
        ctx.begin_phase("a2_")
        build_attn_a2(None, fctx=ctx, io=dict(xT=xT_in, qT=aa_qT, kT=aag_kT,
                                              va=aag_va, yT=st[1]))
        ctx.end_phase()

        ctx.ns = "fl_"
        fw = []
        for l in range(2):
            pre = f"l{l}_"
            fw.append(dict(
                wkp=ctx.dram(pre + "wkp", [1, 128, 8, 512], F32, "ExternalInput"),
                wqp=ctx.dram(pre + "wqp", [1, 128, 8, 512], F32, "ExternalInput"),
                wvt=ctx.dram(pre + "wvt", [128, 8, 512], F32, "ExternalInput"),
                wop=ctx.dram(pre + "wop", [2, 128, 4, 512], F32, "ExternalInput"),
                qs=ctx.dram(pre + "qs", [128, 2], F32, "ExternalInput"),
                w1p=ctx.dram(pre + "w1p", [11, 128, 8, 512], F32, "ExternalInput"),
                b1p=ctx.dram(pre + "b1p", [128, 44], F32, "ExternalInput"),
                w2p=ctx.dram(pre + "w2p", [2, 128, 22, 512], F32, "ExternalInput"),
                b2r=ctx.dram(pre + "b2r", [1, D], F32, "ExternalInput"),
                lng=ctx.dram(pre + "lng", [128, 8], F32, "ExternalInput"),
                lnb=ctx.dram(pre + "lnb", [128, 8], F32, "ExternalInput"),
            ))
        fsh = dict(
            qcos=ctx.dram("qcos", [128, FQE], BF16, "ExternalInput"),
            qsin=ctx.dram("qsin", [128, FQE], BF16, "ExternalInput"),
            kcos=ctx.dram("kcos", [128, FKE], BF16, "ExternalInput"),
            ksin=ctx.dram("ksin", [128, FKE], BF16, "ExternalInput"),
            msk=ctx.dram("msk", [128, 8, 512], BF16, "ExternalInput"),
            mv=ctx.dram("mv", [128, 12], F32, "ExternalInput"),
            m0=ctx.dram("m0", [128, 1], F32, "ExternalInput"),
        )
        ctx.ns = ""

        def l0ffn_call(callidx, src, dst):
            cur = src
            for l in range(2):
                ns = f"f{callidx}{l}_"
                xin = nc.dram_tensor(ns + "xin", [D, FXE], F32)
                x1 = nc.dram_tensor(ns + "x1", [D, 2560], F32)
                nxt = dst if l == 1 else nc.dram_tensor(ns + "mid", [D, TC], F32)
                emit_halo_asm(ctx, cur, xin, 1152, D, None, ns + "h_")
                ctx.begin_phase(ns + "a_")
                io = dict(xin=xin, x1=x1, **fw[l], **fsh)
                build_l0ffn_attn(None, do_ln=(l == 0), fctx=ctx, io=io)
                ctx.end_phase()
                ctx.begin_phase(ns + "f_")
                io2 = dict(x1=x1, x2=nxt, **fw[l], **fsh)
                if l == 1:
                    io2["xmod"] = src
                build_l0ffn_ffn(None, add_module=(l == 1), fctx=ctx, io=io2)
                ctx.end_phase()
                cur = nxt

        l0ffn_call(0, st[1], st[2])

        b_qT = nc.dram_tensor("b_qT", [64, 16, TC], BF16)
        b_kT = nc.dram_tensor("b_kT", [D, TC], BF16)
        b_va = nc.dram_tensor("b_va", [16, TC, 65], BF16)
        bg_kT = nc.dram_tensor("bg_kT", [4, D, TC], BF16)
        bg_va = nc.dram_tensor("bg_va", [4, 16, TC, 65], BF16)
        ctx.begin_phase("b1_")
        build_attn_b1(None, fctx=ctx, io=dict(xT=st[2], qT=b_qT, kT=b_kT,
                                              va=b_va))
        ctx.end_phase()
        emit_cc(ctx, "AllGather", b_kT, bg_kT, "cc_bkT")
        emit_cc(ctx, "AllGather", b_va, bg_va, "cc_bva")
        p.barrier(tag="cc_b")
        ctx.begin_phase("b2_")
        build_attn_b2(None, fctx=ctx, io=dict(xT=st[2], qT=b_qT, kT=bg_kT,
                                              va=bg_va, yT=st[3]))
        ctx.end_phase()

        l0ffn_call(1, st[3], st[4])

        c_kc = nc.dram_tensor("c_kc", [D, TC], F32)
        c_kvc = nc.dram_tensor("c_kvc", [D, TC], F32)
        c_tot = nc.dram_tensor("c_tot", [D, 2], F32)
        cg_tot = nc.dram_tensor("cg_tot", [4, D, 2], F32)
        ctx.begin_phase("c1_")
        build_cum1(None, fctx=ctx, io=dict(xT=st[4], kc=c_kc, kvc=c_kvc,
                                           tot=c_tot))
        ctx.end_phase()
        emit_cc(ctx, "AllGather", c_tot, cg_tot, "cc_tot")
        p.barrier(tag="cc_c")
        ctx.begin_phase("c2_")
        build_cum2(None, fctx=ctx, io=dict(xT=st[4], kc=c_kc, kvc=c_kvc,
                                           tots=cg_tot, yT=st[5]))
        ctx.end_phase()

        ctx.ns = "l1f_"
        l1w = dict(
            w1t=ctx.dram("w1t", [8, 128, 8, 512], F32, "ExternalInput"),
            w2t=ctx.dram("w2t", [8, 128, 8, 512], F32, "ExternalInput"),
            h1t=ctx.dram("h1t", [8, 128, 32, 512], F32, "ExternalInput"),
            h2t=ctx.dram("h2t", [8, 128, 32, 512], F32, "ExternalInput"),
            w3t=ctx.dram("w3t", [2, 128, 32, 512], F32, "ExternalInput"),
            s1=ctx.dram("s1", [128, 32], F32, "ExternalInput"),
            s2=ctx.dram("s2", [128, 32], F32, "ExternalInput"),
            sh1=ctx.dram("sh1", [128, 32], F32, "ExternalInput"),
            sh2=ctx.dram("sh2", [128, 32], F32, "ExternalInput"),
            s3=ctx.dram("s3", [128, 8], F32, "ExternalInput"),
        )
        ctx.ns = ""
        ctx.begin_phase("m1_")
        ctx.wpool_cfg = (32, 512)
        emit_l1_ffn_body(ctx, st[5], st[6], l1w, 512)
        ctx.end_phase()

        sm_xp = nc.dram_tensor("sm_xp", [2048, TC], F32)
        sm_z = nc.dram_tensor("sm_z", [2048, TC], F32)
        sm_xin = nc.dram_tensor("sm_xin", [2048, TC + 128], F32)
        sm_xc = nc.dram_tensor("sm_xc", [2048, TC], BF16)
        sm_dt = nc.dram_tensor("sm_dt", [2048, TC], BF16)
        sm_bc = nc.dram_tensor("sm_bc", [32, TC], F32)
        smg_xc = nc.dram_tensor("smg_xc", [4, 2048, TC], BF16)
        smg_dt = nc.dram_tensor("smg_dt", [4, 2048, TC], BF16)
        smg_bc = nc.dram_tensor("smg_bc", [4, 32, TC], F32)
        sm_ys = nc.dram_tensor("sm_ys", [4, 512, TC], BF16)
        smg_ys = nc.dram_tensor("smg_ys", [4, 4, 512, TC], BF16)
        scr_dt = nc.dram_tensor("scr_dt", [4, 512, TC], BF16)
        scr_xc = nc.dram_tensor("scr_xc", [4, 512, TC], BF16)
        scr_y = nc.dram_tensor("scr_y", [4, 512, TC], BF16)
        ctx.begin_phase("sm1_")
        build_ssm1(None, fctx=ctx, io=dict(xT=st[6], xpart=sm_xp, z=sm_z))
        ctx.end_phase()
        emit_halo_asm(ctx, sm_xp, sm_xin, 128, 2048, None, "smh_")
        ctx.begin_phase("sm2_")
        build_ssm2(None, fctx=ctx, io=dict(xin=sm_xin, xc=sm_xc, dtt=sm_dt,
                                           bc=sm_bc), xin_pad=128)
        ctx.end_phase()
        emit_cc(ctx, "AllGather", sm_xc, smg_xc, "cc_xc")
        emit_cc(ctx, "AllGather", sm_dt, smg_dt, "cc_dt")
        emit_cc(ctx, "AllGather", sm_bc, smg_bc, "cc_bc")
        p.barrier(tag="cc_sm")
        emit_dyn_extract(ctx, smg_dt, scr_dt, "rows", "xdt")
        emit_dyn_extract(ctx, smg_xc, scr_xc, "rows", "xxc")
        p.barrier(tag="xtr_sm")
        ctx.begin_phase("sm3_")
        build_ssm3(None, fctx=ctx, io=dict(dts=scr_dt, xcs=scr_xc, bcf=smg_bc,
                                           ys=sm_ys))
        ctx.end_phase()
        emit_cc(ctx, "AllGather", sm_ys, smg_ys, "cc_ys")
        p.barrier(tag="cc_ys")
        emit_dyn_extract(ctx, smg_ys, scr_y, "blk", "xy")
        p.barrier(tag="xtr_y")
        ctx.begin_phase("sm4_")
        build_ssm4(None, fctx=ctx, io=dict(xT=st[6], y=scr_y, z=sm_z,
                                           yT=st[7]))
        ctx.end_phase()

        ctx.begin_phase("m2_")
        ctx.wpool_cfg = (32, 512)
        emit_l1_ffn_body(ctx, st[7], xT_out, l1w, 512)
        ctx.end_phase()

        p.finalize(nc)
    return ctx


def fused_maps(inputs, ctx):
    g = lambda k: np.ascontiguousarray(np.asarray(inputs[k], np.float32))
    x = np.asarray(inputs["L0_attn_a_x"], np.float32)
    xT_full = np.ascontiguousarray(x.reshape(2 * 8192, D).T)

    perm = qperm_attn_a()
    shared = {}
    shared["a1_wq"] = pack_weight(g("L0_attn_a_Wq")[:, perm])
    wkv = g("L0_attn_a_Wkv")
    shared["a1_wk"] = pack_weight(wkv[:, 0:256], blk=256)
    shared["a1_wv"] = np.ascontiguousarray(
        wkv[:, 256:512].reshape(8, 128, 256).transpose(1, 0, 2))
    shared["a2_wo"] = pack_weight(g("L0_attn_a_Wo")[perm, :])
    shared["a2_bo"] = g("L0_attn_a_bo").reshape(1, D)

    for l in range(2):
        wl = l0ffn_prep_layer(inputs, l)
        for k, v in wl.items():
            shared[f"fl_l{l}_" + {"wkp": "wkp", "wqp": "wqp", "wvt": "wvt",
                                  "wop": "wop", "qs": "qs", "w1p": "w1p",
                                  "b1p": "b1p", "w2p": "w2p", "b2r": "b2r",
                                  "lng": "lng", "lnb": "lnb"}[k]] = v
    shared["fl_msk"] = local_masks()

    shared["b1_wq"] = pack_weight(g("L0_attn_b_Wq"))
    wkm = g("L0_attn_b_Wk")
    shared["b1_wk"] = pack_weight(wkm)
    shared["b1_wkh"] = np.ascontiguousarray(
        wkm.reshape(8, 128, 1024).transpose(1, 0, 2))
    shared["b2_wo"] = pack_weight(g("L0_attn_b_Wo"))
    shared["b2_bo"] = g("L0_attn_b_bo").reshape(1, D)

    shared["c1_wkp"] = pack_weight(g("L1_attn_a_Wk"))
    shared["c1_wvp"] = pack_weight(g("L1_attn_a_Wv"))
    shared["c2_wqp"] = pack_weight(g("L1_attn_a_Wq"))
    shared["c2_wcp"] = pack_weight(g("L1_attn_a_Wc"))

    shared["l1f_w1t"] = pack_weight(g("L1_ffn_w1_w").T)
    shared["l1f_w2t"] = pack_weight(g("L1_ffn_w2_w").T)
    shared["l1f_h1t"] = pack_weight(g("L1_ffn_h1_w").T)
    shared["l1f_h2t"] = pack_weight(g("L1_ffn_h2_w").T)
    shared["l1f_w3t"] = pack_weight(g("L1_ffn_w3_w").T)
    shared["l1f_s1"] = np.ascontiguousarray(g("L1_ffn_w1_s").reshape(32, 128).T)
    shared["l1f_s2"] = np.ascontiguousarray(g("L1_ffn_w2_s").reshape(32, 128).T)
    shared["l1f_sh1"] = np.ascontiguousarray(g("L1_ffn_h1_s").reshape(32, 128).T)
    shared["l1f_sh2"] = np.ascontiguousarray(g("L1_ffn_h2_s").reshape(32, 128).T)
    shared["l1f_s3"] = np.ascontiguousarray(g("L1_ffn_w3_s").reshape(8, 128).T)

    shared["sm1_wip"] = pack_weight(np.ascontiguousarray(g("L1_attn_b_in_proj_w").T))
    shared["sm2_wxp"] = pack_weight(np.ascontiguousarray(g("L1_attn_b_x_proj_w").T), blk=96)
    shared["sm2_wdt"] = pack_weight(np.ascontiguousarray(g("L1_attn_b_dt_proj_w").T))
    shared["sm2_cw"] = np.ascontiguousarray(
        g("L1_attn_b_conv_w")[:, 0, :].reshape(16, 128, 4).transpose(1, 0, 2))
    shared["sm2_cb"] = np.ascontiguousarray(g("L1_attn_b_conv_b").reshape(16, 128).T)
    shared["sm2_dtb"] = np.ascontiguousarray(g("L1_attn_b_dt_proj_b").reshape(16, 128).T)
    shared["sm4_wop"] = pack_weight(np.ascontiguousarray(g("L1_attn_b_out_proj_w").T))
    alog = g("L1_attn_b_A_log")
    ddv = g("L1_attn_b_D")

    shared["cf_consts"] = ctx.cf_array()
    shared.update(ctx.consts)

    maps = []
    for c in range(NCORES):
        m = dict(shared)
        m["xT"] = _shard_cols(xT_full, c)
        r = c % 4
        rm1 = (r + 3) % 4
        m["ridx"] = np.array([[r * 512, r, rm1, r * TC, 0, 0, 0, 0]],
                             np.int32)
        pos = (r * TC + np.arange(TC)).astype(np.float32)
        qc, qn = rope_tables_attn_a(pos)
        m["a1_qcos"], m["a1_qsin"] = qc, qn
        tabs, m0 = l0ffn_core_tables(c)
        m["fl_qcos"], m["fl_qsin"] = tabs["qcos"], tabs["qsin"]
        m["fl_kcos"], m["fl_ksin"] = tabs["kcos"], tabs["ksin"]
        m["fl_mv"] = tabs["mv"]
        m["fl_m0"] = m0
        hmask = np.full((128, 1), 0.0 if r == 0 else 1.0, np.float32)
        for l in range(2):
            for ci in range(2):
                m[f"f{ci}{l}_h_hmask"] = hmask
        m["smh_hmask"] = hmask
        cmask = np.zeros((128, 4), np.float32)
        for cp in range(4):
            if cp < r:
                cmask[:, cp] = 1.0
        m["c2_cmask"] = cmask
        d0 = r * 512
        m["sm3_alog"] = np.ascontiguousarray(
            alog[d0:d0 + 512].reshape(4, 128, 16).transpose(1, 0, 2))
        m["sm3_dd"] = np.ascontiguousarray(ddv[d0:d0 + 512].reshape(4, 128).T)
        maps.append(m)
    return maps, xT_full


_FUSED_CACHE = {}


def kernel_fused(**inputs):
    if "nc" not in _FUSED_CACHE:
        nc = bass.Bass("TRN2", target_bir_lowering=False)
        ctx = build_full(nc)
        _FUSED_CACHE["nc"] = nc
        _FUSED_CACHE["ctx"] = ctx
    nc, ctx = _FUSED_CACHE["nc"], _FUSED_CACHE["ctx"]
    maps, _ = fused_maps(inputs, ctx)
    res = run_bass_kernel_spmd(nc, maps, core_ids=list(range(NCORES)))
    out = np.concatenate([res.results[c]["out"] for c in range(NCORES)], axis=1)
    return np.ascontiguousarray(out.T).reshape(2, 8192, D)


def kernel(**inputs):
    return kernel_fused(**inputs)


def kernel_unfused(**inputs):
    x = np.asarray(inputs["L0_attn_a_x"], np.float32)
    xT = np.ascontiguousarray(x.reshape(2 * 8192, D).T)
    xT = attn_a_host(inputs, xT)
    xT = np.ascontiguousarray(xT)
    xT = l0ffn_call_host(inputs, xT)
    xT = np.ascontiguousarray(xT)
    xT = attn_b_host(inputs, xT)
    xT = np.ascontiguousarray(xT)
    xT = l0ffn_call_host(inputs, xT)
    xT = np.ascontiguousarray(xT)
    xT = cum_host(inputs, xT)
    xT = np.ascontiguousarray(xT)
    xT = run_l1_ffn(inputs, xT)
    xT = np.ascontiguousarray(xT)
    xT = ssm_host(inputs, xT)
    xT = np.ascontiguousarray(xT)
    xT = run_l1_ffn(inputs, xT)
    return np.ascontiguousarray(xT.T).reshape(2, 8192, D)
